# version 27
# baseline (speedup 1.0000x reference)
import numpy as np
import ml_dtypes
import concourse.bass as bass
import concourse.mybir as mybir
from concourse.bass_utils import run_bass_kernel_spmd
from concourse.alu_op_type import AluOpType as OP
from contextlib import ExitStack

F32 = mybir.dt.float32
BF16 = mybir.dt.bfloat16
I32 = mybir.dt.int32
AF = mybir.ActivationFunctionType
AX = mybir.AxisListType

S, OWN, D, FF, R = 4096, 2048, 1024, 4096, 16
NCH = 8
NG = 8
NGOWN = 4
NQT = 16
ENGINES = ("tensor", "vector", "scalar", "sync", "gpsimd")


class MK:
    def __init__(self, debug=0, phases=99):
        self.debug = debug
        self.phases = phases
        self.nc = bass.Bass("TRN2", target_bir_lowering=False)
        self.ops = {e: [] for e in ENGINES}
        self.cnt = {k: 0 for k in ("t", "v", "s", "g")}
        self.es = ExitStack()
        self.esR = ExitStack()
        self.dbg = {}

    def em(self, eng, fn, waits=(), inc=None, inc_n=None):
        if inc is not None and inc not in self.cnt:
            self.cnt[inc] = 0
        for snm, _v in waits:
            if snm not in self.cnt:
                self.cnt[snm] = 0
        if inc is not None and inc_n is None:
            inc_n = 16 if inc.startswith("d") else 1
        tok = None
        if inc is not None:
            self.cnt[inc] += inc_n
            tok = self.cnt[inc]
        self.ops[eng].append((fn, tuple((s, v) for (s, v) in waits if v), inc, inc_n))
        return tok

    def sb(self, name, shape, dt=F32, right=False):
        stack = self.esR if right else self.es
        kw = {"side": "right"} if right else {}
        return stack.enter_context(self.nc.sbuf_tensor(name, shape, dt, **kw))

    def ps(self, name, shape, dt=F32):
        return self.es.enter_context(self.nc.psum_tensor(name, shape, dt))

    def dram(self, name, shape, dt, kind):
        return self.nc.dram_tensor(name, shape, dt, kind=kind).ap()

    def declare(self):
        nc, d = self.nc, self.dram
        self.x_in = d("x_in", [S, D], F32, "ExternalInput")
        self.c_in = d("c_in", [1, D], F32, "ExternalInput")
        self.wmod_in = d("wmod_in", [D, 6 * D], F32, "ExternalInput")
        self.wk_in = d("wk_in", [D, D], F32, "ExternalInput")
        self.wq_in = d("wq_in", [D, D], F32, "ExternalInput")
        self.wv_in = d("wv_in", [D, D], F32, "ExternalInput")
        self.wa_in = d("wa_in", [D, D], F32, "ExternalInput")
        self.wb_in = d("wb_in", [D, D], F32, "ExternalInput")
        self.wfc_in = d("wfc_in", [FF, D], F32, "ExternalInput")
        self.bfc_in = d("bfc_in", [FF], F32, "ExternalInput")
        self.afc_in = d("afc_in", [R, D], F32, "ExternalInput")
        self.bfcB_in = d("bfcB_in", [FF, R], F32, "ExternalInput")
        self.wpj_in = d("wpj_in", [D, FF], F32, "ExternalInput")
        self.bpj_in = d("bpj_in", [1, D], F32, "ExternalInput")
        self.apj_in = d("apj_in", [R, FF], F32, "ExternalInput")
        self.bpjB_in = d("bpjB_in", [D, R], F32, "ExternalInput")
        self.idb_in = d("idb_in", [128, 128], BF16, "ExternalInput")
        self.idf_in = d("idf_in", [128, 128], F32, "ExternalInput")
        self.onesf_in = d("onesf_in", [1, 128], F32, "ExternalInput")

        self.out_hbm = d("out", [OWN, D], F32, "ExternalOutput")
        self.x1_hbm = d("x1_sc", [OWN, D], F32, "Internal")
        self.q_hbm = d("q_sc", [16, 128, NCH, 128], BF16, "Internal")
        self.wfcq_hbm = d("wfcq_sc", [32, 128, 8, 128], BF16, "Internal")
        self.wpjq_hbm = d("wpjq_sc", [FF, D], BF16, "Internal")
        self.bfcT_hbm = d("bfcT_sc", [R, FF], BF16, "Internal")
        self.bpjrow_hbm = d("bpjrow_sc", [1, D], F32, "Internal")

        if self.debug:
            for nm, shp in (("mod", [1, 2048]), ("cols", [128, 48]),
                            ("hst", [128, 4096]), ("akc", [128, 1024]),
                            ("uc", [128, 1024]), ("qc", [128, 1024]),
                            ("amax", [128, 8]), ("x1c", [128, 1024]), ("wfcqc", [128, 1024]),
                            ("wpjqc", [128, 1024])):
                self.dbg[nm] = d("dbg_" + nm, shp, F32, "ExternalOutput")


        sb = self.sb
        self.idb = sb("idb", [128, 128], BF16)
        self.idf = sb("idf", [128, 128], F32)
        self.onesf = sb("onesf", [1, 128], F32)
        self.kcol = sb("kcol", [128, 16], F32)
        self.ca1_col = sb("ca1_col", [128, 8], F32)
        self.sa_col = sb("sa_col", [128, 8], F32)
        self.cb1_col = sb("cb1_col", [128, 8], F32)
        self.sbb_col = sb("sbb_col", [128, 8], F32)
        self.gbT_col = sb("gbT_col", [128, 8], F32)
        self.pjscale_col = sb("pjscale_col", [128, 8], F32)
        self.bfc_col = sb("bfc_col", [128, 32], F32)
        self.scT = sb("scT", [128, 16], BF16)
        self.x_st0 = sb("x_st0", [128, 1024], F32)
        self.xnb = sb("xnb", [128, 1024], BF16)
        self.trash = sb("trash", [128, 1024], F32)
        self.amst = [sb("amst%d" % i, [128, 256], F32, right=True) for i in range(2)]
        self.stats = sb("stats", [128, 32], F32)
        self.amx = sb("amx", [128, 8], F32)
        self.afcT = sb("afcT", [128, NCH * R], BF16)
        self.apjT = sb("apjT", [128, 32 * R], BF16)
        self.bpjT = sb("bpjT", [R, 1024], BF16)
        self.lora1 = sb("lora1", [R, 512], BF16)
        self.lora1p = sb("lora1p", [R, 512], BF16)
        self.mcol = sb("mcol", [128, 8], F32)
        self.lcolv = sb("lcolv", [128, 8], F32)

        self.esP1 = ExitStack()
        def sb1(name, shape, dt=F32):
            return self.esP1.enter_context(nc.sbuf_tensor(name, shape, dt))
        self.garow = sb1("garow", [1, 1024])
        self.gbrow = sb1("gbrow", [1, 1024])
        self.crow = sb1("crow", [1, 1024])
        self.scrow = sb1("scrow", [1, 1024])
        self.rowa = sb1("rowa", [1, 1024])
        self.rowb = sb1("rowb", [1, 1024])
        self.rowc = sb1("rowc", [1, 1024])
        self.wm = sb1("wm", [128, 3072])

        self.ak_fm = sb("ak_fm", [128, NCH * S], BF16, right=True)
        self.u_tm = sb("u_tm", [128, 32 * 1024], BF16, right=True)
        self.schi = self.u_tm[0:1, 6144:7168]
        self.sclo = self.u_tm[0:1, 7168:8192]
        self.l16b = self.u_tm[:, 8192:9216]
        self.l16a = self.wm[:, 0:1024]

        self.pE = self.ps("pE", [128, 2048], F32)
        self.pT0 = self.ps("pT0", [128, 1024], BF16)
        self.pT1 = self.ps("pT1", [128, 1024], BF16)
        self.pO = self.ps("pO", [128, 1024], F32)

    def ph0(self):
        em, s = self.em, self
        em("sync", lambda e: e.dma_start(out=s.idb[:], in_=s.idb_in), inc="d_ph0")
        em("sync", lambda e: e.dma_start(out=s.idf[:], in_=s.idf_in), inc="d_ph0")
        em("sync", lambda e: e.dma_start(out=s.onesf[:], in_=s.onesf_in), inc="d_ph0")
        em("sync", lambda e: e.dma_start(out=s.crow[:], in_=s.c_in), inc="d_ph0")
        for cc in range(32):
            em("sync", lambda e, cc=cc: e.dma_start(
                out=s.bfc_col[:, cc:cc + 1], in_=s.bfc_in[cc * 128:(cc + 1) * 128]),
                inc="d_ph0")
        self.t_ld0 = self.cnt["d_ph0"]
        em("gpsimd", lambda e: e.memset(s.kcol[:, 0:1], 1e-5))
        em("gpsimd", lambda e: e.memset(s.kcol[:, 1:2], -0.5))
        em("gpsimd", lambda e: e.memset(s.kcol[:, 2:3], 0.5))
        t_ms = em("gpsimd", lambda e: e.memset(s.kcol[:, 3:4], 127.0), inc="g")
        self.t_rec127 = em("vector",
                           lambda e: e.reciprocal(out=s.kcol[:, 4:5], in_=s.kcol[:, 3:4]),
                           waits=[("g", t_ms)], inc="v")

    def ph1(self):
        em, s = self.em, self
        t_silu = em("scalar", lambda e: e.activation(out=s.scrow[:], in_=s.crow[:],
                                                     func=AF.Silu),
                    waits=[("d_ph0", self.t_ld0)], inc="s")
        t_hi = em("vector", lambda e: e.tensor_copy(out=s.schi, in_=s.scrow[:]),
                  waits=[("s", t_silu)], inc="v")
        t_lo = em("vector", lambda e: e.tensor_tensor(out=s.sclo, in0=s.scrow[:],
                                                      in1=s.schi, op=OP.subtract),
                  waits=[("v", t_hi)], inc="v")
        t_tr = None
        for kc in range(NCH):
            for j, src in ((0, s.schi), (1, s.sclo)):
                t_tr = em("tensor", lambda e, kc=kc, src=src, j=j: e.transpose(
                    out=s.pT0[:, (kc * 2 + j) * 2:(kc * 2 + j) * 2 + 1],
                    in_=src[0:1, kc * 128:(kc + 1) * 128], identity=s.idb[0:1, 0:1]),
                    waits=[("v", t_lo), ("d_ph0", self.t_ld0)],
                    inc="t" if (kc == NCH - 1 and j == 1) else None)
        t_scT = em("vector", lambda e: e.tensor_copy(out=s.scT[:], in_=s.pT0[:, 0:32:2]),
                   waits=[("t", t_tr)], inc="v")
        self.t_scT = t_scT

        whi = [s.u_tm[:, 0:1536], s.u_tm[:, 3072:4608]]
        wlo = [s.u_tm[:, 1536:3072], s.u_tm[:, 4608:6144]]
        wmh = [s.wm[:, 0:1536], s.wm[:, 1536:3072]]
        acc = [s.pE[0:1, 512 * k:512 * (k + 1)] for k in range(4)] + \
              [s.pO[0:1, 0:512], s.pO[0:1, 512:1024]]
        t_split_v = t_split_s = 0
        t_mmgrp = 0
        t_mm_half = [0, 0]
        t_split_prev = [(0, 0), (0, 0)]
        t_proc_v = 0
        t_row = {}
        for rnd in range(2):
            for kc in range(NCH):
                for hp in range(2):
                    p = (kc * 2 + hp) % 2
                    t_wd = em("sync", lambda e, kc=kc, rnd=rnd, hp=hp, p=p: e.dma_start(
                        out=wmh[p], in_=s.wmod_in[kc * 128:(kc + 1) * 128,
                                                  rnd * 3072 + hp * 1536:
                                                  rnd * 3072 + (hp + 1) * 1536]),
                        waits=[("v", t_split_prev[p][0]), ("s", t_split_prev[p][1])],
                        inc="d_wm%d" % p)
                    t_split_s = em("scalar", lambda e, p=p: e.activation(
                        out=whi[p], in_=wmh[p], func=AF.Copy),
                        waits=[("d_wm%d" % p, t_wd), ("t", t_mm_half[p])], inc="s")
                    t_split_v = em("vector", lambda e, p=p: e.tensor_tensor(
                        out=wlo[p], in0=wmh[p], in1=whi[p], op=OP.subtract),
                        waits=[("s", t_split_s)], inc="v")
                    t_split_prev[p] = (t_split_v, t_split_s)
                    for s3 in range(3):
                        sl = hp * 3 + s3
                        for ti, (par, rhs) in enumerate(((0, whi[p]), (1, whi[p]),
                                                         (0, wlo[p]))):
                            st = (kc == 0 and ti == 0)
                            sp = (kc == NCH - 1 and ti == 2)
                            tok = em("tensor", lambda e, kc=kc, par=par, rhs=rhs, sl=sl,
                                     s3=s3, st=st, sp=sp: e.matmul(
                                         out=acc[sl],
                                         lhsT=s.scT[:, kc * 2 + par:kc * 2 + par + 1],
                                         rhs=rhs[:, s3 * 512:(s3 + 1) * 512],
                                         start=st, stop=sp),
                                     waits=[("v", t_split_v), ("v", t_scT)],
                                     inc="t" if (s3 == 2 and ti == 2) else None)
                            if tok:
                                t_mmgrp = tok
                                t_mm_half[p] = tok
            names = (("sa", "ca", "ga"), ("sb", "cb", "gb"))[rnd]
            rows = {"sa": s.rowa, "ca": s.rowc, "ga": s.garow,
                    "sb": s.rowa, "cb": s.rowc, "gb": s.gbrow}
            t_ev = {}
            for rr, name in enumerate(names):
                for h2 in range(2):
                    t_ev[name] = em("vector", lambda e, rr=rr, h2=h2, name=name:
                                    e.tensor_copy(
                                        out=rows[name][0:1, h2 * 512:(h2 + 1) * 512],
                                        in_=acc[rr * 2 + h2]),
                                    waits=[("t", t_mmgrp), ("v", t_proc_v)], inc="v")
            for name in names:
                if name in ("ga", "gb"):
                    g = rows[name]
                    t_sg = em("scalar", lambda e, g=g: e.activation(
                        out=s.rowb[:], in_=g[:], func=AF.Sign),
                        waits=[("v", t_ev[name])], inc="s")
                    t_ab = em("scalar", lambda e, g=g: e.activation(
                        out=s.scrow[:], in_=g[:], func=AF.Abs),
                        waits=[("s", t_sg)], inc="s")
                    t_mx = em("vector", lambda e: e.tensor_scalar(
                        out=s.scrow[:], in0=s.scrow[:], scalar1=0.25, scalar2=None,
                        op0=OP.max),
                        waits=[("s", t_ab)], inc="v")
                    t_g = em("vector", lambda e, g=g: e.tensor_tensor(
                        out=g[:], in0=s.rowb[:], in1=s.scrow[:], op=OP.mult),
                        waits=[("v", t_mx)], inc="v")
                    t_row[name] = t_g
                    t_proc_v = t_g
                else:
                    dcol = {"sa": s.sa_col, "ca": s.ca1_col,
                            "sb": s.sbb_col, "cb": s.cb1_col}[name]
                    plus1 = name in ("ca", "cb")
                    g = rows[name]
                    t_ct = None
                    for kc in range(NCH):
                        t_ct = em("tensor", lambda e, kc=kc, g=g: e.transpose(
                            out=s.pO[:, 512 + kc:513 + kc],
                            in_=g[0:1, kc * 128:(kc + 1) * 128],
                            identity=s.idf[0:1, 0:1]),
                            waits=[("v", t_ev[name]), ("v", t_proc_v),
                                   ("v", t_ev.get(names[2], t_ev[name]))],
                            inc="t" if kc == NCH - 1 else None)
                    t_proc_v = em("vector", lambda e, dcol=dcol, plus1=plus1:
                                  e.tensor_scalar(out=dcol[:, 0:8],
                                                  in0=s.pO[:, 512:520],
                                                  scalar1=1.0 if plus1 else 0.0,
                                                  scalar2=None, op0=OP.add),
                                  waits=[("t", t_ct)], inc="v")
                    t_row[name + "_col"] = t_proc_v
            t_mmgrp = self.cnt["t"]
        self.t_row = t_row

        t_ct = None
        for kc in range(NCH):
            t_ct = em("tensor", lambda e, kc=kc: e.transpose(
                out=s.pO[:, 512 + kc:513 + kc],
                in_=s.gbrow[0:1, kc * 128:(kc + 1) * 128], identity=s.idf[0:1, 0:1]),
                waits=[("v", t_row["gb"]), ("v", t_proc_v)],
                inc="t" if kc == NCH - 1 else None)
        t_gbT = em("vector", lambda e: e.tensor_copy(out=s.gbT_col[:, 0:8],
                                                     in_=s.pO[:, 512:520]),
                   waits=[("t", t_ct)], inc="v")
        self.t_gbT = t_gbT

        def bcast_row(row_ap, out_sb, wait_list):
            t_mm = None
            for hh in range(2):
                t_mm = em("tensor", lambda e, hh=hh, row_ap=row_ap: e.matmul(
                    out=s.pO[:, hh * 512:(hh + 1) * 512], lhsT=s.onesf[:],
                    rhs=row_ap[0:1, hh * 512:(hh + 1) * 512], start=True, stop=True),
                    waits=wait_list, inc="t" if hh else None)
            return em("vector", lambda e, out_sb=out_sb: e.tensor_copy(
                out=out_sb[:], in_=s.pO[:]), waits=[("t", t_mm)], inc="v")

        t_gbbc = bcast_row(s.gbrow, s.trash, [("v", t_row["gb"]), ("v", t_gbT)])

        t_d = em("sync", lambda e: e.dma_start(out=s.l16a[0:16, 0:1024], in_=s.afc_in),
                 waits=[("s", self.cnt["s"]), ("v", self.cnt["v"])], inc="d_l16")
        t_cast = em("scalar", lambda e: e.activation(out=s.l16b[0:16, 0:1024],
                                                     in_=s.l16a[0:16, 0:1024],
                                                     func=AF.Copy, scale=2.0),
                    waits=[("d_l16", t_d)], inc="s")
        t_tr2 = None
        for ic in range(NCH):
            t_tr2 = em("tensor", lambda e, ic=ic: e.transpose(
                out=s.pT0[:, ic * 16:(ic + 1) * 16],
                in_=s.l16b[0:16, ic * 128:(ic + 1) * 128], identity=s.idb[0:16, 0:16]),
                waits=[("s", t_cast), ("v", t_scT)], inc="t" if ic == NCH - 1 else None)
        t_afcT = em("vector", lambda e: e.tensor_copy(out=s.afcT[:],
                                                      in_=s.pT0[:, 0:NCH * 16]),
                    waits=[("t", t_tr2)], inc="v")
        t_apjT = 0
        for pc in range(4):
            t_d = em("sync", lambda e, pc=pc: e.dma_start(
                out=s.l16a[0:16, 0:1024], in_=s.apj_in[0:16, pc * 1024:(pc + 1) * 1024]),
                waits=[("s", t_cast), ("t", t_tr2)], inc="d_l16")
            t_cast = em("scalar", lambda e: e.activation(
                out=s.l16b[0:16, 0:1024], in_=s.l16a[0:16, 0:1024], func=AF.Copy, scale=2.0),
                waits=[("d_l16", t_d), ("v", t_afcT if pc == 0 else t_apjT)], inc="s")
            for j in range(NCH):
                t_tr2 = em("tensor", lambda e, j=j: e.transpose(
                    out=s.pT0[:, j * 16:(j + 1) * 16],
                    in_=s.l16b[0:16, j * 128:(j + 1) * 128],
                    identity=s.idb[0:16, 0:16]),
                    waits=[("s", t_cast)], inc="t" if j == NCH - 1 else None)
            t_apjT = em("vector", lambda e, pc=pc: e.tensor_copy(
                out=s.apjT[:, pc * 128:(pc + 1) * 128], in_=s.pT0[:, 0:128]),
                waits=[("t", t_tr2)], inc="v")
        t_d = em("sync", lambda e: e.dma_start(
            out=s.l16a[:, 0:128].rearrange("p (j c) -> p j c", c=16),
            in_=s.bpjB_in.rearrange("(j p) c -> p j c", p=128)),
            waits=[("s", t_cast), ("t", t_tr2), ("v", t_apjT)], inc="d_l16")
        t_cast2 = em("scalar", lambda e: e.activation(
            out=s.l16b[:, 0:128], in_=s.l16a[:, 0:128], func=AF.Copy),
            waits=[("d_l16", t_d)], inc="s")
        t_bt = None
        for dc in range(NCH):
            t_bt = em("tensor", lambda e, dc=dc: e.transpose(
                out=s.pT1[0:16, dc * 128:(dc + 1) * 128],
                in_=s.l16b[:, dc * 16:(dc + 1) * 16],
                identity=s.idb[:, :]),
                waits=[("s", t_cast2)], inc="t" if dc == NCH - 1 else None)
        t_bp0 = em("vector", lambda e: e.tensor_copy(out=s.bpjT[:],
                                                     in_=s.pT1[0:16, 0:1024]),
                   waits=[("t", t_bt)], inc="v")
        t_bpjT = em("vector", lambda e: e.tensor_tensor(
            out=s.bpjT[:], in0=s.bpjT[:], in1=s.trash[0:16, :], op=OP.mult),
            waits=[("v", t_bp0), ("v", t_gbbc)], inc="v")
        self.t_bpjT = t_bpjT
        t_d = em("sync", lambda e: e.dma_start(out=s.rowb[:], in_=s.bpj_in),
                 waits=[("v", self.cnt["v"])], inc="d_bpj")
        t_bb = em("vector", lambda e: e.tensor_tensor(out=s.rowb[:], in0=s.rowb[:],
                                                      in1=s.gbrow[:], op=OP.mult),
                  waits=[("d_bpj", t_d), ("v", t_bpjT)], inc="v")
        t_bpjgb = em("gpsimd", lambda e: e.dma_start(out=s.bpjrow_hbm, in_=s.rowb[:]),
                     waits=[("v", t_bb)], inc="d_bpr")
        self.t_bpjgb = t_bpjgb
        t_gabc = bcast_row(s.garow, s.trash, [("v", t_row["ga"]), ("v", t_bpjT),
                                              ("v", t_bb)])
        self.t_gabc = t_gabc
        self.ga_bc = s.trash
        l16c = s.u_tm[:, 9216:10240]
        t_pc = 0
        t_cast3 = [0, 0]
        t_bt3 = [t_bt, t_bt]
        t_ev2 = 0
        for b in range(4):
            rb = b % 2
            t_d = em("sync", lambda e, b=b, rb=rb: e.dma_start(
                out=s.l16a[:, 128 + rb * 128:128 + (rb + 1) * 128]
                .rearrange("p (j c) -> p j c", c=16),
                in_=s.bfcB_in[b * 1024:(b + 1) * 1024, :]
                .rearrange("(j p) c -> p j c", p=128)),
                waits=[("s", t_cast3[rb] or t_cast2)], inc="d_lb%d" % rb)
            t_cast3[rb] = em("scalar", lambda e, rb=rb: e.activation(
                out=s.l16b[:, 128 + rb * 128:128 + (rb + 1) * 128],
                in_=s.l16a[:, 128 + rb * 128:128 + (rb + 1) * 128], func=AF.Copy),
                waits=[("d_lb%d" % rb, t_d), ("t", t_bt3[rb])], inc="s")
            t_tr3 = None
            for j in range(8):
                t_tr3 = em("tensor", lambda e, j=j, rb=rb: e.transpose(
                    out=s.pT1[0:16, j * 128:(j + 1) * 128],
                    in_=s.l16b[:, 128 + rb * 128 + j * 16:128 + rb * 128 + (j + 1) * 16],
                    identity=s.idb[:, :]),
                    waits=[("s", t_cast3[rb]), ("v", t_ev2)],
                    inc="t" if j == 7 else None)
            t_bt3[rb] = t_tr3
            t_ev2 = em("vector", lambda e: e.tensor_copy(out=l16c[0:16, 0:1024],
                                                         in_=s.pT1[0:16, 0:1024]),
                       waits=[("t", t_tr3), ("d_lr1", t_pc)], inc="v")
            t_pc = em("gpsimd", lambda e, b=b: e.dma_start(
                out=s.bfcT_hbm[0:16, b * 1024:(b + 1) * 1024],
                in_=l16c[0:16, 0:1024]),
                waits=[("v", t_ev2)], inc="d_lr1")
        self.t_ph1_end = dict(self.cnt)

        if self.debug:
            em("gpsimd", lambda e: e.dma_start(out=self.dbg["mod"][0:1, 0:1024],
                                               in_=s.garow[:]),
               waits=[("v", t_row["ga"])], inc="d_dbg")
            em("gpsimd", lambda e: e.dma_start(out=self.dbg["mod"][0:1, 1024:2048],
                                               in_=s.gbrow[:]),
               waits=[("v", t_row["gb"])], inc="d_dbg")
            em("gpsimd", lambda e: e.dma_start(out=self.dbg["cols"][:, 0:8],
                                               in_=s.ca1_col[:]),
               waits=[("v", t_row["ca_col"])], inc="d_dbg")
            em("gpsimd", lambda e: e.dma_start(out=self.dbg["cols"][:, 8:16],
                                               in_=s.sa_col[:]),
               waits=[("v", t_row["sa_col"])], inc="d_dbg")
            em("gpsimd", lambda e: e.dma_start(out=self.dbg["cols"][:, 16:24],
                                               in_=s.gbT_col[:]),
               waits=[("v", t_gbT)], inc="d_dbg")
            em("gpsimd", lambda e: e.dma_start(out=self.dbg["cols"][:, 24:32],
                                               in_=s.cb1_col[:]),
               waits=[("v", t_row["cb_col"])], inc="d_dbg")
            em("gpsimd", lambda e: e.dma_start(out=self.dbg["cols"][:, 32:40],
                                               in_=s.sbb_col[:]),
               waits=[("v", t_row["sb_col"])], inc="d_dbg")

    def ph3(self):
        em, s = self.em, self
        self.esP1.close()
        self._t_am_v = self.cnt["v"]
        self._t_am_v2 = [self.cnt["v"], self.cnt["v"]]
        self._t_am_b = [self.cnt["v"], self.cnt["v"]]
        self._am_idx = 0
        self.esW = ExitStack()
        self.wqT = self.esW.enter_context(self.nc.sbuf_tensor("wqT", [128, 8192], BF16))
        self.wkaT = self.esW.enter_context(self.nc.sbuf_tensor("wkaT", [128, 8192], BF16))
        self.wvbT = self.esW.enter_context(self.nc.sbuf_tensor("wvbT", [128, 8192], BF16))
        self.h_stage = self.esW.enter_context(self.nc.sbuf_tensor("h_stage", [128, 4096], BF16))
        self.x_st1 = self.esW.enter_context(self.nc.sbuf_tensor("x_st1", [128, 1024], F32))

        self.qev = [self.u_tm[:, 31744:32256], self.u_tm[:, 32256:32768]]
        self.x_stage = [self.x_st0, self.x_st1]
        fold1 = s.ak_fm[:, 0:8192]
        fold2 = s.ak_fm[:, 8192:16384]
        wtmp = s.u_tm[:, 9216:10240]

        t_xs = [0, 0]
        t_xs_eng = ["v", "v"]
        t_wtmp = 0
        t_pT = [self.cnt["t"], self.cnt["t"]]
        t_pT_ev = [self.cnt["s"], self.cnt["s"]]
        t_pT_ev = [0, 0]
        ph1_t = self.cnt["t"]
        ph1_v = self.cnt["v"]
        ph1_s = self.cnt["s"]
        bar = [(k, self.cnt.get(k, 0)) for k in ("d_bpr", "d_lr1", "d_l16", "d_wm",
                                                 "d_bpj", "d_ph0", "d_dbg")]

        def load_cast(src_ap, kc, dst_ap, p, extra_waits=()):
            t_d = em("sync", lambda e, src_ap=src_ap, p=p: e.dma_start(
                out=s.x_stage[p][:], in_=src_ap),
                waits=[(t_xs_eng[p], t_xs[p])] + list(extra_waits) + bar,
                inc="d_xs%d" % p)
            t_c = em("vector", lambda e, p=p, dst_ap=dst_ap: e.tensor_copy(
                out=dst_ap, in_=s.x_stage[p][:]),
                waits=[("d_xs%d" % p, t_d)] + bar, inc="v")
            t_xs[p] = t_c
            t_xs_eng[p] = "v"
            self.amax_pump(2)
            return t_c

        t_wk = []
        for kc in range(NCH):
            t_wk.append(load_cast(s.wk_in[kc * 128:(kc + 1) * 128, :], kc,
                                  fold1[:, kc * 1024:(kc + 1) * 1024], kc % 2))
        t_wa_ev = []
        for ot in range(NCH):
            t_c = load_cast(s.wa_in[ot * 128:(ot + 1) * 128, :], ot, wtmp, ot % 2,
                            extra_waits=[("t", t_wtmp)])
            pp = ot % 2
            pT = (s.pT0, s.pT1)[pp]
            t_tr = None
            for kc in range(NCH):
                t_tr = em("tensor", lambda e, kc=kc, pT=pT: e.transpose(
                    out=pT[:, kc * 128:(kc + 1) * 128],
                    in_=wtmp[:, kc * 128:(kc + 1) * 128], identity=s.idb[:, :]),
                    waits=[("v", t_c), ("s", t_pT_ev[pp]), ("t", ph1_t),
                           ("v", ph1_v)] + bar,
                    inc="t" if kc == NCH - 1 else None)
            t_wtmp = t_tr
            t_pT[pp] = t_tr
            t_ev = None
            for kc in range(NCH):
                t_ev = em("scalar", lambda e, kc=kc, ot=ot, pT=pT: e.activation(
                    out=fold2[:, kc * 1024 + ot * 128:kc * 1024 + (ot + 1) * 128],
                    in_=pT[:, kc * 128:(kc + 1) * 128], func=AF.Copy),
                    waits=[("t", t_tr)] + bar, inc="s" if kc == NCH - 1 else None)
            t_pT_ev[pp] = t_ev
            t_wa_ev.append(t_ev)
        pE_free = [ph1_v] * 4
        t_fold_mm = []
        last_wka_mm = None
        for ic in range(NCH):
            for oh in range(2):
                sl = (ic * 2 + oh) % 4
                t_mm = None
                for kc in range(NCH):
                    t_mm = em("tensor", lambda e, ic=ic, oh=oh, kc=kc, sl=sl: e.matmul(
                        out=s.pE[:, sl * 512:(sl + 1) * 512],
                        lhsT=fold1[:, kc * 1024 + ic * 128:kc * 1024 + (ic + 1) * 128],
                        rhs=fold2[:, kc * 1024 + oh * 512:kc * 1024 + (oh + 1) * 512],
                        start=(kc == 0), stop=(kc == NCH - 1)),
                        waits=[("v", t_wk[-1]), ("s", t_wa_ev[-1]),
                               ("v", pE_free[sl])],
                        inc="t" if kc == NCH - 1 else None)
                last_wka_mm = t_mm
                pE_free[sl] = em("vector", lambda e, ic=ic, oh=oh, sl=sl: e.tensor_copy(
                    out=s.wkaT[:, ic * 1024 + oh * 512:ic * 1024 + (oh + 1) * 512],
                    in_=s.pE[:, sl * 512:(sl + 1) * 512]),
                    waits=[("t", t_mm)], inc="v")
                self.amax_pump(2)
        t_wka_done = pE_free[:]
        t_wv = []
        for kc in range(NCH):
            t_wv.append(load_cast(s.wv_in[kc * 128:(kc + 1) * 128, :], kc,
                                  fold1[:, kc * 1024:(kc + 1) * 1024], kc % 2,
                                  extra_waits=[("t", last_wka_mm)]))
        t_wb = []
        for ot in range(NCH):
            t_c = load_cast(s.wb_in[ot * 128:(ot + 1) * 128, :], ot, wtmp, ot % 2,
                            extra_waits=[("t", t_wtmp), ("t", last_wka_mm)])
            pp = ot % 2
            pT = (s.pT0, s.pT1)[pp]
            t_tr = None
            for kc in range(NCH):
                t_tr = em("tensor", lambda e, kc=kc, pT=pT: e.transpose(
                    out=pT[:, kc * 128:(kc + 1) * 128],
                    in_=wtmp[:, kc * 128:(kc + 1) * 128], identity=s.idb[:, :]),
                    waits=[("v", t_c), ("s", t_pT_ev[pp]), ("t", last_wka_mm)],
                    inc="t" if kc == NCH - 1 else None)
            t_wtmp = t_tr
            t_pT[pp] = t_tr
            t_ev = None
            for kc in range(NCH):
                t_ev = em("scalar", lambda e, kc=kc, ot=ot, pT=pT: e.activation(
                    out=fold2[:, kc * 1024 + ot * 128:kc * 1024 + (ot + 1) * 128],
                    in_=pT[:, kc * 128:(kc + 1) * 128], func=AF.Copy),
                    waits=[("t", t_tr)], inc="s" if kc == NCH - 1 else None)
            t_pT_ev[pp] = t_ev
            t_wb.append(t_ev)
        last_wvb_mm = None
        for ic in range(NCH):
            for oh in range(2):
                sl = (ic * 2 + oh) % 4
                t_mm = None
                for kc in range(NCH):
                    t_mm = em("tensor", lambda e, ic=ic, oh=oh, kc=kc, sl=sl: e.matmul(
                        out=s.pE[:, sl * 512:(sl + 1) * 512],
                        lhsT=fold1[:, kc * 1024 + ic * 128:kc * 1024 + (ic + 1) * 128],
                        rhs=fold2[:, kc * 1024 + oh * 512:kc * 1024 + (oh + 1) * 512],
                        start=(kc == 0), stop=(kc == NCH - 1)),
                        waits=[("v", t_wv[-1]), ("s", t_wb[-1]), ("v", pE_free[sl])],
                        inc="t" if kc == NCH - 1 else None)
                last_wvb_mm = t_mm
                pE_free[sl] = em("vector", lambda e, ic=ic, oh=oh, sl=sl: e.tensor_tensor(
                    out=s.wvbT[:, ic * 1024 + oh * 512:ic * 1024 + (oh + 1) * 512],
                    in0=s.pE[:, sl * 512:(sl + 1) * 512],
                    in1=s.trash[:, oh * 512:(oh + 1) * 512], op=OP.mult),
                    waits=[("t", t_mm), ("v", self.t_gabc)], inc="v")
                self.amax_pump(2)
        for ot in range(NCH):
            t_c = load_cast(s.wq_in[ot * 128:(ot + 1) * 128, :], ot, wtmp, ot % 2,
                            extra_waits=[("t", t_wtmp)])
            pp = ot % 2
            pT = (s.pT0, s.pT1)[pp]
            t_tr = None
            for kc in range(NCH):
                t_tr = em("tensor", lambda e, kc=kc, pT=pT: e.transpose(
                    out=pT[:, kc * 128:(kc + 1) * 128],
                    in_=wtmp[:, kc * 128:(kc + 1) * 128], identity=s.idb[:, :]),
                    waits=[("v", t_c), ("s", t_pT_ev[pp]), ("v", last_wvb_mm and 0 or 0),
                           ("t", last_wvb_mm)],
                    inc="t" if kc == NCH - 1 else None)
            t_wtmp = t_tr
            t_ev = None
            for kc in range(NCH):
                t_ev = em("scalar", lambda e, kc=kc, ot=ot, pT=pT: e.activation(
                    out=s.wqT[:, kc * 1024 + ot * 128:kc * 1024 + (ot + 1) * 128],
                    in_=pT[:, kc * 128:(kc + 1) * 128], func=AF.Copy),
                    waits=[("t", t_tr)], inc="s" if kc == NCH - 1 else None)
            t_pT_ev[pp] = t_ev
        self.t_ph3 = {"v": self.cnt["v"], "s": self.cnt["s"], "t": self.cnt["t"]}
        self.pE_free = pE_free
        self.t_xs = t_xs
        self.t_xs_eng = t_xs_eng
        self.t_pT = t_pT
        self.t_pT_ev = t_pT_ev

        if self.debug:
            em("gpsimd", lambda e: e.dma_start(out=self.dbg["qc"], in_=s.wqT[:, 0:1024]),
               waits=[("s", self.t_ph3["s"])], inc="d_dbg")


    def amax_pump(self, n=1):
        em, s = self.em, self
        for _ in range(n):
            idx = self._am_idx
            if idx >= 256:
                return
            self._am_idx += 1
            p = idx % 2
            k = idx // 2
            if p == 0:
                ot, hf = k // 4, k % 4
                src_ap = s.wfc_in[ot * 128:(ot + 1) * 128, hf * 256:(hf + 1) * 256]
                c0, c1 = 0, 1
            else:
                dc, rest = k // 16, k % 16
                src_ap = s.wpj_in[dc * 128:(dc + 1) * 128, rest * 256:(rest + 1) * 256]
                c0, c1 = 2, 3
            first = idx < 2
            t_d = em("gpsimd", lambda e, src_ap=src_ap, p=p: e.dma_start(
                out=s.amst[p][:], in_=src_ap),
                waits=[("v", self._t_am_b[p])], inc="d_am%d" % p)
            t_r = em("vector", lambda e, c0=c0, c1=c1, first=first, p=p: e.tensor_reduce(
                out=s.amx[:, (c0 if first else c1):(c0 if first else c1) + 1],
                in_=s.amst[p][:], axis=AX.X, op=OP.max, apply_absolute_value=True),
                waits=[("d_am%d" % p, t_d), ("v", self._t_am_v2[p])], inc="v")
            if not first:
                t_r = em("vector", lambda e, c0=c0, c1=c1: e.tensor_tensor(
                    out=s.amx[:, c0:c0 + 1], in0=s.amx[:, c0:c0 + 1],
                    in1=s.amx[:, c1:c1 + 1], op=OP.max),
                    waits=[("v", t_r)], inc="v")
            self._t_am_b[p] = t_r
            self._t_am_v2[p] = t_r
            self._t_am_v = t_r

    def ph4(self):
        em, s = self.em, self
        t_xs, t_xs_eng = self.t_xs, self.t_xs_eng
        t_pT, t_pT_ev = self.t_pT, self.t_pT_ev
        pE_free = self.pE_free
        pO_free = [self.t_gabc, self.t_bpjgb]
        pO_free = [self.cnt["v"], self.cnt["v"]]
        t_trash_free = self.t_ph3["v"]
        t_hst_mm = 0
        t_stats = [0, 0]
        t_xnb = 0
        t_qev = [0, 0]
        self.q_done = []
        t_akev = t_uev = t_qv = 0
        for g in range(NG):
            t_evac_grp = []
            gpar = 16 * (g % 2)
            for pair in range(2):
              t_sumv = 0
              t_sq = 0
              for tt2 in range(2):
                tt = pair * 2 + tt2
                kt = g * 4 + tt
                p = kt % 2
                cb = gpar + tt * 4
                t_d = em("sync", lambda e, kt=kt, p=p: e.dma_start(
                    out=s.x_stage[p][:], in_=s.x_in[kt * 128:(kt + 1) * 128, :]),
                    waits=[(t_xs_eng[p], t_xs[p])], inc="d_xs%d" % p)
                t_sumv = em("vector", lambda e, p=p, cb=cb: e.reduce_sum(
                    out=s.stats[:, cb:cb + 1], in_=s.x_stage[p][:], axis=AX.X),
                    waits=[("d_xs%d" % p, t_d), ("v", t_stats[g % 2])], inc="v")
                t_sq = em("scalar", lambda e, p=p, cb=cb: e.activation(
                    out=s.trash[:], in_=s.x_stage[p][:], func=AF.Square,
                    accum_out=s.stats[:, cb + 1:cb + 2]),
                    waits=[("d_xs%d" % p, t_d), ("v", t_trash_free),
                           ("s", getattr(self, "_t_sq_prev", 0)),
                           ("v", t_stats[g % 2])], inc="s")
                self._t_sq_prev = t_sq
              pb0 = gpar + pair * 8
              t_mv = em("vector", lambda e, pb0=pb0: e.tensor_scalar(
                  out=s.stats[:, pb0 + 2:pb0 + 8:4],
                  in0=s.stats[:, pb0 + 0:pb0 + 8:4],
                  scalar1=1.0 / 1024, scalar2=None, op0=OP.mult),
                  waits=[("v", t_sumv), ("s", t_sq)], inc="v")
              t_v1 = em("vector", lambda e, pb0=pb0: e.tensor_scalar(
                  out=s.stats[:, pb0 + 3:pb0 + 8:4],
                  in0=s.stats[:, pb0 + 1:pb0 + 8:4],
                  scalar1=1.0 / 1024, scalar2=None, op0=OP.mult),
                  waits=[("v", t_mv)], inc="v")
              t_ms = em("vector", lambda e, pb0=pb0: e.tensor_tensor(
                  out=s.stats[:, pb0 + 1:pb0 + 8:4],
                  in0=s.stats[:, pb0 + 2:pb0 + 8:4],
                  in1=s.stats[:, pb0 + 2:pb0 + 8:4], op=OP.mult),
                  waits=[("v", t_v1)], inc="v")
              t_var = em("vector", lambda e, pb0=pb0: e.tensor_tensor(
                  out=s.stats[:, pb0 + 0:pb0 + 8:4],
                  in0=s.stats[:, pb0 + 3:pb0 + 8:4],
                  in1=s.stats[:, pb0 + 1:pb0 + 8:4], op=OP.subtract),
                  waits=[("v", t_ms)], inc="v")
              t_std = em("scalar", lambda e, pb0=pb0: e.activation(
                  out=s.stats[:, pb0 + 1:pb0 + 8:4],
                  in_=s.stats[:, pb0 + 0:pb0 + 8:4],
                  func=AF.Sqrt, bias=s.kcol[:, 0:1]),
                  waits=[("v", t_var), ("s", self._t_sq_prev)], inc="s")
              self._t_sq_prev = t_std
              t_rstd = em("vector", lambda e, pb0=pb0: e.reciprocal(
                  out=s.stats[:, pb0 + 3:pb0 + 8:4],
                  in_=s.stats[:, pb0 + 1:pb0 + 8:4]),
                  waits=[("s", t_std)], inc="v")
              for tt2 in range(2):
                tt = pair * 2 + tt2
                kt = g * 4 + tt
                p = kt % 2
                cb = gpar + tt * 4
                t_xn = em("vector", lambda e, p=p, cb=cb: e.tensor_scalar(
                    out=s.xnb[:], in0=s.x_stage[p][:],
                    scalar1=s.stats[:, cb + 2:cb + 3],
                    scalar2=s.stats[:, cb + 3:cb + 4],
                    op0=OP.subtract, op1=OP.mult),
                    waits=[("v", t_rstd), ("t", t_xnb)], inc="v")
                t_xs[p] = t_xn
                t_xs_eng[p] = "v"
                pp = kt % 2
                pT = (s.pT0, s.pT1)[pp]
                t_tr = None
                for ic in range(NCH):
                    t_tr = em("tensor", lambda e, ic=ic, pT=pT: e.transpose(
                        out=pT[:, ic * 128:(ic + 1) * 128],
                        in_=s.xnb[:, ic * 128:(ic + 1) * 128], identity=s.idb[:, :]),
                        waits=[("v", t_xn), ("s", t_pT_ev[pp]), ("t", t_hst_mm)],
                        inc="t" if ic == NCH - 1 else None)
                t_xnb = t_tr
                t_pT[pp] = t_tr
                t_ev = None
                for ic in range(NCH):
                    t_ev = em("scalar", lambda e, ic=ic, tt=tt, pT=pT: e.activation(
                        out=s.h_stage[:, ic * 512 + tt * 128:ic * 512 + (tt + 1) * 128],
                        in_=pT[:, ic * 128:(ic + 1) * 128], func=AF.Identity,
                        bias=s.sa_col[:, ic:ic + 1], scale=s.ca1_col[:, ic:ic + 1]),
                        waits=[("t", t_tr)], inc="s" if ic == NCH - 1 else None)
                t_pT_ev[pp] = t_ev
                t_evac_grp.append(t_ev)
            t_stats[g % 2] = self.cnt["v"]
            t_hs = t_evac_grp[-1]
            for oc in range(NCH):
                sl = oc % 4
                t_mm = None
                for ic in range(NCH):
                    t_mm = em("tensor", lambda e, oc=oc, ic=ic, g=g, sl=sl: e.matmul(
                        out=s.pE[:, sl * 512:(sl + 1) * 512],
                        lhsT=s.wkaT[:, ic * 1024 + oc * 128:ic * 1024 + (oc + 1) * 128],
                        rhs=s.h_stage[:, ic * 512:(ic + 1) * 512],
                        start=(ic == 0), stop=(ic == NCH - 1)),
                        waits=[("s", t_hs), ("v", pE_free[sl])],
                        inc="t" if ic == NCH - 1 else None)
                t_hst_mm = t_mm
                pE_free[sl] = em("vector", lambda e, oc=oc, g=g, sl=sl: e.tensor_copy(
                    out=s.ak_fm[:, oc * 4096 + g * 512:oc * 4096 + (g + 1) * 512],
                    in_=s.pE[:, sl * 512:(sl + 1) * 512]),
                    waits=[("t", t_mm)], inc="v")
                t_akev = pE_free[sl]
                self.amax_pump(1)
            for tt in range(4):
                for vc in range(2):
                    t_mm = None
                    for ic in range(NCH):
                        t_mm = em("tensor", lambda e, tt=tt, vc=vc, ic=ic, g=g: e.matmul(
                            out=s.pO[:, vc * 512:(vc + 1) * 512],
                            lhsT=s.h_stage[:, ic * 512 + tt * 128:ic * 512 + (tt + 1) * 128],
                            rhs=s.wvbT[:, ic * 1024 + vc * 512:ic * 1024 + (vc + 1) * 512],
                            start=(ic == 0), stop=(ic == NCH - 1)),
                            waits=[("s", t_hs), ("v", pO_free[vc])],
                            inc="t" if ic == NCH - 1 else None)
                    t_hst_mm = t_mm
                    uev_w = [("t", t_mm)]
                    if g * 4 + tt == 31:
                        uev_w += [("d_qe0", self.cnt.get("d_qe0", 0)),
                                  ("d_qe1", self.cnt.get("d_qe1", 0))]
                    pO_free[vc] = em("vector", lambda e, tt=tt, vc=vc, g=g: e.tensor_copy(
                        out=s.u_tm[:, (g * 4 + tt) * 1024 + vc * 512:
                                   (g * 4 + tt) * 1024 + (vc + 1) * 512],
                        in_=s.pO[:, vc * 512:(vc + 1) * 512]),
                        waits=uev_w, inc="v")
                    t_uev = pO_free[vc]
                    self.amax_pump(1)
            if g < NGOWN:
                for oc in range(NCH):
                    sl = oc % 4
                    t_mm = None
                    for ic in range(NCH):
                        t_mm = em("tensor", lambda e, oc=oc, ic=ic, g=g, sl=sl: e.matmul(
                            out=s.pE[:, sl * 512:(sl + 1) * 512],
                            lhsT=s.wqT[:, ic * 1024 + oc * 128:ic * 1024 + (oc + 1) * 128],
                            rhs=s.h_stage[:, ic * 512:(ic + 1) * 512],
                            start=(ic == 0), stop=(ic == NCH - 1)),
                            waits=[("s", t_hs), ("v", pE_free[sl])],
                            inc="t" if ic == NCH - 1 else None)
                    t_hst_mm = t_mm
                    qp = oc % 2
                    t_qv = em("vector", lambda e, oc=oc, g=g, sl=sl, qp=qp: e.tensor_copy(
                        out=s.qev[qp][:], in_=s.pE[:, sl * 512:(sl + 1) * 512]),
                        waits=[("t", t_mm), ("d_qe%d" % qp, t_qev[qp])], inc="v")
                    pE_free[sl] = t_qv
                    for iq in range(4):
                        t_qev[qp] = em("gpsimd", lambda e, oc=oc, g=g, qp=qp, iq=iq:
                                       e.dma_start(
                                           out=s.q_hbm[g * 4 + iq, :, oc, :],
                                           in_=s.qev[qp][:, iq * 128:(iq + 1) * 128]),
                                       waits=[("v", t_qv)], inc="d_qe%d" % qp)
        self.pE_free = pE_free
        self.pO_free = pO_free
        self.t_hst_mm = t_hst_mm
        self.t_pT = t_pT
        self.t_pT_ev = t_pT_ev
        self.t_ph4 = {"v": self.cnt["v"], "s": self.cnt["s"], "t": self.cnt["t"],
                      "ak": t_akev, "u": t_uev}
        if self.debug:
            em("gpsimd", lambda e: e.dma_start(out=self.dbg["hst"], in_=s.h_stage[:]),
               waits=[("s", self.cnt["s"]), ("t", t_hst_mm)], inc="d_dbg")
            em("gpsimd", lambda e: e.dma_start(out=self.dbg["akc"],
                                               in_=s.ak_fm[:, 0:1024]),
               waits=[("v", t_akev)], inc="d_dbg")
            em("gpsimd", lambda e: e.dma_start(out=self.dbg["uc"],
                                               in_=s.u_tm[:, 0:1024]),
               waits=[("v", t_uev)], inc="d_dbg")

    def ph5_alloc(self):
        em, s = self.em, self
        self.esW.close()
        self.esA = ExitStack()
        def sba(name, shape, dt):
            return self.esA.enter_context(self.nc.sbuf_tensor(name, shape, dt))
        self.alphaA = sba("alphaA", [128, 2048], BF16)
        self.alphaB = sba("alphaB", [128, 2048], BF16)
        self.alphaT = sba("alphaT", [128, 4096], BF16)
        self.q_stage = [sba("q_stg%d" % i, [128, 1024], BF16) for i in range(2)]
        self.x1b = sba("x1b", [128, 1024], F32)
        self.wstb = [sba("wst%d" % i, [128, 1024], F32) for i in range(2)]
        self.qt = [sba("qt%d" % i, [128, 1024], F32) for i in range(2)]
        self.qki = sba("qki", [128, 512], I32)
        self.qkf = [sba("qkf%d" % i, [128, 1024], F32) for i in range(2)]
        self.qbf = [sba("qbf%d" % i, [128, 1024], BF16) for i in range(5)]

    def quant_finalize(self):
        em, s = self.em, self
        while self._am_idx < 256:
            self.amax_pump(8)
        t_g = self._t_am_v
        t_prev_t = self.cnt["t"]
        t_prev_ev = 0
        for mi, (acol, dstc) in enumerate(((0, 8), (2, 9))):
            t_tr = em("tensor", lambda e, acol=acol: e.transpose(
                out=s.pO[0:1, 0:128], in_=s.amx[:, acol:acol + 1],
                identity=s.idf[:, :]),
                waits=[("v", t_g), ("v", t_prev_ev), ("t", t_prev_t),
                       ("v", self.t_ph4["v"]),
                       ("s", self.t_ph4["s"]),
                       ("d_dbg", self.cnt.get("d_dbg", 0))], inc="t")
            t_r = em("vector", lambda e: e.tensor_reduce(
                out=s.kcol[0:1, 10:11], in_=s.pO[0:1, 0:128], axis=AX.X, op=OP.max),
                waits=[("t", t_tr)], inc="v")
            t_bc = em("tensor", lambda e: e.matmul(
                out=s.pO[:, 0:1], lhsT=s.onesf[:], rhs=s.kcol[0:1, 10:11],
                start=True, stop=True),
                waits=[("v", t_r)], inc="t")
            t_v = em("vector", lambda e, dstc=dstc: e.tensor_copy(
                out=s.kcol[:, dstc:dstc + 1], in_=s.pO[:, 0:1]),
                waits=[("t", t_bc)], inc="v")
            t_prev_t = t_bc
            t_prev_ev = t_v
        t_r1 = em("vector", lambda e: e.reciprocal(out=s.kcol[:, 10:11],
                                                   in_=s.kcol[:, 8:9]),
                  waits=[("v", t_v)], inc="v")
        t_m127f = em("vector", lambda e: e.tensor_scalar(
            out=s.kcol[:, 5:6], in0=s.kcol[:, 10:11], scalar1=127.0, scalar2=None,
            op0=OP.mult), waits=[("v", t_r1)], inc="v")
        t_r2 = em("vector", lambda e: e.reciprocal(out=s.kcol[:, 10:11],
                                                   in_=s.kcol[:, 9:10]),
                  waits=[("v", t_m127f)], inc="v")
        t_m127p = em("vector", lambda e: e.tensor_scalar(
            out=s.kcol[:, 7:8], in0=s.kcol[:, 10:11], scalar1=127.0, scalar2=None,
            op0=OP.mult), waits=[("v", t_r2)], inc="v")
        t_qsf = em("vector", lambda e: e.tensor_tensor(
            out=s.kcol[:, 6:7], in0=s.kcol[:, 8:9], in1=s.kcol[:, 4:5], op=OP.mult),
            waits=[("v", t_m127p)], inc="v")
        t_pjq = em("vector", lambda e: e.tensor_tensor(
            out=s.kcol[:, 10:11], in0=s.kcol[:, 9:10], in1=s.kcol[:, 4:5], op=OP.mult),
            waits=[("v", t_qsf)], inc="v")
        t_pjs = em("vector", lambda e: e.tensor_scalar(
            out=s.pjscale_col[:, 0:8], in0=s.gbT_col[:, 0:8],
            scalar1=s.kcol[:, 10:11], scalar2=None, op0=OP.mult),
            waits=[("v", t_pjq), ("v", self.t_gbT)], inc="v")
        self.t_qfin = t_pjs
        if self.debug:
            em("gpsimd", lambda e: e.dma_start(out=self.dbg["amax"][:, 0:4],
                                               in_=s.amx[:, 0:4]),
               waits=[("v", t_pjs)], inc="d_dbg")

    def _qsrc(self, idx):
        s = self
        if idx < 32:
            return (s.wfc_in[idx * 128:(idx + 1) * 128, :],
                    s.kcol[:, 5:6], s.kcol[:, 6:7])
        k = idx - 32
        dc, pc = k // 4, k % 4
        return (s.wpj_in[dc * 128:(dc + 1) * 128, pc * 1024:(pc + 1) * 1024],
                s.kcol[:, 7:8], s.pjscale_col[:, dc:dc + 1])

    def quant_stageA(self, idx):
        em, s = self.em, self
        st = self._qstate
        p = idx % 2
        qt, qkf = s.qt[p], s.qkf[p]
        _, m127, _ = self._qsrc(idx)
        if idx == 0:
            src_ap, _, _ = self._qsrc(0)
            st["dws"][0] = em("sync", lambda e, src_ap=src_ap: e.dma_start(
                out=s.wstb[0][:], in_=src_ap),
                waits=[("v", self.t_qfin)], inc="d_ws0")
        if idx + 1 < 64:
            src_ap, _, _ = self._qsrc(idx + 1)
            pn = (idx + 1) % 2
            st["dws"][pn] = em("sync", lambda e, src_ap=src_ap, pn=pn: e.dma_start(
                out=s.wstb[pn][:], in_=src_ap),
                waits=[("v", st["e1_%d" % pn]), ("v", self.t_qfin)],
                inc="d_ws%d" % pn)
        t_t = em("vector", lambda e, p=p, m127=m127, qt=qt: e.tensor_scalar(
            out=qt[:], in0=s.wstb[p][:], scalar1=m127, scalar2=None, op0=OP.mult),
            waits=[("d_ws%d" % p, st["dws"][p]), ("s", st["s12%d" % p]),
                   ("v", st["qq%d" % p]), ("g", st["d%d" % p])], inc="v")
        st["wst"][p] = t_t
        t_kf = None
        for hh in range(2):
            t_ki = em("vector", lambda e, qt=qt, hh=hh: e.tensor_copy(
                out=s.qki[:], in_=qt[:, hh * 512:(hh + 1) * 512]),
                waits=[("v", t_t), ("s", st["kf"] if hh == 0 else t_kf)], inc="v")
            t_kf = em("scalar", lambda e, qkf=qkf, hh=hh: e.activation(
                out=qkf[:, hh * 512:(hh + 1) * 512], in_=s.qki[:], func=AF.Copy),
                waits=[("v", t_ki), ("v", st["qq%d" % p]), ("s", st["q%d" % p]),
                       ("g", st["d%d" % p])],
                inc="s")
        st["kf"] = t_kf
        t_dd = em("gpsimd", lambda e, qt=qt, qkf=qkf: e.tensor_tensor(
            out=qt[:], in0=qt[:], in1=qkf[:], op=OP.subtract),
            waits=[("s", t_kf)], inc="g")
        st["d%d" % p] = t_dd

    def quant_stageB1(self, idx):
        em, s = self.em, self
        st = self._qstate
        p = idx % 2
        qt, wsb = s.qt[p], s.wstb[p]
        t_s1 = em("scalar", lambda e, qt=qt, wsb=wsb: e.activation(
            out=wsb[:], in_=qt[:], func=AF.Sign, bias=s.kcol[:, 1:2]),
            waits=[("g", st["d%d" % p]), ("v", st["wst"][p])], inc="s")
        t_s2 = em("scalar", lambda e, qt=qt: e.activation(
            out=qt[:], in_=qt[:], func=AF.Sign, bias=s.kcol[:, 2:3]),
            waits=[("s", t_s1)], inc="s")
        st["s12%d" % p] = t_s2
        st["ts1_%d" % p] = t_s1

    def quant_stageB2(self, idx):
        em, s = self.em, self
        st = self._qstate
        p = idx % 2
        qt, qkf, wsb = s.qt[p], s.qkf[p], s.wstb[p]
        t_e1 = em("vector", lambda e, qkf=qkf, wsb=wsb: e.scalar_tensor_tensor(
            out=qkf[:], in0=wsb[:], scalar=0.5, in1=qkf[:],
            op0=OP.mult, op1=OP.add),
            waits=[("s", st["ts1_%d" % p]), ("s", st["q%d" % p])], inc="v")
        st["e1_%d" % p] = t_e1
        t_q = em("vector", lambda e, qkf=qkf, qt=qt: e.scalar_tensor_tensor(
            out=qkf[:], in0=qt[:], scalar=0.5, in1=qkf[:],
            op0=OP.mult, op1=OP.add),
            waits=[("s", st["s12%d" % p]), ("v", t_e1)], inc="v")
        st["qq%d" % p] = t_q

    def quant_stageB3(self, idx):
        em, s = self.em, self
        st = self._qstate
        p = idx % 2
        pp = idx % 5
        qkf, qb = s.qkf[p], s.qbf[pp]
        _, _, scale = self._qsrc(idx)
        t_b = em("scalar", lambda e, scale=scale, qb=qb, qkf=qkf: e.activation(
            out=qb[:], in_=qkf[:], func=AF.Identity, scale=scale),
            waits=[("v", st["qq%d" % p]),
                   ("d_pc0", self.cnt.get("d_pc0", 0))], inc="s")
        st["q%d" % p] = t_b
        st["qb"][pp] = t_b

    def quant_slot(self, k, do_store=True):
        if do_store:
            while self._store_next <= k - 6 and self._store_next < 64:
                self.quant_tile_store(self._store_next)
                self._store_next += 1
        if 0 <= k - 1 < 64:
            self.quant_stageB1(k - 1)
            self.quant_stageB2(k - 1)
        if 0 <= k - 2 < 64:
            self.quant_stageB3(k - 2)
        if k < 64:
            self.quant_stageA(k)

    def quant_tile_store(self, idx):
        em, s = self.em, self
        st = self._qstate
        pp = idx % 5
        pT = s.pT0
        t_b = st["qb"][pp]
        qq = 0
        t_tr = None
        for j in range(8):
            t_tr = em("tensor", lambda e, j=j, pT=pT, pp=pp: e.transpose(
                out=pT[:, j * 128:(j + 1) * 128],
                in_=s.qbf[pp][:, j * 128:(j + 1) * 128], identity=s.idb[:, :]),
                waits=[("s", t_b), self.pTst["free"][qq]],
                inc="t" if j == 7 else None)
        self.pTst["tr"][qq] = t_tr
        t_ev = em("vector", lambda e, pT=pT, pp=pp: e.tensor_copy(
            out=s.qbf[pp][:], in_=pT[:]),
            waits=[("t", t_tr), ("d_pc0", self.cnt.get("d_pc0", 0))], inc="v")
        self.pTst["free"][qq] = ("v", t_ev)
        if idx < 32:
            dst = s.wfcq_hbm[idx]
        else:
            k = idx - 32
            dc, pc = k // 4, k % 4
            dst = s.wpjq_hbm[pc * 1024:(pc + 1) * 1024, dc * 128:(dc + 1) * 128] \
                .rearrange("(j p) o -> p j o", p=128)
        em("gpsimd", lambda e, dst=dst, pp=pp: e.dma_start(
            out=dst, in_=s.qbf[pp][:]),
            waits=[("v", t_ev)], inc="d_pc0")

    def ph5(self):
        em, s = self.em, self
        self.ph5_alloc()
        self.quant_finalize()
        self._qstate = {"wst": [self.t_qfin, self.t_qfin], "kf": 0,
                        "s120": 0, "s121": 0, "d0": 0, "d1": 0,
                        "q0": 0, "q1": 0, "qq0": 0, "qq1": 0,
                        "ts1_0": 0, "ts1_1": 0, "e1_0": 0, "e1_1": 0,
                        "qb": [0, 0, 0, 0, 0], "dws": [0, 0]}
        qslot_next = [0]

        ph4 = self.t_ph4
        self.pTst = {"free": [("v", self.cnt["v"]), ("v", self.cnt["v"])],
                     "tr": [self.cnt["t"], self.cnt["t"]]}
        st = {
            "expB": 0,
            "expA": 0,
            "transA_done": 0,
            "transB_done": 0,
            "alphaT_ev": [],

            "pO_stt": self.cnt["v"],
            "lcol_stt": 0,
            "lcol_stt2": [0, 0],
            "rescale": 0,
            "linv": 0,
            "x0": 0,
            "qs": [0, 0],
            "qs_mm": [ph4["t"], ph4["t"]],
            "x1": 0,
        }
        qbar = [("d_qe0", self.cnt.get("d_qe0", 0)), ("d_qe1", self.cnt.get("d_qe1", 0)),
                ("d_dbg", self.cnt.get("d_dbg", 0))]

        def emit_trans(half, ic_chunk, alpha, t_alpha_ready, eng_ready):
            evs = []
            t_tr = None
            for grp in range(2):
                pp = 1
                pT = s.pT1
                for j in range(8):
                    jb = grp * 8 + j
                    t_tr = em("tensor", lambda e, jb=jb, j=j, pT=pT, alpha=alpha:
                              e.transpose(
                                  out=pT[:, j * 128:(j + 1) * 128],
                                  in_=alpha[:, jb * 128:(jb + 1) * 128],
                                  identity=s.idb[:, :]),
                              waits=[(eng_ready, t_alpha_ready),
                                     self.pTst["free"][pp]],
                              inc="t" if j == 7 else None)
                self.pTst["tr"][pp] = t_tr
                ev = em("vector", lambda e, half=half, grp=grp, pT=pT: e.tensor_copy(
                    out=s.alphaT[:, half * 2048 + grp * 1024:
                                 half * 2048 + (grp + 1) * 1024],
                    in_=pT[:]),
                    waits=[("t", t_tr)], inc="v")
                self.pTst["free"][pp] = ("v", ev)
                evs.append(ev)
            return t_tr, evs

        self._store_next = 0

        def pump1(do_store=True):
            if qslot_next[0] < 70:
                self.quant_slot(qslot_next[0], do_store)
                qslot_next[0] += 1

        for ic in range(NQT):
            p = ic % 2
            pump1(False)
            t_q = em("sync", lambda e, ic=ic, p=p: e.dma_start(
                out=s.q_stage[p][:], in_=s.q_hbm[ic]),
                waits=[("t", st["qs_mm"][p])] + qbar, inc="d_q%d" % p)
            st["qs"][p] = t_q
            t_eA = None
            for oc in range(NCH):
                for jb in range(4):
                    t_eA = em("tensor", lambda e, oc=oc, jb=jb, p=p: e.matmul(
                        out=s.pE[:, jb * 512:(jb + 1) * 512],
                        lhsT=s.q_stage[p][:, oc * 128:(oc + 1) * 128],
                        rhs=s.ak_fm[:, oc * 4096 + jb * 512:oc * 4096 + (jb + 1) * 512],
                        start=(oc == 0), stop=(oc == NCH - 1)),
                        waits=[("d_q%d" % p, t_q), ("s", st["expB"]),
                               ("v", ph4["ak"])],
                        inc="t" if (oc == NCH - 1 and jb == 3) else None)
            t_mA = em("vector", lambda e: e.tensor_reduce(
                out=s.mcol[:, 0:1], in_=s.pE[:], axis=AX.X, op=OP.max, negate=True),
                waits=[("t", t_eA), ("s", st["expA"]), ("v", st["rescale"])], inc="v")
            pump1()
            lb = (ic % 2) * 4
            t_expA = em("scalar", lambda e, lb=lb: e.activation(
                out=s.alphaA[:], in_=s.pE[:], func=AF.Exp, bias=s.mcol[:, 0:1],
                accum_out=s.lcolv[:, lb:lb + 1]),
                waits=[("v", t_mA), ("t", st["transA_done"]), ("v", st["linv"]),
                       ("v", st["lcol_stt2"][ic % 2]),
                       ("d_dbg", self.cnt.get("d_dbg", 0) if ic == 0 else 0)], inc="s")
            st["expA"] = t_expA
            if ic > 0:
                self._emit_pv(ic - 1, st, emit_trans, part=0)
            t_eB = None
            for oc in range(NCH):
                for jb in range(4):
                    t_eB = em("tensor", lambda e, oc=oc, jb=jb, p=p: e.matmul(
                        out=s.pE[:, jb * 512:(jb + 1) * 512],
                        lhsT=s.q_stage[p][:, oc * 128:(oc + 1) * 128],
                        rhs=s.ak_fm[:, oc * 4096 + 2048 + jb * 512:
                                    oc * 4096 + 2048 + (jb + 1) * 512],
                        start=(oc == 0), stop=(oc == NCH - 1)),
                        waits=[("s", t_expA)],
                        inc="t" if (oc == NCH - 1 and jb == 3) else None)
            st["qs_mm"][p] = t_eB
            t_mB = em("vector", lambda e: e.tensor_reduce(
                out=s.mcol[:, 2:3], in_=s.pE[:], axis=AX.X, op=OP.max, negate=True),
                waits=[("t", t_eB), ("s", st["expA"])], inc="v")
            t_nm = em("vector", lambda e: e.tensor_tensor(
                out=s.mcol[:, 1:2], in0=s.mcol[:, 0:1], in1=s.mcol[:, 2:3], op=OP.min),
                waits=[("v", t_mB)], inc="v")
            t_dc = em("vector", lambda e: e.tensor_tensor(
                out=s.mcol[:, 3:4], in0=s.mcol[:, 1:2], in1=s.mcol[:, 0:1],
                op=OP.subtract),
                waits=[("v", t_nm)], inc="v")
            t_expB = em("scalar", lambda e, lb=lb: e.activation(
                out=s.alphaB[:], in_=s.pE[:], func=AF.Exp, bias=s.mcol[:, 1:2],
                accum_out=s.lcolv[:, lb + 1:lb + 2]),
                waits=[("v", t_nm), ("t", st["transB_done"]),
                       ("v", st["lcol_stt2"][ic % 2])], inc="s")
            st["expB"] = t_expB
            t_sA = em("scalar", lambda e: e.activation(
                out=s.mcol[:, 4:5], in_=s.mcol[:, 3:4], func=AF.Exp),
                waits=[("v", t_dc)], inc="s")
            t_rs = em("vector", lambda e: e.tensor_scalar(
                out=s.alphaA[:], in0=s.alphaA[:], scalar1=s.mcol[:, 4:5],
                scalar2=None, op0=OP.mult),
                waits=[("s", t_sA), ("s", t_expA)], inc="v")
            st["rescale"] = t_rs
            t_l2 = em("vector", lambda e, lb=lb: e.tensor_tensor(
                out=s.lcolv[:, lb + 2:lb + 3], in0=s.lcolv[:, lb:lb + 1],
                in1=s.mcol[:, 4:5], op=OP.mult),
                waits=[("s", t_sA), ("s", t_expA), ("v", st["lcol_stt"])], inc="v")
            t_l = em("vector", lambda e, lb=lb: e.tensor_tensor(
                out=s.lcolv[:, lb + 2:lb + 3], in0=s.lcolv[:, lb + 2:lb + 3],
                in1=s.lcolv[:, lb + 1:lb + 2], op=OP.add),
                waits=[("v", t_l2), ("s", t_expB)], inc="v")
            t_linv = em("vector", lambda e, lb=lb: e.reciprocal(
                out=s.lcolv[:, lb + 3:lb + 4], in_=s.lcolv[:, lb + 2:lb + 3]),
                waits=[("v", t_l)], inc="v")
            st["linv"] = t_linv
            pump1()
            if ic > 0:
                self._emit_pv(ic - 1, st, emit_trans, part=1)
            pump1()
            tA, evA = emit_trans(0, ic, s.alphaA, t_rs, "v")
            st["transA_done"] = tA
            st["alphaT_ev"] = evA
            st["_ic_linv"] = t_linv
        self._emit_pv(NQT - 1, st, emit_trans, part=0)
        self._emit_pv(NQT - 1, st, emit_trans, part=1)
        while qslot_next[0] < 70:
            self.quant_slot(qslot_next[0])
            qslot_next[0] += 1
        self.t_ph5 = dict(self.cnt)
        self.st5 = st

    def _emit_pv(self, ic, st, emit_trans, part=0):
        em, s = self.em, self
        if part == 0:
            tB, evB = emit_trans(1, ic, s.alphaB, st["expB"], "s")
            st["transB_done"] = tB
            st["evB"] = evB
            st["xd"] = em("sync", lambda e, ic=ic: e.dma_start(
                out=s.x_st0[:], in_=s.x_in[ic * 128:(ic + 1) * 128, :]),
                waits=[("v", st["x0"])], inc="d_xr")
            for jc in range(16):
                for vc in range(2):
                    em("tensor", lambda e, jc=jc, vc=vc: e.matmul(
                        out=s.pO[:, vc * 512:(vc + 1) * 512],
                        lhsT=s.alphaT[:, jc * 128:(jc + 1) * 128],
                        rhs=s.u_tm[:, jc * 1024 + vc * 512:jc * 1024 + (vc + 1) * 512],
                        start=(jc == 0), stop=False),
                        waits=[("v", st["alphaT_ev"][-1]), ("v", evB[-1]),
                               ("v", st["pO_stt"]), ("v", self.t_ph4["u"])])
            return
        evB = st["evB"]
        t_xd = st["xd"]
        t_pv = None
        for jc in range(16, 32):
            for vc in range(2):
                t_pv = em("tensor", lambda e, jc=jc, vc=vc: e.matmul(
                    out=s.pO[:, vc * 512:(vc + 1) * 512],
                    lhsT=s.alphaT[:, jc * 128:(jc + 1) * 128],
                    rhs=s.u_tm[:, jc * 1024 + vc * 512:jc * 1024 + (vc + 1) * 512],
                    start=False, stop=(jc == 31)),
                    waits=[("v", st["alphaT_ev"][-1]), ("v", evB[-1]),
                           ("v", st["pO_stt"]), ("v", self.t_ph4["u"])],
                    inc="t" if (jc == 31 and vc == 1) else None)
        lb = (ic % 2) * 4
        t_s1 = em("vector", lambda e, lb=lb: e.scalar_tensor_tensor(
            out=s.x1b[:, 0:512], in0=s.pO[:, 0:512], scalar=s.lcolv[:, lb + 3:lb + 4],
            in1=s.x_st0[:, 0:512], op0=OP.mult, op1=OP.add),
            waits=[("t", t_pv), ("d_xr", t_xd), ("d_x1", st["x1"]),
                   ("d_dbg", st.get("x1dbg", 0)),
                   ("v", st.get("_ic_linv", 0))], inc="v")
        t_s2 = em("vector", lambda e, lb=lb: e.scalar_tensor_tensor(
            out=s.x1b[:, 512:1024], in0=s.pO[:, 512:1024],
            scalar=s.lcolv[:, lb + 3:lb + 4],
            in1=s.x_st0[:, 512:1024], op0=OP.mult, op1=OP.add),
            waits=[("v", t_s1)], inc="v")
        st["pO_stt"] = t_s2
        st["lcol_stt"] = t_s2
        st["lcol_stt2"][ic % 2] = t_s2
        st["x0"] = t_s2
        st["x1"] = em("gpsimd", lambda e, ic=ic: e.dma_start(
            out=s.x1_hbm[ic * 128:(ic + 1) * 128, :], in_=s.x1b[:]),
            waits=[("v", t_s2)], inc="d_x1")
        if self.debug and ic == 0:
            st["x1dbg"] = em("gpsimd", lambda e: e.dma_start(out=self.dbg["x1c"],
                                                             in_=s.x1b[:]),
                             waits=[("v", t_s2)], inc="d_dbg")

    def ph6(self):
        em, s = self.em, self
        self.esA.close()
        self.esR.close()
        self.esF = ExitStack()
        def sbf(name, shape, dt):
            return self.esF.enter_context(self.nc.sbuf_tensor(name, shape, dt))
        wpj = sbf("wpj_sb", [128, 32 * 1024], BF16)
        hid = sbf("hid", [128, 32 * 512], BF16)
        h2 = sbf("h2", [128, 4096], BF16)
        x1blk = sbf("x1blk", [128, 4096], F32)
        bpjgb = sbf("bpjgb", [128, 1024], F32)
        bfcT = sbf("bfcT_sb", [16, 4096], BF16)
        wfcst = [sbf("wfcst%d" % i, [128, 1024], BF16) for i in range(4)]
        xnb2 = sbf("xnb2", [128, 1024], BF16)
        outb = [sbf("outb0", [128, 1024], F32)]
        rowp = x1blk

        c5 = self.t_ph5
        bar6 = [(k, c5.get(k, 0)) for k in ("d_x1", "d_pc0", "d_pc1", "d_q0", "d_q1",
                                            "d_xr", "d_ws0", "d_ws1", "d_dbg",
                                            "d_am", "d_qe0", "d_qe1")]
        bar6e = [(k, c5[k]) for k in ("v", "s", "t", "g")]

        t_wpj = None
        for hc in range(32):
            t_wpj = em("sync", lambda e, hc=hc: e.dma_start(
                out=wpj[:, hc * 1024:(hc + 1) * 1024],
                in_=s.wpjq_hbm[hc * 128:(hc + 1) * 128, :]),
                waits=bar6, inc="d_wpj")
        t_bfcT = em("sync", lambda e: e.dma_start(out=bfcT[:], in_=s.bfcT_hbm),
                    waits=bar6 + [("d_lr1", self.cnt["d_lr1"])], inc="d_bfc")
        t_rp = em("sync", lambda e: e.dma_start(out=rowp[0:1, 0:1024],
                                                in_=s.bpjrow_hbm),
                  waits=bar6 + [("d_bpr", self.cnt["d_bpr"])], inc="d_rp")
        t_bcmm = None
        for hh in range(2):
            t_bcmm = em("tensor", lambda e, hh=hh: e.matmul(
                out=s.pO[:, hh * 512:(hh + 1) * 512], lhsT=s.onesf[:],
                rhs=rowp[0:1, hh * 512:(hh + 1) * 512], start=True, stop=True),
                waits=[("d_rp", t_rp), ("v", c5["v"]), ("s", c5["s"])] ,
                inc="t" if hh else None)
        t_bpjgb = em("vector", lambda e: e.tensor_copy(out=bpjgb[:], in_=s.pO[:]),
                     waits=[("t", t_bcmm)] + bar6, inc="v")

        t_stats = [c5["v"], c5["v"]]
        t_xnb2 = c5["t"]
        t_sq_prev = c5["s"]
        t_pT_ev = [c5["s"], c5["s"]]
        t_h2_mm = c5["t"]
        t_hid_mm = c5["t"]
        t_x1blk = [c5["v"]] * 4
        t_lora1_mm = c5["t"]
        t_lora1p_mm = c5["t"]
        t_wf = [0, 0, 0, 0]
        t_wf_mm = [c5["t"], c5["t"], c5["t"], c5["t"]]
        t_out = [0, 0]
        t_outv = [c5["v"], c5["v"]]
        pE_free = [c5["s"]] * 4
        pE_free_v = [c5["v"]] * 4
        t_pv_out = [t_bpjgb, t_bpjgb]
        for tb in range(4):
            t_xd = []
            for tt in range(4):
                w_extra = [("t", t_bcmm)] if (tb == 0 and tt == 0) else []
                t_xd.append(em("sync", lambda e, tb=tb, tt=tt: e.dma_start(
                    out=x1blk[:, tt * 1024:(tt + 1) * 1024],
                    in_=s.x1_hbm[tb * 512 + tt * 128:tb * 512 + (tt + 1) * 128, :]),
                    waits=[("v", t_x1blk[tt]), ("d_x1", c5["d_x1"])] + w_extra,
                    inc="d_xb%d" % tt))
            for tt in range(4):
                kt = tb * 4 + tt
                pb = 16 * (kt % 2)
                xap = x1blk[:, tt * 1024:(tt + 1) * 1024]
                t_sum = em("vector", lambda e, xap=xap, pb=pb: e.reduce_sum(
                    out=s.stats[:, pb:pb + 1], in_=xap, axis=AX.X),
                    waits=[("d_xb%d" % tt, t_xd[tt]), ("v", t_stats[kt % 2])] + bar6,
                    inc="v")
                t_sq = em("scalar", lambda e, xap=xap, pb=pb: e.activation(
                    out=xnb2[:], in_=xap, func=AF.Square,
                    accum_out=s.stats[:, pb + 1:pb + 2]),
                    waits=[("d_xb%d" % tt, t_xd[tt]), ("s", t_sq_prev), ("t", t_xnb2),
                           ("v", t_stats[kt % 2])], inc="s")
                t_sq_prev = t_sq
                t_m = em("vector", lambda e, pb=pb: e.tensor_scalar(
                    out=s.stats[:, pb + 2:pb + 3], in0=s.stats[:, pb:pb + 1],
                    scalar1=1.0 / 1024, scalar2=None, op0=OP.mult),
                    waits=[("v", t_sum)], inc="v")
                t_v1 = em("vector", lambda e, pb=pb: e.tensor_scalar(
                    out=s.stats[:, pb + 3:pb + 4], in0=s.stats[:, pb + 1:pb + 2],
                    scalar1=1.0 / 1024, scalar2=None, op0=OP.mult),
                    waits=[("s", t_sq), ("v", t_m)], inc="v")
                t_ms = em("vector", lambda e, pb=pb: e.tensor_tensor(
                    out=s.stats[:, pb + 4:pb + 5], in0=s.stats[:, pb + 2:pb + 3],
                    in1=s.stats[:, pb + 2:pb + 3], op=OP.mult),
                    waits=[("v", t_v1)], inc="v")
                t_var = em("vector", lambda e, pb=pb: e.tensor_tensor(
                    out=s.stats[:, pb + 5:pb + 6], in0=s.stats[:, pb + 3:pb + 4],
                    in1=s.stats[:, pb + 4:pb + 5], op=OP.subtract),
                    waits=[("v", t_ms)], inc="v")
                t_std = em("scalar", lambda e, pb=pb: e.activation(
                    out=s.stats[:, pb + 6:pb + 7], in_=s.stats[:, pb + 5:pb + 6],
                    func=AF.Sqrt, bias=s.kcol[:, 0:1]),
                    waits=[("v", t_var)], inc="s")
                t_rstd = em("vector", lambda e, pb=pb: e.reciprocal(
                    out=s.stats[:, pb + 7:pb + 8], in_=s.stats[:, pb + 6:pb + 7]),
                    waits=[("s", t_std)], inc="v")
                t_xn = em("vector", lambda e, xap=xap, pb=pb: e.tensor_scalar(
                    out=xnb2[:], in0=xap,
                    scalar1=s.stats[:, pb + 2:pb + 3], scalar2=s.stats[:, pb + 7:pb + 8],
                    op0=OP.subtract, op1=OP.mult),
                    waits=[("v", t_rstd), ("t", t_xnb2), ("s", t_sq)], inc="v")
                t_stats[kt % 2] = t_xn
                pp = kt % 2
                pT = (s.pT0, s.pT1)[pp]
                t_tr = None
                for ic in range(NCH):
                    t_tr = em("tensor", lambda e, ic=ic, pT=pT: e.transpose(
                        out=pT[:, ic * 128:(ic + 1) * 128],
                        in_=xnb2[:, ic * 128:(ic + 1) * 128], identity=s.idb[:, :]),
                        waits=[("v", t_xn), ("s", t_pT_ev[pp]), ("t", t_h2_mm)],
                        inc="t" if ic == NCH - 1 else None)
                t_xnb2 = t_tr
                t_ev = None
                for ic in range(NCH):
                    t_ev = em("scalar", lambda e, ic=ic, tt=tt, pT=pT: e.activation(
                        out=h2[:, ic * 512 + tt * 128:ic * 512 + (tt + 1) * 128],
                        in_=pT[:, ic * 128:(ic + 1) * 128], func=AF.Identity,
                        bias=s.sbb_col[:, ic:ic + 1], scale=s.cb1_col[:, ic:ic + 1]),
                        waits=[("t", t_tr)], inc="s" if ic == NCH - 1 else None)
                t_pT_ev[pp] = t_ev
            t_h2 = t_ev
            t_mm = None
            for ic in range(NCH):
                t_mm = em("tensor", lambda e, ic=ic: e.matmul(
                    out=s.pE[0:16, 0:512], lhsT=s.afcT[:, ic * 16:(ic + 1) * 16],
                    rhs=h2[:, ic * 512:(ic + 1) * 512],
                    start=(ic == 0), stop=(ic == NCH - 1)),
                    waits=[("s", t_h2), ("s", pE_free[0]), ("v", pE_free_v[0]),
                           ("t", t_lora1_mm)],
                    inc="t" if ic == NCH - 1 else None)
            t_l1 = em("vector", lambda e: e.tensor_copy(out=s.lora1[:],
                                                        in_=s.pE[0:16, 0:512]),
                      waits=[("t", t_mm)], inc="v")
            pE_free_v[0] = t_l1
            for oc in range(32):
                p = oc % 4
                if oc == 0:
                    for pf in range(3):
                        t_wf[pf] = em("sync", lambda e, pf=pf: e.dma_start(
                            out=wfcst[pf][:], in_=s.wfcq_hbm[pf]),
                            waits=[("t", t_wf_mm[pf])], inc="d_wf%d" % pf)
                if oc + 3 < 32:
                    pn = (oc + 3) % 4
                    t_wf[pn] = em("sync", lambda e, oc=oc, pn=pn: e.dma_start(
                        out=wfcst[pn][:], in_=s.wfcq_hbm[oc + 3]),
                        waits=[("t", t_wf_mm[pn])], inc="d_wf%d" % pn)
                t_wd = t_wf[p]
                sl = oc % 4
                t_mm = None
                for ic in range(NCH):
                    t_mm = em("tensor", lambda e, ic=ic, oc=oc, p=p, sl=sl: e.matmul(
                        out=s.pE[:, sl * 512:(sl + 1) * 512],
                        lhsT=wfcst[p][:, ic * 128:(ic + 1) * 128],
                        rhs=h2[:, ic * 512:(ic + 1) * 512],
                        start=(ic == 0), stop=False),
                        waits=[("d_wf%d" % p, t_wd), ("s", pE_free[sl]),
                               ("v", pE_free_v[sl]), ("s", t_h2), ("v", t_l1)],
                        inc=None)
                t_mm = em("tensor", lambda e, oc=oc, sl=sl: e.matmul(
                    out=s.pE[:, sl * 512:(sl + 1) * 512],
                    lhsT=bfcT[0:16, oc * 128:(oc + 1) * 128],
                    rhs=s.lora1[:],
                    start=False, stop=True),
                    waits=[("v", t_l1), ("d_bfc", t_bfcT)], inc="t")
                t_wf_mm[oc % 4] = t_mm
                pE_free[sl] = em("scalar", lambda e, oc=oc, sl=sl: e.activation(
                    out=hid[:, oc * 512:(oc + 1) * 512],
                    in_=s.pE[:, sl * 512:(sl + 1) * 512], func=AF.Gelu,
                    bias=s.bfc_col[:, oc:oc + 1]),
                    waits=[("t", t_mm), ("t", t_hid_mm)], inc="s")
            t_fc = pE_free[(32 - 1) % 4]
            t_lora1_mm = t_wf_mm[(32 - 1) % 2]
            t_h2_mm = t_wf_mm[(32 - 1) % 2]
            t_mm = None
            for hc in range(32):
                t_mm = em("tensor", lambda e, hc=hc: e.matmul(
                    out=s.pE[0:16, 512:1024], lhsT=s.apjT[:, hc * 16:(hc + 1) * 16],
                    rhs=hid[:, hc * 512:(hc + 1) * 512],
                    start=(hc == 0), stop=(hc == 31)),
                    waits=[("s", pE_free[1]), ("v", pE_free_v[1]),
                           ("s", pE_free[(32 - 1) % 4]),
                           ("t", t_lora1p_mm)],
                    inc="t" if hc == 31 else None)
            t_l1p = em("vector", lambda e: e.tensor_copy(out=s.lora1p[:],
                                                         in_=s.pE[0:16, 512:1024]),
                       waits=[("t", t_mm)], inc="v")
            pE_free_v[1] = t_l1p
            for tt in range(4):
                po = 0
                t_mm = None
                for vc in range(2):
                    for hc in range(32):
                        t_mm = em("tensor", lambda e, tt=tt, vc=vc, hc=hc: e.matmul(
                            out=s.pO[:, vc * 512:(vc + 1) * 512],
                            lhsT=hid[:, hc * 512 + tt * 128:hc * 512 + (tt + 1) * 128],
                            rhs=wpj[:, hc * 1024 + vc * 512:hc * 1024 + (vc + 1) * 512],
                            start=(hc == 0), stop=False),
                            waits=[("s", t_fc), ("v", t_pv_out[vc]),
                                   ("d_wpj", t_wpj), ("v", t_l1p)],
                            inc=None)
                    t_mm = em("tensor", lambda e, tt=tt, vc=vc: e.matmul(
                        out=s.pO[:, vc * 512:(vc + 1) * 512],
                        lhsT=s.lora1p[0:16, tt * 128:(tt + 1) * 128],
                        rhs=s.bpjT[0:16, vc * 512:(vc + 1) * 512],
                        start=False, stop=True),
                        waits=[("v", t_l1p)], inc="t")
                    t_lora1p_mm = t_mm
                t_hid_mm = t_mm
                t_e1 = em("vector", lambda e, tt=tt, po=po: e.scalar_tensor_tensor(
                    out=outb[po][:, 0:512], in0=s.pO[:, 0:512], scalar=1.0,
                    in1=bpjgb[:, 0:512], op0=OP.bypass, op1=OP.add),
                    waits=[("t", t_mm), ("v", t_bpjgb),
                           ("d_out%d" % po, t_out[po])], inc="v")
                t_e2 = em("vector", lambda e, tt=tt, po=po: e.scalar_tensor_tensor(
                    out=outb[po][:, 512:1024], in0=s.pO[:, 512:1024], scalar=1.0,
                    in1=bpjgb[:, 512:1024], op0=OP.bypass, op1=OP.add),
                    waits=[("v", t_e1)], inc="v")
                t_pv_out[0] = t_e1
                t_pv_out[1] = t_e2
                t_a1 = em("vector", lambda e, tt=tt, po=po: e.tensor_tensor(
                    out=outb[po][:], in0=outb[po][:],
                    in1=x1blk[:, tt * 1024:(tt + 1) * 1024], op=OP.add),
                    waits=[("v", t_e2)], inc="v")
                t_x1blk[tt] = t_a1
                t_outv[po] = t_a1
                t_out[po] = em("gpsimd", lambda e, tb=tb, tt=tt, po=po: e.dma_start(
                    out=s.out_hbm[tb * 512 + tt * 128:tb * 512 + (tt + 1) * 128, :],
                    in_=outb[po][:]),
                    waits=[("v", t_a1)], inc="d_out%d" % po)
        self.t_ph6 = dict(self.cnt)
        if self.debug:
            em("gpsimd", lambda e: e.dma_start(out=self.dbg["cols"][:, 40:48],
                                               in_=s.pjscale_col[:]),
               waits=[("v", self.t_qfin)], inc="d_dbg")
            em("gpsimd", lambda e: e.dma_start(out=self.dbg["wpjqc"][0:16, :],
                                               in_=s.bpjT[:]),
               waits=[("v", self.t_bpjT)], inc="d_dbg")
            em("gpsimd", lambda e: e.dma_start(out=self.dbg["wfcqc"][0:128, 0:1024],
                                               in_=wpj[:, 4 * 1024:5 * 1024]),
               waits=[("d_wpj", self.cnt["d_wpj"])], inc="d_dbg")
            em("gpsimd", lambda e: e.dma_start(out=self.dbg["wpjqc"][16:32, 0:1024],
                                               in_=bpjgb[0:16, :]),
               waits=[("v", t_bpjgb)], inc="d_dbg")

    def finish(self):
        nc = self.nc
        final_waits = [(k, v) for k, v in self.cnt.items()
                       if k.startswith("d") and v > 0]
        self.em("gpsimd", lambda e: e.engine_nop(), waits=final_waits)
        sem = {k: self.es.enter_context(nc.semaphore("sem_" + k))
               for k in self.cnt}
        self.sem = sem
        blk = self.es.enter_context(nc.Block())
        last_wait = {e: {} for e in ENGINES}

        def run_section(engname, engobj):
            for fn, waits, inc, inc_n in self.ops[engname]:
                for sname, v in waits:
                    if last_wait[engname].get(sname, 0) < v:
                        engobj.wait_ge(sem[sname], v)
                        last_wait[engname][sname] = v
                instr = fn(engobj)
                if inc is not None:
                    instr.then_inc(sem[inc], inc_n)

        @blk.tensor
        def _(e):
            run_section("tensor", e)

        @blk.vector
        def _(e):
            run_section("vector", e)

        @blk.scalar
        def _(e):
            run_section("scalar", e)

        @blk.sync
        def _(e):
            run_section("sync", e)

        @blk.gpsimd
        def _(e):
            run_section("gpsimd", e)

        return nc


def build(debug=0, phases=99):
    mk = MK(debug=debug, phases=phases)
    mk.declare()
    mk.ph0()
    if phases >= 1:
        mk.ph1()
    if phases >= 3:
        mk.ph3()
    if phases >= 4:
        mk.ph4()
    if phases >= 5:
        mk.ph5()
    if phases >= 6:
        mk.ph6()
    return mk.finish()





_NC_CACHE = {}


def _get_nc():
    if "nc" not in _NC_CACHE:
        _NC_CACHE["nc"] = build(debug=0, phases=6)
    return _NC_CACHE["nc"]


def kernel(L0_attn_x, L0_attn_Wk, L0_attn_Wq, L0_attn_Wv, L0_attn_Wa, L0_attn_Wb,
           L0_ffn_W_fc, L0_ffn_b_fc, L0_ffn_A_fc, L0_ffn_B_fc, L0_ffn_W_proj,
           L0_ffn_b_proj, L0_ffn_A_proj, L0_ffn_B_proj, L0_c, L0_wmod):
    x = np.ascontiguousarray(np.asarray(L0_attn_x, dtype=np.float32))
    idb = np.eye(128, dtype=ml_dtypes.bfloat16)
    idf = np.eye(128, dtype=np.float32)
    onesf = np.ones((1, 128), dtype=np.float32)
    common = {
        "c_in": None,
        "wmod_in": np.ascontiguousarray(L0_wmod, dtype=np.float32),
        "wk_in": np.ascontiguousarray(L0_attn_Wk, dtype=np.float32),
        "wq_in": np.ascontiguousarray(L0_attn_Wq, dtype=np.float32),
        "wv_in": np.ascontiguousarray(L0_attn_Wv, dtype=np.float32),
        "wa_in": np.ascontiguousarray(L0_attn_Wa, dtype=np.float32),
        "wb_in": np.ascontiguousarray(L0_attn_Wb, dtype=np.float32),
        "wfc_in": np.ascontiguousarray(L0_ffn_W_fc, dtype=np.float32),
        "bfc_in": np.ascontiguousarray(L0_ffn_b_fc, dtype=np.float32),
        "afc_in": np.ascontiguousarray(L0_ffn_A_fc, dtype=np.float32),
        "bfcB_in": np.ascontiguousarray(L0_ffn_B_fc, dtype=np.float32),
        "wpj_in": np.ascontiguousarray(L0_ffn_W_proj, dtype=np.float32),
        "bpj_in": np.ascontiguousarray(
            np.asarray(L0_ffn_b_proj, dtype=np.float32).reshape(1, D)),
        "apj_in": np.ascontiguousarray(L0_ffn_A_proj, dtype=np.float32),
        "bpjB_in": np.ascontiguousarray(L0_ffn_B_proj, dtype=np.float32),
        "idb_in": idb, "idf_in": idf, "onesf_in": onesf,
    }
    in_maps = []
    for c in range(8):
        b, hf = c // 2, c % 2
        xb = x[b]
        x_c = np.concatenate([xb[hf * OWN:(hf + 1) * OWN],
                              xb[(1 - hf) * OWN:(2 - hf) * OWN]], axis=0)
        m = dict(common)
        m["x_in"] = np.ascontiguousarray(x_c)
        m["c_in"] = np.ascontiguousarray(
            np.asarray(L0_c, dtype=np.float32)[b:b + 1, :])
        in_maps.append(m)
    nc = _get_nc()
    res = run_bass_kernel_spmd(nc, in_maps, core_ids=list(range(8)))
    out = np.empty((4, S, D), dtype=np.float32)
    for c in range(8):
        b, hf = c // 2, c % 2
        out[b, hf * OWN:(hf + 1) * OWN, :] = res.results[c]["out"]
    return out


# version 28
# speedup vs baseline: 1.0901x; 1.0901x over previous
import numpy as np
import ml_dtypes
import concourse.bass as bass
import concourse.mybir as mybir
from concourse.bass_utils import run_bass_kernel_spmd
from concourse.alu_op_type import AluOpType as OP
from contextlib import ExitStack

F32 = mybir.dt.float32
BF16 = mybir.dt.bfloat16
I32 = mybir.dt.int32
AF = mybir.ActivationFunctionType
AX = mybir.AxisListType

S, OWN, D, FF, R = 4096, 2048, 1024, 4096, 16
NCH = 8
NG = 8
NGOWN = 4
NQT = 16
ENGINES = ("tensor", "vector", "scalar", "sync", "gpsimd")


class MK:
    def __init__(self, debug=0, phases=99):
        self.debug = debug
        self.phases = phases
        self.nc = bass.Bass("TRN2", target_bir_lowering=False)
        self.ops = {e: [] for e in ENGINES}
        self.cnt = {k: 0 for k in ("t", "v", "s", "g")}
        self.es = ExitStack()
        self.esR = ExitStack()
        self.dbg = {}

    def em(self, eng, fn, waits=(), inc=None, inc_n=None):
        if inc is not None and inc not in self.cnt:
            self.cnt[inc] = 0
        for snm, _v in waits:
            if snm not in self.cnt:
                self.cnt[snm] = 0
        if inc is not None and inc_n is None:
            inc_n = 16 if inc.startswith("d") else 1
        tok = None
        if inc is not None:
            self.cnt[inc] += inc_n
            tok = self.cnt[inc]
        self.ops[eng].append((fn, tuple((s, v) for (s, v) in waits if v), inc, inc_n))
        return tok

    def sb(self, name, shape, dt=F32, right=False):
        stack = self.esR if right else self.es
        kw = {"side": "right"} if right else {}
        return stack.enter_context(self.nc.sbuf_tensor(name, shape, dt, **kw))

    def ps(self, name, shape, dt=F32):
        return self.es.enter_context(self.nc.psum_tensor(name, shape, dt))

    def dram(self, name, shape, dt, kind):
        return self.nc.dram_tensor(name, shape, dt, kind=kind).ap()

    def declare(self):
        nc, d = self.nc, self.dram
        self.x_in = d("x_in", [S, D], F32, "ExternalInput")
        self.c_in = d("c_in", [1, D], F32, "ExternalInput")
        self.wmod_in = d("wmod_in", [D, 6 * D], F32, "ExternalInput")
        self.wk_in = d("wk_in", [D, D], F32, "ExternalInput")
        self.wq_in = d("wq_in", [D, D], F32, "ExternalInput")
        self.wv_in = d("wv_in", [D, D], F32, "ExternalInput")
        self.wa_in = d("wa_in", [D, D], F32, "ExternalInput")
        self.wb_in = d("wb_in", [D, D], F32, "ExternalInput")
        self.wfc_in = d("wfc_in", [FF, D], F32, "ExternalInput")
        self.bfc_in = d("bfc_in", [FF], F32, "ExternalInput")
        self.afc_in = d("afc_in", [R, D], F32, "ExternalInput")
        self.bfcB_in = d("bfcB_in", [FF, R], F32, "ExternalInput")
        self.wpj_in = d("wpj_in", [D, FF], F32, "ExternalInput")
        self.bpj_in = d("bpj_in", [1, D], F32, "ExternalInput")
        self.apj_in = d("apj_in", [R, FF], F32, "ExternalInput")
        self.bpjB_in = d("bpjB_in", [D, R], F32, "ExternalInput")
        self.idb_in = d("idb_in", [128, 128], BF16, "ExternalInput")
        self.idf_in = d("idf_in", [128, 128], F32, "ExternalInput")
        self.onesf_in = d("onesf_in", [1, 128], F32, "ExternalInput")

        self.out_hbm = d("out", [OWN, D], F32, "ExternalOutput")
        self.x1_hbm = d("x1_sc", [OWN, D], F32, "Internal")
        self.q_hbm = d("q_sc", [16, 128, NCH, 128], BF16, "Internal")
        self.wfcq_hbm = d("wfcq_sc", [32, 128, 8, 128], BF16, "Internal")
        self.wpjq_hbm = d("wpjq_sc", [FF, D], BF16, "Internal")
        self.bfcT_hbm = d("bfcT_sc", [R, FF], BF16, "Internal")
        self.bpjrow_hbm = d("bpjrow_sc", [1, D], F32, "Internal")

        if self.debug:
            for nm, shp in (("mod", [1, 2048]), ("cols", [128, 48]),
                            ("hst", [128, 4096]), ("akc", [128, 1024]),
                            ("uc", [128, 1024]), ("qc", [128, 1024]),
                            ("amax", [128, 8]), ("x1c", [128, 1024]), ("wfcqc", [128, 1024]),
                            ("wpjqc", [128, 1024])):
                self.dbg[nm] = d("dbg_" + nm, shp, F32, "ExternalOutput")


        sb = self.sb
        self.idb = sb("idb", [128, 128], BF16)
        self.idf = sb("idf", [128, 128], F32)
        self.onesf = sb("onesf", [1, 128], F32)
        self.kcol = sb("kcol", [128, 16], F32)
        self.ca1_col = sb("ca1_col", [128, 8], F32)
        self.sa_col = sb("sa_col", [128, 8], F32)
        self.cb1_col = sb("cb1_col", [128, 8], F32)
        self.sbb_col = sb("sbb_col", [128, 8], F32)
        self.gbT_col = sb("gbT_col", [128, 8], F32)
        self.pjscale_col = sb("pjscale_col", [128, 8], F32)
        self.bfc_col = sb("bfc_col", [128, 32], F32)
        self.scT = sb("scT", [128, 16], BF16)
        self.x_st0 = sb("x_st0", [128, 1024], F32)
        self.xnb = sb("xnb", [128, 1024], BF16)
        self.trash = sb("trash", [128, 1024], F32)
        self.amst = [sb("amst%d" % i, [128, 256], F32, right=True) for i in range(2)]
        self.stats = sb("stats", [128, 32], F32)
        self.amx = sb("amx", [128, 8], F32)
        self.afcT = sb("afcT", [128, NCH * R], BF16)
        self.apjT = sb("apjT", [128, 32 * R], BF16)
        self.bpjT = sb("bpjT", [R, 1024], BF16)
        self.lora1 = sb("lora1", [R, 512], BF16)
        self.lora1p = sb("lora1p", [R, 512], BF16)
        self.mcol = sb("mcol", [128, 8], F32)
        self.lcolv = sb("lcolv", [128, 8], F32)

        self.esP1 = ExitStack()
        def sb1(name, shape, dt=F32):
            return self.esP1.enter_context(nc.sbuf_tensor(name, shape, dt))
        self.garow = sb1("garow", [1, 1024])
        self.gbrow = sb1("gbrow", [1, 1024])
        self.crow = sb1("crow", [1, 1024])
        self.scrow = sb1("scrow", [1, 1024])
        self.rowa = sb1("rowa", [1, 1024])
        self.rowb = sb1("rowb", [1, 1024])
        self.rowc = sb1("rowc", [1, 1024])
        self.wm = sb1("wm", [128, 3072])

        self.ak_fm = sb("ak_fm", [128, NCH * S], BF16, right=True)
        self.u_tm = sb("u_tm", [128, 32 * 1024], BF16, right=True)
        self.schi = self.u_tm[0:1, 6144:7168]
        self.sclo = self.u_tm[0:1, 7168:8192]
        self.l16b = self.u_tm[:, 8192:9216]
        self.l16a = self.wm[:, 0:1024]

        self.pE = self.ps("pE", [128, 2048], F32)
        self.pT0 = self.ps("pT0", [128, 1024], BF16)
        self.pT1 = self.ps("pT1", [128, 1024], BF16)
        self.pO = self.ps("pO", [128, 1024], F32)

    def ph0(self):
        em, s = self.em, self
        em("sync", lambda e: e.dma_start(out=s.idb[:], in_=s.idb_in), inc="d_ph0")
        em("sync", lambda e: e.dma_start(out=s.idf[:], in_=s.idf_in), inc="d_ph0")
        em("sync", lambda e: e.dma_start(out=s.onesf[:], in_=s.onesf_in), inc="d_ph0")
        em("sync", lambda e: e.dma_start(out=s.crow[:], in_=s.c_in), inc="d_ph0")
        for cc in range(32):
            em("sync", lambda e, cc=cc: e.dma_start(
                out=s.bfc_col[:, cc:cc + 1], in_=s.bfc_in[cc * 128:(cc + 1) * 128]),
                inc="d_ph0")
        self.t_ld0 = self.cnt["d_ph0"]
        em("gpsimd", lambda e: e.memset(s.kcol[:, 0:1], 1e-5))
        em("gpsimd", lambda e: e.memset(s.kcol[:, 1:2], -0.5))
        em("gpsimd", lambda e: e.memset(s.kcol[:, 2:3], 0.5))
        t_ms = em("gpsimd", lambda e: e.memset(s.kcol[:, 3:4], 127.0), inc="g")
        self.t_rec127 = em("vector",
                           lambda e: e.reciprocal(out=s.kcol[:, 4:5], in_=s.kcol[:, 3:4]),
                           waits=[("g", t_ms)], inc="v")

    def ph1(self):
        em, s = self.em, self
        t_silu = em("scalar", lambda e: e.activation(out=s.scrow[:], in_=s.crow[:],
                                                     func=AF.Silu),
                    waits=[("d_ph0", self.t_ld0)], inc="s")
        t_hi = em("vector", lambda e: e.tensor_copy(out=s.schi, in_=s.scrow[:]),
                  waits=[("s", t_silu)], inc="v")
        t_lo = em("vector", lambda e: e.tensor_tensor(out=s.sclo, in0=s.scrow[:],
                                                      in1=s.schi, op=OP.subtract),
                  waits=[("v", t_hi)], inc="v")
        t_tr = None
        for kc in range(NCH):
            for j, src in ((0, s.schi), (1, s.sclo)):
                t_tr = em("tensor", lambda e, kc=kc, src=src, j=j: e.transpose(
                    out=s.pT0[:, (kc * 2 + j) * 2:(kc * 2 + j) * 2 + 1],
                    in_=src[0:1, kc * 128:(kc + 1) * 128], identity=s.idb[0:1, 0:1]),
                    waits=[("v", t_lo), ("d_ph0", self.t_ld0)],
                    inc="t" if (kc == NCH - 1 and j == 1) else None)
        t_scT = em("vector", lambda e: e.tensor_copy(out=s.scT[:], in_=s.pT0[:, 0:32:2]),
                   waits=[("t", t_tr)], inc="v")
        self.t_scT = t_scT

        whi = [s.u_tm[:, 0:1536], s.u_tm[:, 3072:4608]]
        wlo = [s.u_tm[:, 1536:3072], s.u_tm[:, 4608:6144]]
        wmh = [s.wm[:, 0:1536], s.wm[:, 1536:3072]]
        acc = [s.pE[0:1, 512 * k:512 * (k + 1)] for k in range(4)] + \
              [s.pO[0:1, 0:512], s.pO[0:1, 512:1024]]
        t_split_v = t_split_s = 0
        t_mmgrp = 0
        t_mm_half = [0, 0]
        t_split_prev = [(0, 0), (0, 0)]
        t_proc_v = 0
        t_row = {}
        for rnd in range(2):
            for kc in range(NCH):
                for hp in range(2):
                    p = (kc * 2 + hp) % 2
                    t_wd = em("sync", lambda e, kc=kc, rnd=rnd, hp=hp, p=p: e.dma_start(
                        out=wmh[p], in_=s.wmod_in[kc * 128:(kc + 1) * 128,
                                                  rnd * 3072 + hp * 1536:
                                                  rnd * 3072 + (hp + 1) * 1536]),
                        waits=[("v", t_split_prev[p][0]), ("s", t_split_prev[p][1])],
                        inc="d_wm%d" % p)
                    t_split_s = em("scalar", lambda e, p=p: e.activation(
                        out=whi[p], in_=wmh[p], func=AF.Copy),
                        waits=[("d_wm%d" % p, t_wd), ("t", t_mm_half[p])], inc="s")
                    t_split_v = em("vector", lambda e, p=p: e.tensor_tensor(
                        out=wlo[p], in0=wmh[p], in1=whi[p], op=OP.subtract),
                        waits=[("s", t_split_s)], inc="v")
                    t_split_prev[p] = (t_split_v, t_split_s)
                    for s3 in range(3):
                        sl = hp * 3 + s3
                        for ti, (par, rhs) in enumerate(((0, whi[p]), (1, whi[p]),
                                                         (0, wlo[p]))):
                            st = (kc == 0 and ti == 0)
                            sp = (kc == NCH - 1 and ti == 2)
                            tok = em("tensor", lambda e, kc=kc, par=par, rhs=rhs, sl=sl,
                                     s3=s3, st=st, sp=sp: e.matmul(
                                         out=acc[sl],
                                         lhsT=s.scT[:, kc * 2 + par:kc * 2 + par + 1],
                                         rhs=rhs[:, s3 * 512:(s3 + 1) * 512],
                                         start=st, stop=sp),
                                     waits=[("v", t_split_v), ("v", t_scT)],
                                     inc="t" if (s3 == 2 and ti == 2) else None)
                            if tok:
                                t_mmgrp = tok
                                t_mm_half[p] = tok
            names = (("sa", "ca", "ga"), ("sb", "cb", "gb"))[rnd]
            rows = {"sa": s.rowa, "ca": s.rowc, "ga": s.garow,
                    "sb": s.rowa, "cb": s.rowc, "gb": s.gbrow}
            t_ev = {}
            for rr, name in enumerate(names):
                for h2 in range(2):
                    t_ev[name] = em("vector", lambda e, rr=rr, h2=h2, name=name:
                                    e.tensor_copy(
                                        out=rows[name][0:1, h2 * 512:(h2 + 1) * 512],
                                        in_=acc[rr * 2 + h2]),
                                    waits=[("t", t_mmgrp), ("v", t_proc_v)], inc="v")
            for name in names:
                if name in ("ga", "gb"):
                    g = rows[name]
                    t_sg = em("scalar", lambda e, g=g: e.activation(
                        out=s.rowb[:], in_=g[:], func=AF.Sign),
                        waits=[("v", t_ev[name])], inc="s")
                    t_ab = em("scalar", lambda e, g=g: e.activation(
                        out=s.scrow[:], in_=g[:], func=AF.Abs),
                        waits=[("s", t_sg)], inc="s")
                    t_mx = em("vector", lambda e: e.tensor_scalar(
                        out=s.scrow[:], in0=s.scrow[:], scalar1=0.25, scalar2=None,
                        op0=OP.max),
                        waits=[("s", t_ab)], inc="v")
                    t_g = em("vector", lambda e, g=g: e.tensor_tensor(
                        out=g[:], in0=s.rowb[:], in1=s.scrow[:], op=OP.mult),
                        waits=[("v", t_mx)], inc="v")
                    t_row[name] = t_g
                    t_proc_v = t_g
                else:
                    dcol = {"sa": s.sa_col, "ca": s.ca1_col,
                            "sb": s.sbb_col, "cb": s.cb1_col}[name]
                    plus1 = name in ("ca", "cb")
                    g = rows[name]
                    t_ct = None
                    for kc in range(NCH):
                        t_ct = em("tensor", lambda e, kc=kc, g=g: e.transpose(
                            out=s.pO[:, 512 + kc:513 + kc],
                            in_=g[0:1, kc * 128:(kc + 1) * 128],
                            identity=s.idf[0:1, 0:1]),
                            waits=[("v", t_ev[name]), ("v", t_proc_v),
                                   ("v", t_ev.get(names[2], t_ev[name]))],
                            inc="t" if kc == NCH - 1 else None)
                    t_proc_v = em("vector", lambda e, dcol=dcol, plus1=plus1:
                                  e.tensor_scalar(out=dcol[:, 0:8],
                                                  in0=s.pO[:, 512:520],
                                                  scalar1=1.0 if plus1 else 0.0,
                                                  scalar2=None, op0=OP.add),
                                  waits=[("t", t_ct)], inc="v")
                    t_row[name + "_col"] = t_proc_v
            t_mmgrp = self.cnt["t"]
        self.t_row = t_row

        t_ct = None
        for kc in range(NCH):
            t_ct = em("tensor", lambda e, kc=kc: e.transpose(
                out=s.pO[:, 512 + kc:513 + kc],
                in_=s.gbrow[0:1, kc * 128:(kc + 1) * 128], identity=s.idf[0:1, 0:1]),
                waits=[("v", t_row["gb"]), ("v", t_proc_v)],
                inc="t" if kc == NCH - 1 else None)
        t_gbT = em("vector", lambda e: e.tensor_copy(out=s.gbT_col[:, 0:8],
                                                     in_=s.pO[:, 512:520]),
                   waits=[("t", t_ct)], inc="v")
        self.t_gbT = t_gbT

        def bcast_row(row_ap, out_sb, wait_list):
            t_mm = None
            for hh in range(2):
                t_mm = em("tensor", lambda e, hh=hh, row_ap=row_ap: e.matmul(
                    out=s.pO[:, hh * 512:(hh + 1) * 512], lhsT=s.onesf[:],
                    rhs=row_ap[0:1, hh * 512:(hh + 1) * 512], start=True, stop=True),
                    waits=wait_list, inc="t" if hh else None)
            return em("vector", lambda e, out_sb=out_sb: e.tensor_copy(
                out=out_sb[:], in_=s.pO[:]), waits=[("t", t_mm)], inc="v")

        t_gbbc = bcast_row(s.gbrow, s.trash, [("v", t_row["gb"]), ("v", t_gbT)])

        t_d = em("sync", lambda e: e.dma_start(out=s.l16a[0:16, 0:1024], in_=s.afc_in),
                 waits=[("s", self.cnt["s"]), ("v", self.cnt["v"])], inc="d_l16")
        t_cast = em("scalar", lambda e: e.activation(out=s.l16b[0:16, 0:1024],
                                                     in_=s.l16a[0:16, 0:1024],
                                                     func=AF.Copy, scale=2.0),
                    waits=[("d_l16", t_d)], inc="s")
        t_tr2 = None
        for ic in range(NCH):
            t_tr2 = em("tensor", lambda e, ic=ic: e.transpose(
                out=s.pT0[:, ic * 16:(ic + 1) * 16],
                in_=s.l16b[0:16, ic * 128:(ic + 1) * 128], identity=s.idb[0:16, 0:16]),
                waits=[("s", t_cast), ("v", t_scT)], inc="t" if ic == NCH - 1 else None)
        t_afcT = em("vector", lambda e: e.tensor_copy(out=s.afcT[:],
                                                      in_=s.pT0[:, 0:NCH * 16]),
                    waits=[("t", t_tr2)], inc="v")
        t_apjT = 0
        for pc in range(4):
            t_d = em("sync", lambda e, pc=pc: e.dma_start(
                out=s.l16a[0:16, 0:1024], in_=s.apj_in[0:16, pc * 1024:(pc + 1) * 1024]),
                waits=[("s", t_cast), ("t", t_tr2)], inc="d_l16")
            t_cast = em("scalar", lambda e: e.activation(
                out=s.l16b[0:16, 0:1024], in_=s.l16a[0:16, 0:1024], func=AF.Copy, scale=2.0),
                waits=[("d_l16", t_d), ("v", t_afcT if pc == 0 else t_apjT)], inc="s")
            for j in range(NCH):
                t_tr2 = em("tensor", lambda e, j=j: e.transpose(
                    out=s.pT0[:, j * 16:(j + 1) * 16],
                    in_=s.l16b[0:16, j * 128:(j + 1) * 128],
                    identity=s.idb[0:16, 0:16]),
                    waits=[("s", t_cast)], inc="t" if j == NCH - 1 else None)
            t_apjT = em("vector", lambda e, pc=pc: e.tensor_copy(
                out=s.apjT[:, pc * 128:(pc + 1) * 128], in_=s.pT0[:, 0:128]),
                waits=[("t", t_tr2)], inc="v")
        t_d = em("sync", lambda e: e.dma_start(
            out=s.l16a[:, 0:128].rearrange("p (j c) -> p j c", c=16),
            in_=s.bpjB_in.rearrange("(j p) c -> p j c", p=128)),
            waits=[("s", t_cast), ("t", t_tr2), ("v", t_apjT)], inc="d_l16")
        t_cast2 = em("scalar", lambda e: e.activation(
            out=s.l16b[:, 0:128], in_=s.l16a[:, 0:128], func=AF.Copy),
            waits=[("d_l16", t_d)], inc="s")
        t_bt = None
        for dc in range(NCH):
            t_bt = em("tensor", lambda e, dc=dc: e.transpose(
                out=s.pT1[0:16, dc * 128:(dc + 1) * 128],
                in_=s.l16b[:, dc * 16:(dc + 1) * 16],
                identity=s.idb[:, :]),
                waits=[("s", t_cast2)], inc="t" if dc == NCH - 1 else None)
        t_bp0 = em("vector", lambda e: e.tensor_copy(out=s.bpjT[:],
                                                     in_=s.pT1[0:16, 0:1024]),
                   waits=[("t", t_bt)], inc="v")
        t_bpjT = em("vector", lambda e: e.tensor_tensor(
            out=s.bpjT[:], in0=s.bpjT[:], in1=s.trash[0:16, :], op=OP.mult),
            waits=[("v", t_bp0), ("v", t_gbbc)], inc="v")
        self.t_bpjT = t_bpjT
        t_d = em("sync", lambda e: e.dma_start(out=s.rowb[:], in_=s.bpj_in),
                 waits=[("v", self.cnt["v"])], inc="d_bpj")
        t_bb = em("vector", lambda e: e.tensor_tensor(out=s.rowb[:], in0=s.rowb[:],
                                                      in1=s.gbrow[:], op=OP.mult),
                  waits=[("d_bpj", t_d), ("v", t_bpjT)], inc="v")
        t_bpjgb = em("gpsimd", lambda e: e.dma_start(out=s.bpjrow_hbm, in_=s.rowb[:]),
                     waits=[("v", t_bb)], inc="d_bpr")
        self.t_bpjgb = t_bpjgb
        t_gabc = bcast_row(s.garow, s.trash, [("v", t_row["ga"]), ("v", t_bpjT),
                                              ("v", t_bb)])
        self.t_gabc = t_gabc
        self.ga_bc = s.trash
        l16c = s.u_tm[:, 9216:10240]
        t_pc = 0
        t_cast3 = [0, 0]
        t_bt3 = [t_bt, t_bt]
        t_ev2 = 0
        for b in range(4):
            rb = b % 2
            t_d = em("sync", lambda e, b=b, rb=rb: e.dma_start(
                out=s.l16a[:, 128 + rb * 128:128 + (rb + 1) * 128]
                .rearrange("p (j c) -> p j c", c=16),
                in_=s.bfcB_in[b * 1024:(b + 1) * 1024, :]
                .rearrange("(j p) c -> p j c", p=128)),
                waits=[("s", t_cast3[rb] or t_cast2)], inc="d_lb%d" % rb)
            t_cast3[rb] = em("scalar", lambda e, rb=rb: e.activation(
                out=s.l16b[:, 128 + rb * 128:128 + (rb + 1) * 128],
                in_=s.l16a[:, 128 + rb * 128:128 + (rb + 1) * 128], func=AF.Copy),
                waits=[("d_lb%d" % rb, t_d), ("t", t_bt3[rb])], inc="s")
            t_tr3 = None
            for j in range(8):
                t_tr3 = em("tensor", lambda e, j=j, rb=rb: e.transpose(
                    out=s.pT1[0:16, j * 128:(j + 1) * 128],
                    in_=s.l16b[:, 128 + rb * 128 + j * 16:128 + rb * 128 + (j + 1) * 16],
                    identity=s.idb[:, :]),
                    waits=[("s", t_cast3[rb]), ("v", t_ev2)],
                    inc="t" if j == 7 else None)
            t_bt3[rb] = t_tr3
            t_ev2 = em("vector", lambda e: e.tensor_copy(out=l16c[0:16, 0:1024],
                                                         in_=s.pT1[0:16, 0:1024]),
                       waits=[("t", t_tr3), ("d_lr1", t_pc)], inc="v")
            t_pc = em("gpsimd", lambda e, b=b: e.dma_start(
                out=s.bfcT_hbm[0:16, b * 1024:(b + 1) * 1024],
                in_=l16c[0:16, 0:1024]),
                waits=[("v", t_ev2)], inc="d_lr1")
        self.t_ph1_end = dict(self.cnt)

        if self.debug:
            em("gpsimd", lambda e: e.dma_start(out=self.dbg["mod"][0:1, 0:1024],
                                               in_=s.garow[:]),
               waits=[("v", t_row["ga"])], inc="d_dbg")
            em("gpsimd", lambda e: e.dma_start(out=self.dbg["mod"][0:1, 1024:2048],
                                               in_=s.gbrow[:]),
               waits=[("v", t_row["gb"])], inc="d_dbg")
            em("gpsimd", lambda e: e.dma_start(out=self.dbg["cols"][:, 0:8],
                                               in_=s.ca1_col[:]),
               waits=[("v", t_row["ca_col"])], inc="d_dbg")
            em("gpsimd", lambda e: e.dma_start(out=self.dbg["cols"][:, 8:16],
                                               in_=s.sa_col[:]),
               waits=[("v", t_row["sa_col"])], inc="d_dbg")
            em("gpsimd", lambda e: e.dma_start(out=self.dbg["cols"][:, 16:24],
                                               in_=s.gbT_col[:]),
               waits=[("v", t_gbT)], inc="d_dbg")
            em("gpsimd", lambda e: e.dma_start(out=self.dbg["cols"][:, 24:32],
                                               in_=s.cb1_col[:]),
               waits=[("v", t_row["cb_col"])], inc="d_dbg")
            em("gpsimd", lambda e: e.dma_start(out=self.dbg["cols"][:, 32:40],
                                               in_=s.sbb_col[:]),
               waits=[("v", t_row["sb_col"])], inc="d_dbg")

    def ph3(self):
        em, s = self.em, self
        self.esP1.close()
        self._t_am_v = self.cnt["v"]
        self._t_am_v2 = [self.cnt["v"], self.cnt["v"]]
        self._t_am_b = [self.cnt["v"], self.cnt["v"]]
        self._am_idx = 0
        self.esW = ExitStack()
        self.wqT = self.esW.enter_context(self.nc.sbuf_tensor("wqT", [128, 8192], BF16))
        self.wkaT = self.esW.enter_context(self.nc.sbuf_tensor("wkaT", [128, 8192], BF16))
        self.wvbT = self.esW.enter_context(self.nc.sbuf_tensor("wvbT", [128, 8192], BF16))
        self.h_stage = self.esW.enter_context(self.nc.sbuf_tensor("h_stage", [128, 4096], BF16))
        self.x_st1 = self.esW.enter_context(self.nc.sbuf_tensor("x_st1", [128, 1024], F32))

        self.qev = [self.u_tm[:, 31744:32256], self.u_tm[:, 32256:32768]]
        self.x_stage = [self.x_st0, self.x_st1]
        fold1 = s.ak_fm[:, 0:8192]
        fold2 = s.ak_fm[:, 8192:16384]
        wtmp = s.u_tm[:, 9216:10240]

        t_xs = [0, 0]
        t_xs_eng = ["v", "v"]
        t_wtmp = 0
        t_pT = [self.cnt["t"], self.cnt["t"]]
        t_pT_ev = [self.cnt["s"], self.cnt["s"]]
        t_pT_ev = [0, 0]
        ph1_t = self.cnt["t"]
        ph1_v = self.cnt["v"]
        ph1_s = self.cnt["s"]
        bar = [(k, self.cnt.get(k, 0)) for k in ("d_bpr", "d_lr1", "d_l16", "d_wm",
                                                 "d_bpj", "d_ph0", "d_dbg")]

        def load_cast(src_ap, kc, dst_ap, p, extra_waits=()):
            t_d = em("sync", lambda e, src_ap=src_ap, p=p: e.dma_start(
                out=s.x_stage[p][:], in_=src_ap),
                waits=[(t_xs_eng[p], t_xs[p])] + list(extra_waits) + bar,
                inc="d_xs%d" % p)
            t_c = em("vector", lambda e, p=p, dst_ap=dst_ap: e.tensor_copy(
                out=dst_ap, in_=s.x_stage[p][:]),
                waits=[("d_xs%d" % p, t_d)] + bar, inc="v")
            t_xs[p] = t_c
            t_xs_eng[p] = "v"
            self.amax_pump(2)
            return t_c

        t_wk = []
        for kc in range(NCH):
            t_wk.append(load_cast(s.wk_in[kc * 128:(kc + 1) * 128, :], kc,
                                  fold1[:, kc * 1024:(kc + 1) * 1024], kc % 2))
        t_wa_ev = []
        for ot in range(NCH):
            t_c = load_cast(s.wa_in[ot * 128:(ot + 1) * 128, :], ot, wtmp, ot % 2,
                            extra_waits=[("t", t_wtmp)])
            pp = ot % 2
            pT = (s.pT0, s.pT1)[pp]
            t_tr = None
            for kc in range(NCH):
                t_tr = em("tensor", lambda e, kc=kc, pT=pT: e.transpose(
                    out=pT[:, kc * 128:(kc + 1) * 128],
                    in_=wtmp[:, kc * 128:(kc + 1) * 128], identity=s.idb[:, :]),
                    waits=[("v", t_c), ("s", t_pT_ev[pp]), ("t", ph1_t),
                           ("v", ph1_v)] + bar,
                    inc="t" if kc == NCH - 1 else None)
            t_wtmp = t_tr
            t_pT[pp] = t_tr
            t_ev = None
            for kc in range(NCH):
                t_ev = em("scalar", lambda e, kc=kc, ot=ot, pT=pT: e.activation(
                    out=fold2[:, kc * 1024 + ot * 128:kc * 1024 + (ot + 1) * 128],
                    in_=pT[:, kc * 128:(kc + 1) * 128], func=AF.Copy),
                    waits=[("t", t_tr)] + bar, inc="s" if kc == NCH - 1 else None)
            t_pT_ev[pp] = t_ev
            t_wa_ev.append(t_ev)
        pE_free = [ph1_v] * 4
        t_fold_mm = []
        last_wka_mm = None
        for ic in range(NCH):
            for oh in range(2):
                sl = (ic * 2 + oh) % 4
                t_mm = None
                for kc in range(NCH):
                    t_mm = em("tensor", lambda e, ic=ic, oh=oh, kc=kc, sl=sl: e.matmul(
                        out=s.pE[:, sl * 512:(sl + 1) * 512],
                        lhsT=fold1[:, kc * 1024 + ic * 128:kc * 1024 + (ic + 1) * 128],
                        rhs=fold2[:, kc * 1024 + oh * 512:kc * 1024 + (oh + 1) * 512],
                        start=(kc == 0), stop=(kc == NCH - 1)),
                        waits=[("v", t_wk[-1]), ("s", t_wa_ev[-1]),
                               ("v", pE_free[sl])],
                        inc="t" if kc == NCH - 1 else None)
                last_wka_mm = t_mm
                pE_free[sl] = em("vector", lambda e, ic=ic, oh=oh, sl=sl: e.tensor_copy(
                    out=s.wkaT[:, ic * 1024 + oh * 512:ic * 1024 + (oh + 1) * 512],
                    in_=s.pE[:, sl * 512:(sl + 1) * 512]),
                    waits=[("t", t_mm)], inc="v")
                self.amax_pump(2)
        t_wka_done = pE_free[:]
        t_wv = []
        for kc in range(NCH):
            t_wv.append(load_cast(s.wv_in[kc * 128:(kc + 1) * 128, :], kc,
                                  fold1[:, kc * 1024:(kc + 1) * 1024], kc % 2,
                                  extra_waits=[("t", last_wka_mm)]))
        t_wb = []
        for ot in range(NCH):
            t_c = load_cast(s.wb_in[ot * 128:(ot + 1) * 128, :], ot, wtmp, ot % 2,
                            extra_waits=[("t", t_wtmp), ("t", last_wka_mm)])
            pp = ot % 2
            pT = (s.pT0, s.pT1)[pp]
            t_tr = None
            for kc in range(NCH):
                t_tr = em("tensor", lambda e, kc=kc, pT=pT: e.transpose(
                    out=pT[:, kc * 128:(kc + 1) * 128],
                    in_=wtmp[:, kc * 128:(kc + 1) * 128], identity=s.idb[:, :]),
                    waits=[("v", t_c), ("s", t_pT_ev[pp]), ("t", last_wka_mm)],
                    inc="t" if kc == NCH - 1 else None)
            t_wtmp = t_tr
            t_pT[pp] = t_tr
            t_ev = None
            for kc in range(NCH):
                t_ev = em("scalar", lambda e, kc=kc, ot=ot, pT=pT: e.activation(
                    out=fold2[:, kc * 1024 + ot * 128:kc * 1024 + (ot + 1) * 128],
                    in_=pT[:, kc * 128:(kc + 1) * 128], func=AF.Copy),
                    waits=[("t", t_tr)], inc="s" if kc == NCH - 1 else None)
            t_pT_ev[pp] = t_ev
            t_wb.append(t_ev)
        last_wvb_mm = None
        for ic in range(NCH):
            for oh in range(2):
                sl = (ic * 2 + oh) % 4
                t_mm = None
                for kc in range(NCH):
                    t_mm = em("tensor", lambda e, ic=ic, oh=oh, kc=kc, sl=sl: e.matmul(
                        out=s.pE[:, sl * 512:(sl + 1) * 512],
                        lhsT=fold1[:, kc * 1024 + ic * 128:kc * 1024 + (ic + 1) * 128],
                        rhs=fold2[:, kc * 1024 + oh * 512:kc * 1024 + (oh + 1) * 512],
                        start=(kc == 0), stop=(kc == NCH - 1)),
                        waits=[("v", t_wv[-1]), ("s", t_wb[-1]), ("v", pE_free[sl])],
                        inc="t" if kc == NCH - 1 else None)
                last_wvb_mm = t_mm
                pE_free[sl] = em("vector", lambda e, ic=ic, oh=oh, sl=sl: e.tensor_tensor(
                    out=s.wvbT[:, ic * 1024 + oh * 512:ic * 1024 + (oh + 1) * 512],
                    in0=s.pE[:, sl * 512:(sl + 1) * 512],
                    in1=s.trash[:, oh * 512:(oh + 1) * 512], op=OP.mult),
                    waits=[("t", t_mm), ("v", self.t_gabc)], inc="v")
                self.amax_pump(2)
        for ot in range(NCH):
            t_c = load_cast(s.wq_in[ot * 128:(ot + 1) * 128, :], ot, wtmp, ot % 2,
                            extra_waits=[("t", t_wtmp)])
            pp = ot % 2
            pT = (s.pT0, s.pT1)[pp]
            t_tr = None
            for kc in range(NCH):
                t_tr = em("tensor", lambda e, kc=kc, pT=pT: e.transpose(
                    out=pT[:, kc * 128:(kc + 1) * 128],
                    in_=wtmp[:, kc * 128:(kc + 1) * 128], identity=s.idb[:, :]),
                    waits=[("v", t_c), ("s", t_pT_ev[pp]), ("v", last_wvb_mm and 0 or 0),
                           ("t", last_wvb_mm)],
                    inc="t" if kc == NCH - 1 else None)
            t_wtmp = t_tr
            t_ev = None
            for kc in range(NCH):
                t_ev = em("scalar", lambda e, kc=kc, ot=ot, pT=pT: e.activation(
                    out=s.wqT[:, kc * 1024 + ot * 128:kc * 1024 + (ot + 1) * 128],
                    in_=pT[:, kc * 128:(kc + 1) * 128], func=AF.Copy),
                    waits=[("t", t_tr)], inc="s" if kc == NCH - 1 else None)
            t_pT_ev[pp] = t_ev
        self.t_ph3 = {"v": self.cnt["v"], "s": self.cnt["s"], "t": self.cnt["t"]}
        self.pE_free = pE_free
        self.t_xs = t_xs
        self.t_xs_eng = t_xs_eng
        self.t_pT = t_pT
        self.t_pT_ev = t_pT_ev

        if self.debug:
            em("gpsimd", lambda e: e.dma_start(out=self.dbg["qc"], in_=s.wqT[:, 0:1024]),
               waits=[("s", self.t_ph3["s"])], inc="d_dbg")


    def amax_pump(self, n=1):
        em, s = self.em, self
        for _ in range(n):
            idx = self._am_idx
            if idx >= 256:
                return
            self._am_idx += 1
            p = idx % 2
            k = idx // 2
            if p == 0:
                ot, hf = k // 4, k % 4
                src_ap = s.wfc_in[ot * 128:(ot + 1) * 128, hf * 256:(hf + 1) * 256]
                c0, c1 = 0, 1
            else:
                dc, rest = k // 16, k % 16
                src_ap = s.wpj_in[dc * 128:(dc + 1) * 128, rest * 256:(rest + 1) * 256]
                c0, c1 = 2, 3
            first = idx < 2
            t_d = em("gpsimd", lambda e, src_ap=src_ap, p=p: e.dma_start(
                out=s.amst[p][:], in_=src_ap),
                waits=[("v", self._t_am_b[p])], inc="d_am%d" % p)
            t_r = em("vector", lambda e, c0=c0, c1=c1, first=first, p=p: e.tensor_reduce(
                out=s.amx[:, (c0 if first else c1):(c0 if first else c1) + 1],
                in_=s.amst[p][:], axis=AX.X, op=OP.max, apply_absolute_value=True),
                waits=[("d_am%d" % p, t_d), ("v", self._t_am_v2[p])], inc="v")
            if not first:
                t_r = em("vector", lambda e, c0=c0, c1=c1: e.tensor_tensor(
                    out=s.amx[:, c0:c0 + 1], in0=s.amx[:, c0:c0 + 1],
                    in1=s.amx[:, c1:c1 + 1], op=OP.max),
                    waits=[("v", t_r)], inc="v")
            self._t_am_b[p] = t_r
            self._t_am_v2[p] = t_r
            self._t_am_v = t_r

    def ph4(self):
        em, s = self.em, self
        t_xs, t_xs_eng = self.t_xs, self.t_xs_eng
        t_pT, t_pT_ev = self.t_pT, self.t_pT_ev
        pE_free = self.pE_free
        pO_free = [self.t_gabc, self.t_bpjgb]
        pO_free = [self.cnt["v"], self.cnt["v"]]
        t_trash_free = self.t_ph3["v"]
        t_hst_mm = 0
        t_stats = [0, 0]
        t_xnb = 0
        t_qev = [0, 0]
        self.q_done = []
        t_akev = t_uev = t_qv = 0
        for g in range(NG):
            t_evac_grp = []
            gpar = 16 * (g % 2)
            for pair in range(2):
              t_sumv = 0
              t_sq = 0
              for tt2 in range(2):
                tt = pair * 2 + tt2
                kt = g * 4 + tt
                p = kt % 2
                cb = gpar + tt * 4
                t_d = em("sync", lambda e, kt=kt, p=p: e.dma_start(
                    out=s.x_stage[p][:], in_=s.x_in[kt * 128:(kt + 1) * 128, :]),
                    waits=[(t_xs_eng[p], t_xs[p])], inc="d_xs%d" % p)
                t_sumv = em("vector", lambda e, p=p, cb=cb: e.reduce_sum(
                    out=s.stats[:, cb:cb + 1], in_=s.x_stage[p][:], axis=AX.X),
                    waits=[("d_xs%d" % p, t_d), ("v", t_stats[g % 2])], inc="v")
                t_sq = em("scalar", lambda e, p=p, cb=cb: e.activation(
                    out=s.trash[:], in_=s.x_stage[p][:], func=AF.Square,
                    accum_out=s.stats[:, cb + 1:cb + 2]),
                    waits=[("d_xs%d" % p, t_d), ("v", t_trash_free),
                           ("s", getattr(self, "_t_sq_prev", 0)),
                           ("v", t_stats[g % 2])], inc="s")
                self._t_sq_prev = t_sq
              pb0 = gpar + pair * 8
              t_mv = em("vector", lambda e, pb0=pb0: e.tensor_scalar(
                  out=s.stats[:, pb0 + 2:pb0 + 8:4],
                  in0=s.stats[:, pb0 + 0:pb0 + 8:4],
                  scalar1=1.0 / 1024, scalar2=None, op0=OP.mult),
                  waits=[("v", t_sumv), ("s", t_sq)], inc="v")
              t_v1 = em("vector", lambda e, pb0=pb0: e.tensor_scalar(
                  out=s.stats[:, pb0 + 3:pb0 + 8:4],
                  in0=s.stats[:, pb0 + 1:pb0 + 8:4],
                  scalar1=1.0 / 1024, scalar2=None, op0=OP.mult),
                  waits=[("v", t_mv)], inc="v")
              t_ms = em("vector", lambda e, pb0=pb0: e.tensor_tensor(
                  out=s.stats[:, pb0 + 1:pb0 + 8:4],
                  in0=s.stats[:, pb0 + 2:pb0 + 8:4],
                  in1=s.stats[:, pb0 + 2:pb0 + 8:4], op=OP.mult),
                  waits=[("v", t_v1)], inc="v")
              t_var = em("vector", lambda e, pb0=pb0: e.tensor_tensor(
                  out=s.stats[:, pb0 + 0:pb0 + 8:4],
                  in0=s.stats[:, pb0 + 3:pb0 + 8:4],
                  in1=s.stats[:, pb0 + 1:pb0 + 8:4], op=OP.subtract),
                  waits=[("v", t_ms)], inc="v")
              t_std = em("scalar", lambda e, pb0=pb0: e.activation(
                  out=s.stats[:, pb0 + 1:pb0 + 8:4],
                  in_=s.stats[:, pb0 + 0:pb0 + 8:4],
                  func=AF.Sqrt, bias=s.kcol[:, 0:1]),
                  waits=[("v", t_var), ("s", self._t_sq_prev)], inc="s")
              self._t_sq_prev = t_std
              t_rstd = em("vector", lambda e, pb0=pb0: e.reciprocal(
                  out=s.stats[:, pb0 + 3:pb0 + 8:4],
                  in_=s.stats[:, pb0 + 1:pb0 + 8:4]),
                  waits=[("s", t_std)], inc="v")
              for tt2 in range(2):
                tt = pair * 2 + tt2
                kt = g * 4 + tt
                p = kt % 2
                cb = gpar + tt * 4
                t_xn = em("vector", lambda e, p=p, cb=cb: e.tensor_scalar(
                    out=s.xnb[:], in0=s.x_stage[p][:],
                    scalar1=s.stats[:, cb + 2:cb + 3],
                    scalar2=s.stats[:, cb + 3:cb + 4],
                    op0=OP.subtract, op1=OP.mult),
                    waits=[("v", t_rstd), ("t", t_xnb)], inc="v")
                t_xs[p] = t_xn
                t_xs_eng[p] = "v"
                pp = kt % 2
                pT = (s.pT0, s.pT1)[pp]
                t_tr = None
                for ic in range(NCH):
                    t_tr = em("tensor", lambda e, ic=ic, pT=pT: e.transpose(
                        out=pT[:, ic * 128:(ic + 1) * 128],
                        in_=s.xnb[:, ic * 128:(ic + 1) * 128], identity=s.idb[:, :]),
                        waits=[("v", t_xn), ("s", t_pT_ev[pp]), ("t", t_hst_mm)],
                        inc="t" if ic == NCH - 1 else None)
                t_xnb = t_tr
                t_pT[pp] = t_tr
                t_ev = None
                for ic in range(NCH):
                    t_ev = em("scalar", lambda e, ic=ic, tt=tt, pT=pT: e.activation(
                        out=s.h_stage[:, ic * 512 + tt * 128:ic * 512 + (tt + 1) * 128],
                        in_=pT[:, ic * 128:(ic + 1) * 128], func=AF.Identity,
                        bias=s.sa_col[:, ic:ic + 1], scale=s.ca1_col[:, ic:ic + 1]),
                        waits=[("t", t_tr)], inc="s" if ic == NCH - 1 else None)
                t_pT_ev[pp] = t_ev
                t_evac_grp.append(t_ev)
            t_stats[g % 2] = self.cnt["v"]
            t_hs = t_evac_grp[-1]
            for oc in range(NCH):
                sl = oc % 4
                t_mm = None
                for ic in range(NCH):
                    t_mm = em("tensor", lambda e, oc=oc, ic=ic, g=g, sl=sl: e.matmul(
                        out=s.pE[:, sl * 512:(sl + 1) * 512],
                        lhsT=s.wkaT[:, ic * 1024 + oc * 128:ic * 1024 + (oc + 1) * 128],
                        rhs=s.h_stage[:, ic * 512:(ic + 1) * 512],
                        start=(ic == 0), stop=(ic == NCH - 1)),
                        waits=[("s", t_hs), ("v", pE_free[sl])],
                        inc="t" if ic == NCH - 1 else None)
                t_hst_mm = t_mm
                pE_free[sl] = em("vector", lambda e, oc=oc, g=g, sl=sl: e.tensor_copy(
                    out=s.ak_fm[:, oc * 4096 + g * 512:oc * 4096 + (g + 1) * 512],
                    in_=s.pE[:, sl * 512:(sl + 1) * 512]),
                    waits=[("t", t_mm)], inc="v")
                t_akev = pE_free[sl]
                self.amax_pump(1)
            for tt in range(4):
                for vc in range(2):
                    t_mm = None
                    for ic in range(NCH):
                        t_mm = em("tensor", lambda e, tt=tt, vc=vc, ic=ic, g=g: e.matmul(
                            out=s.pO[:, vc * 512:(vc + 1) * 512],
                            lhsT=s.h_stage[:, ic * 512 + tt * 128:ic * 512 + (tt + 1) * 128],
                            rhs=s.wvbT[:, ic * 1024 + vc * 512:ic * 1024 + (vc + 1) * 512],
                            start=(ic == 0), stop=(ic == NCH - 1)),
                            waits=[("s", t_hs), ("v", pO_free[vc])],
                            inc="t" if ic == NCH - 1 else None)
                    t_hst_mm = t_mm
                    uev_w = [("t", t_mm)]
                    if g * 4 + tt == 31:
                        uev_w += [("d_qe0", self.cnt.get("d_qe0", 0)),
                                  ("d_qe1", self.cnt.get("d_qe1", 0))]
                    pO_free[vc] = em("vector", lambda e, tt=tt, vc=vc, g=g: e.tensor_copy(
                        out=s.u_tm[:, (g * 4 + tt) * 1024 + vc * 512:
                                   (g * 4 + tt) * 1024 + (vc + 1) * 512],
                        in_=s.pO[:, vc * 512:(vc + 1) * 512]),
                        waits=uev_w, inc="v")
                    t_uev = pO_free[vc]
                    self.amax_pump(1)
            if g < NGOWN:
                for oc in range(NCH):
                    sl = oc % 4
                    t_mm = None
                    for ic in range(NCH):
                        t_mm = em("tensor", lambda e, oc=oc, ic=ic, g=g, sl=sl: e.matmul(
                            out=s.pE[:, sl * 512:(sl + 1) * 512],
                            lhsT=s.wqT[:, ic * 1024 + oc * 128:ic * 1024 + (oc + 1) * 128],
                            rhs=s.h_stage[:, ic * 512:(ic + 1) * 512],
                            start=(ic == 0), stop=(ic == NCH - 1)),
                            waits=[("s", t_hs), ("v", pE_free[sl])],
                            inc="t" if ic == NCH - 1 else None)
                    t_hst_mm = t_mm
                    qp = oc % 2
                    t_qv = em("vector", lambda e, oc=oc, g=g, sl=sl, qp=qp: e.tensor_copy(
                        out=s.qev[qp][:], in_=s.pE[:, sl * 512:(sl + 1) * 512]),
                        waits=[("t", t_mm), ("d_qe%d" % qp, t_qev[qp])], inc="v")
                    pE_free[sl] = t_qv
                    for iq in range(4):
                        t_qev[qp] = em("gpsimd", lambda e, oc=oc, g=g, qp=qp, iq=iq:
                                       e.dma_start(
                                           out=s.q_hbm[g * 4 + iq, :, oc, :],
                                           in_=s.qev[qp][:, iq * 128:(iq + 1) * 128]),
                                       waits=[("v", t_qv)], inc="d_qe%d" % qp)
        self.pE_free = pE_free
        self.pO_free = pO_free
        self.t_hst_mm = t_hst_mm
        self.t_pT = t_pT
        self.t_pT_ev = t_pT_ev
        self.t_ph4 = {"v": self.cnt["v"], "s": self.cnt["s"], "t": self.cnt["t"],
                      "ak": t_akev, "u": t_uev}
        if self.debug:
            em("gpsimd", lambda e: e.dma_start(out=self.dbg["hst"], in_=s.h_stage[:]),
               waits=[("s", self.cnt["s"]), ("t", t_hst_mm)], inc="d_dbg")
            em("gpsimd", lambda e: e.dma_start(out=self.dbg["akc"],
                                               in_=s.ak_fm[:, 0:1024]),
               waits=[("v", t_akev)], inc="d_dbg")
            em("gpsimd", lambda e: e.dma_start(out=self.dbg["uc"],
                                               in_=s.u_tm[:, 0:1024]),
               waits=[("v", t_uev)], inc="d_dbg")

    def ph5_alloc(self):
        em, s = self.em, self
        self.esW.close()
        self.esA = ExitStack()
        def sba(name, shape, dt):
            return self.esA.enter_context(self.nc.sbuf_tensor(name, shape, dt))
        self.alphaA = sba("alphaA", [128, 2048], BF16)
        self.alphaB = sba("alphaB", [128, 2048], BF16)
        self.alphaT = sba("alphaT", [128, 4096], BF16)
        self.q_stage = [sba("q_stg%d" % i, [128, 1024], BF16) for i in range(2)]
        self.x1b = sba("x1b", [128, 1024], F32)
        self.wstb = [sba("wst%d" % i, [128, 1024], F32) for i in range(2)]
        self.qt = [sba("qt%d" % i, [128, 1024], F32) for i in range(2)]
        self.qki = sba("qki", [128, 512], I32)
        self.qkf = [sba("qkf%d" % i, [128, 1024], F32) for i in range(2)]
        self.qbf = [sba("qbf%d" % i, [128, 1024], BF16) for i in range(5)]

    def quant_finalize(self):
        em, s = self.em, self
        while self._am_idx < 256:
            self.amax_pump(8)
        t_g = self._t_am_v
        t_prev_t = self.cnt["t"]
        t_prev_ev = 0
        for mi, (acol, dstc) in enumerate(((0, 8), (2, 9))):
            t_tr = em("tensor", lambda e, acol=acol: e.transpose(
                out=s.pO[0:1, 0:128], in_=s.amx[:, acol:acol + 1],
                identity=s.idf[:, :]),
                waits=[("v", t_g), ("v", t_prev_ev), ("t", t_prev_t),
                       ("v", self.t_ph4["v"]),
                       ("s", self.t_ph4["s"]),
                       ("d_dbg", self.cnt.get("d_dbg", 0))], inc="t")
            t_r = em("vector", lambda e: e.tensor_reduce(
                out=s.kcol[0:1, 10:11], in_=s.pO[0:1, 0:128], axis=AX.X, op=OP.max),
                waits=[("t", t_tr)], inc="v")
            t_bc = em("tensor", lambda e: e.matmul(
                out=s.pO[:, 0:1], lhsT=s.onesf[:], rhs=s.kcol[0:1, 10:11],
                start=True, stop=True),
                waits=[("v", t_r)], inc="t")
            t_v = em("vector", lambda e, dstc=dstc: e.tensor_copy(
                out=s.kcol[:, dstc:dstc + 1], in_=s.pO[:, 0:1]),
                waits=[("t", t_bc)], inc="v")
            t_prev_t = t_bc
            t_prev_ev = t_v
        t_r1 = em("vector", lambda e: e.reciprocal(out=s.kcol[:, 10:11],
                                                   in_=s.kcol[:, 8:9]),
                  waits=[("v", t_v)], inc="v")
        t_m127f = em("vector", lambda e: e.tensor_scalar(
            out=s.kcol[:, 5:6], in0=s.kcol[:, 10:11], scalar1=127.0, scalar2=None,
            op0=OP.mult), waits=[("v", t_r1)], inc="v")
        t_r2 = em("vector", lambda e: e.reciprocal(out=s.kcol[:, 10:11],
                                                   in_=s.kcol[:, 9:10]),
                  waits=[("v", t_m127f)], inc="v")
        t_m127p = em("vector", lambda e: e.tensor_scalar(
            out=s.kcol[:, 7:8], in0=s.kcol[:, 10:11], scalar1=127.0, scalar2=None,
            op0=OP.mult), waits=[("v", t_r2)], inc="v")
        t_qsf = em("vector", lambda e: e.tensor_tensor(
            out=s.kcol[:, 6:7], in0=s.kcol[:, 8:9], in1=s.kcol[:, 4:5], op=OP.mult),
            waits=[("v", t_m127p)], inc="v")
        t_pjq = em("vector", lambda e: e.tensor_tensor(
            out=s.kcol[:, 10:11], in0=s.kcol[:, 9:10], in1=s.kcol[:, 4:5], op=OP.mult),
            waits=[("v", t_qsf)], inc="v")
        t_pjs = em("vector", lambda e: e.tensor_scalar(
            out=s.pjscale_col[:, 0:8], in0=s.gbT_col[:, 0:8],
            scalar1=s.kcol[:, 10:11], scalar2=None, op0=OP.mult),
            waits=[("v", t_pjq), ("v", self.t_gbT)], inc="v")
        self.t_qfin = t_pjs
        if self.debug:
            em("gpsimd", lambda e: e.dma_start(out=self.dbg["amax"][:, 0:4],
                                               in_=s.amx[:, 0:4]),
               waits=[("v", t_pjs)], inc="d_dbg")

    def _qsrc(self, idx):
        s = self
        if idx < 32:
            return (s.wfc_in[idx * 128:(idx + 1) * 128, :],
                    s.kcol[:, 5:6], s.kcol[:, 6:7])
        k = idx - 32
        dc, pc = k // 4, k % 4
        return (s.wpj_in[dc * 128:(dc + 1) * 128, pc * 1024:(pc + 1) * 1024],
                s.kcol[:, 7:8], s.pjscale_col[:, dc:dc + 1])

    def quant_stageA(self, idx):
        em, s = self.em, self
        st = self._qstate
        p = idx % 2
        qt, qkf = s.qt[p], s.qkf[p]
        _, m127, _ = self._qsrc(idx)
        if idx == 0:
            src_ap, _, _ = self._qsrc(0)
            st["dws"][0] = em("sync", lambda e, src_ap=src_ap: e.dma_start(
                out=s.wstb[0][:], in_=src_ap),
                waits=[("v", self.t_qfin)], inc="d_ws0")
        if idx + 1 < 64:
            src_ap, _, _ = self._qsrc(idx + 1)
            pn = (idx + 1) % 2
            st["dws"][pn] = em("sync", lambda e, src_ap=src_ap, pn=pn: e.dma_start(
                out=s.wstb[pn][:], in_=src_ap),
                waits=[("v", st["e1_%d" % pn]), ("v", self.t_qfin)],
                inc="d_ws%d" % pn)
        t_t = em("vector", lambda e, p=p, m127=m127, qt=qt: e.tensor_scalar(
            out=qt[:], in0=s.wstb[p][:], scalar1=m127, scalar2=None, op0=OP.mult),
            waits=[("d_ws%d" % p, st["dws"][p]), ("s", st["s12%d" % p]),
                   ("v", st["qq%d" % p]), ("g", st["d%d" % p])], inc="v")
        st["wst"][p] = t_t
        t_kf = None
        for hh in range(2):
            t_ki = em("vector", lambda e, qt=qt, hh=hh: e.tensor_copy(
                out=s.qki[:], in_=qt[:, hh * 512:(hh + 1) * 512]),
                waits=[("v", t_t), ("s", st["kf"] if hh == 0 else t_kf)], inc="v")
            t_kf = em("scalar", lambda e, qkf=qkf, hh=hh: e.activation(
                out=qkf[:, hh * 512:(hh + 1) * 512], in_=s.qki[:], func=AF.Copy),
                waits=[("v", t_ki), ("v", st["qq%d" % p]), ("s", st["q%d" % p]),
                       ("g", st["d%d" % p])],
                inc="s")
        st["kf"] = t_kf
        t_dd = em("gpsimd", lambda e, qt=qt, qkf=qkf: e.tensor_tensor(
            out=qt[:], in0=qt[:], in1=qkf[:], op=OP.subtract),
            waits=[("s", t_kf)], inc="g")
        st["d%d" % p] = t_dd

    def quant_stageB1(self, idx):
        em, s = self.em, self
        st = self._qstate
        p = idx % 2
        qt, wsb = s.qt[p], s.wstb[p]
        t_s1 = em("scalar", lambda e, qt=qt, wsb=wsb: e.activation(
            out=wsb[:], in_=qt[:], func=AF.Sign, bias=s.kcol[:, 1:2]),
            waits=[("g", st["d%d" % p]), ("v", st["wst"][p])], inc="s")
        t_s2 = em("scalar", lambda e, qt=qt: e.activation(
            out=qt[:], in_=qt[:], func=AF.Sign, bias=s.kcol[:, 2:3]),
            waits=[("s", t_s1)], inc="s")
        st["s12%d" % p] = t_s2
        st["ts1_%d" % p] = t_s1

    def quant_stageB2(self, idx):
        em, s = self.em, self
        st = self._qstate
        p = idx % 2
        qt, qkf, wsb = s.qt[p], s.qkf[p], s.wstb[p]
        t_e1 = em("vector", lambda e, qkf=qkf, wsb=wsb: e.scalar_tensor_tensor(
            out=qkf[:], in0=wsb[:], scalar=0.5, in1=qkf[:],
            op0=OP.mult, op1=OP.add),
            waits=[("s", st["ts1_%d" % p]), ("s", st["q%d" % p])], inc="v")
        st["e1_%d" % p] = t_e1
        t_q = em("vector", lambda e, qkf=qkf, qt=qt: e.scalar_tensor_tensor(
            out=qkf[:], in0=qt[:], scalar=0.5, in1=qkf[:],
            op0=OP.mult, op1=OP.add),
            waits=[("s", st["s12%d" % p]), ("v", t_e1)], inc="v")
        st["qq%d" % p] = t_q

    def quant_stageB3(self, idx):
        em, s = self.em, self
        st = self._qstate
        p = idx % 2
        pp = idx % 5
        qkf, qb = s.qkf[p], s.qbf[pp]
        _, _, scale = self._qsrc(idx)
        t_b = em("scalar", lambda e, scale=scale, qb=qb, qkf=qkf: e.activation(
            out=qb[:], in_=qkf[:], func=AF.Identity, scale=scale),
            waits=[("v", st["qq%d" % p]),
                   ("d_pc0", self.cnt.get("d_pc0", 0))], inc="s")
        st["q%d" % p] = t_b
        st["qb"][pp] = t_b

    def quant_slot(self, k):
        if 0 <= k - 1 < 64:
            self.quant_stageB1(k - 1)
            self.quant_stageB2(k - 1)
        if 0 <= k - 2 < 64:
            self.quant_stageB3(k - 2)
        if k < 64:
            self.quant_stageA(k)
        if 0 <= k - 6 < 64:
            self.quant_tile_store(k - 6)

    def quant_tile_store(self, idx):
        em, s = self.em, self
        st = self._qstate
        pp = idx % 5
        pT = s.pT0
        t_b = st["qb"][pp]
        qq = 0
        t_tr = None
        for j in range(8):
            t_tr = em("tensor", lambda e, j=j, pT=pT, pp=pp: e.transpose(
                out=pT[:, j * 128:(j + 1) * 128],
                in_=s.qbf[pp][:, j * 128:(j + 1) * 128], identity=s.idb[:, :]),
                waits=[("s", t_b), self.pTst["free"][qq]],
                inc="t" if j == 7 else None)
        self.pTst["tr"][qq] = t_tr
        t_ev = em("vector", lambda e, pT=pT, pp=pp: e.tensor_copy(
            out=s.qbf[pp][:], in_=pT[:]),
            waits=[("t", t_tr), ("d_pc0", self.cnt.get("d_pc0", 0))], inc="v")
        self.pTst["free"][qq] = ("v", t_ev)
        if idx < 32:
            dst = s.wfcq_hbm[idx]
        else:
            k = idx - 32
            dc, pc = k // 4, k % 4
            dst = s.wpjq_hbm[pc * 1024:(pc + 1) * 1024, dc * 128:(dc + 1) * 128] \
                .rearrange("(j p) o -> p j o", p=128)
        em("gpsimd", lambda e, dst=dst, pp=pp: e.dma_start(
            out=dst, in_=s.qbf[pp][:]),
            waits=[("v", t_ev)], inc="d_pc0")

    def ph5(self):
        em, s = self.em, self
        self.ph5_alloc()
        self.quant_finalize()
        self._qstate = {"wst": [self.t_qfin, self.t_qfin], "kf": 0,
                        "s120": 0, "s121": 0, "d0": 0, "d1": 0,
                        "q0": 0, "q1": 0, "qq0": 0, "qq1": 0,
                        "ts1_0": 0, "ts1_1": 0, "e1_0": 0, "e1_1": 0,
                        "qb": [0, 0, 0, 0, 0], "dws": [0, 0]}
        qslot_next = [0]

        ph4 = self.t_ph4
        self.pTst = {"free": [("v", self.cnt["v"]), ("v", self.cnt["v"])],
                     "tr": [self.cnt["t"], self.cnt["t"]]}
        st = {
            "expB": 0,
            "expA": 0,
            "transA_done": 0,
            "transB_done": 0,
            "alphaT_ev": [],

            "pO_stt": self.cnt["v"],
            "lcol_stt": 0,
            "lcol_stt2": [0, 0],
            "rescale": 0,
            "linv": 0,
            "x0": 0,
            "qs": [0, 0],
            "qs_mm": [ph4["t"], ph4["t"]],
            "x1": 0,
        }
        qbar = [("d_qe0", self.cnt.get("d_qe0", 0)), ("d_qe1", self.cnt.get("d_qe1", 0)),
                ("d_dbg", self.cnt.get("d_dbg", 0))]

        def emit_trans(half, ic_chunk, alpha, t_alpha_ready, eng_ready):
            evs = []
            t_tr = None
            for grp in range(2):
                pp = 1
                pT = s.pT1
                for j in range(8):
                    jb = grp * 8 + j
                    t_tr = em("tensor", lambda e, jb=jb, j=j, pT=pT, alpha=alpha:
                              e.transpose(
                                  out=pT[:, j * 128:(j + 1) * 128],
                                  in_=alpha[:, jb * 128:(jb + 1) * 128],
                                  identity=s.idb[:, :]),
                              waits=[(eng_ready, t_alpha_ready),
                                     self.pTst["free"][pp]],
                              inc="t" if j == 7 else None)
                self.pTst["tr"][pp] = t_tr
                ev = em("vector", lambda e, half=half, grp=grp, pT=pT: e.tensor_copy(
                    out=s.alphaT[:, half * 2048 + grp * 1024:
                                 half * 2048 + (grp + 1) * 1024],
                    in_=pT[:]),
                    waits=[("t", t_tr)], inc="v")
                self.pTst["free"][pp] = ("v", ev)
                evs.append(ev)
            return t_tr, evs

        def pump1():
            if qslot_next[0] < 70:
                self.quant_slot(qslot_next[0])
                qslot_next[0] += 1

        for ic in range(NQT):
            p = ic % 2
            t_q = em("sync", lambda e, ic=ic, p=p: e.dma_start(
                out=s.q_stage[p][:], in_=s.q_hbm[ic]),
                waits=[("t", st["qs_mm"][p])] + qbar, inc="d_q%d" % p)
            st["qs"][p] = t_q
            t_eA = None
            for oc in range(NCH):
                for jb in range(4):
                    t_eA = em("tensor", lambda e, oc=oc, jb=jb, p=p: e.matmul(
                        out=s.pE[:, jb * 512:(jb + 1) * 512],
                        lhsT=s.q_stage[p][:, oc * 128:(oc + 1) * 128],
                        rhs=s.ak_fm[:, oc * 4096 + jb * 512:oc * 4096 + (jb + 1) * 512],
                        start=(oc == 0), stop=(oc == NCH - 1)),
                        waits=[("d_q%d" % p, t_q), ("s", st["expB"]),
                               ("v", ph4["ak"])],
                        inc="t" if (oc == NCH - 1 and jb == 3) else None)
            pump1()
            t_mA = em("vector", lambda e: e.tensor_reduce(
                out=s.mcol[:, 0:1], in_=s.pE[:], axis=AX.X, op=OP.max, negate=True),
                waits=[("t", t_eA), ("s", st["expA"]), ("v", st["rescale"])], inc="v")
            pump1()
            lb = (ic % 2) * 4
            t_expA = em("scalar", lambda e, lb=lb: e.activation(
                out=s.alphaA[:], in_=s.pE[:], func=AF.Exp, bias=s.mcol[:, 0:1],
                accum_out=s.lcolv[:, lb:lb + 1]),
                waits=[("v", t_mA), ("t", st["transA_done"]), ("v", st["linv"]),
                       ("v", st["lcol_stt2"][ic % 2]),
                       ("d_dbg", self.cnt.get("d_dbg", 0) if ic == 0 else 0)], inc="s")
            st["expA"] = t_expA
            if ic > 0:
                self._emit_pv(ic - 1, st, emit_trans, part=0)
            t_eB = None
            for oc in range(NCH):
                for jb in range(4):
                    t_eB = em("tensor", lambda e, oc=oc, jb=jb, p=p: e.matmul(
                        out=s.pE[:, jb * 512:(jb + 1) * 512],
                        lhsT=s.q_stage[p][:, oc * 128:(oc + 1) * 128],
                        rhs=s.ak_fm[:, oc * 4096 + 2048 + jb * 512:
                                    oc * 4096 + 2048 + (jb + 1) * 512],
                        start=(oc == 0), stop=(oc == NCH - 1)),
                        waits=[("s", t_expA)],
                        inc="t" if (oc == NCH - 1 and jb == 3) else None)
            st["qs_mm"][p] = t_eB
            t_mB = em("vector", lambda e: e.tensor_reduce(
                out=s.mcol[:, 2:3], in_=s.pE[:], axis=AX.X, op=OP.max, negate=True),
                waits=[("t", t_eB), ("s", st["expA"])], inc="v")
            t_nm = em("vector", lambda e: e.tensor_tensor(
                out=s.mcol[:, 1:2], in0=s.mcol[:, 0:1], in1=s.mcol[:, 2:3], op=OP.min),
                waits=[("v", t_mB)], inc="v")
            t_dc = em("vector", lambda e: e.tensor_tensor(
                out=s.mcol[:, 3:4], in0=s.mcol[:, 1:2], in1=s.mcol[:, 0:1],
                op=OP.subtract),
                waits=[("v", t_nm)], inc="v")
            t_expB = em("scalar", lambda e, lb=lb: e.activation(
                out=s.alphaB[:], in_=s.pE[:], func=AF.Exp, bias=s.mcol[:, 1:2],
                accum_out=s.lcolv[:, lb + 1:lb + 2]),
                waits=[("v", t_nm), ("t", st["transB_done"]),
                       ("v", st["lcol_stt2"][ic % 2])], inc="s")
            st["expB"] = t_expB
            t_sA = em("scalar", lambda e: e.activation(
                out=s.mcol[:, 4:5], in_=s.mcol[:, 3:4], func=AF.Exp),
                waits=[("v", t_dc)], inc="s")
            t_rs = em("vector", lambda e: e.tensor_scalar(
                out=s.alphaA[:], in0=s.alphaA[:], scalar1=s.mcol[:, 4:5],
                scalar2=None, op0=OP.mult),
                waits=[("s", t_sA), ("s", t_expA)], inc="v")
            st["rescale"] = t_rs
            t_l2 = em("vector", lambda e, lb=lb: e.tensor_tensor(
                out=s.lcolv[:, lb + 2:lb + 3], in0=s.lcolv[:, lb:lb + 1],
                in1=s.mcol[:, 4:5], op=OP.mult),
                waits=[("s", t_sA), ("s", t_expA), ("v", st["lcol_stt"])], inc="v")
            t_l = em("vector", lambda e, lb=lb: e.tensor_tensor(
                out=s.lcolv[:, lb + 2:lb + 3], in0=s.lcolv[:, lb + 2:lb + 3],
                in1=s.lcolv[:, lb + 1:lb + 2], op=OP.add),
                waits=[("v", t_l2), ("s", t_expB)], inc="v")
            t_linv = em("vector", lambda e, lb=lb: e.reciprocal(
                out=s.lcolv[:, lb + 3:lb + 4], in_=s.lcolv[:, lb + 2:lb + 3]),
                waits=[("v", t_l)], inc="v")
            st["linv"] = t_linv
            pump1()
            if ic > 0:
                self._emit_pv(ic - 1, st, emit_trans, part=1)
            pump1()
            tA, evA = emit_trans(0, ic, s.alphaA, t_rs, "v")
            st["transA_done"] = tA
            st["alphaT_ev"] = evA
            st["_ic_linv"] = t_linv
        self._emit_pv(NQT - 1, st, emit_trans, part=0)
        self._emit_pv(NQT - 1, st, emit_trans, part=1)
        while qslot_next[0] < 70:
            self.quant_slot(qslot_next[0])
            qslot_next[0] += 1
        self.t_ph5 = dict(self.cnt)
        self.st5 = st

    def _emit_pv(self, ic, st, emit_trans, part=0):
        em, s = self.em, self
        if part == 0:
            tB, evB = emit_trans(1, ic, s.alphaB, st["expB"], "s")
            st["transB_done"] = tB
            st["evB"] = evB
            st["xd"] = em("sync", lambda e, ic=ic: e.dma_start(
                out=s.x_st0[:], in_=s.x_in[ic * 128:(ic + 1) * 128, :]),
                waits=[("v", st["x0"])], inc="d_xr")
            for jc in range(16):
                for vc in range(2):
                    em("tensor", lambda e, jc=jc, vc=vc: e.matmul(
                        out=s.pO[:, vc * 512:(vc + 1) * 512],
                        lhsT=s.alphaT[:, jc * 128:(jc + 1) * 128],
                        rhs=s.u_tm[:, jc * 1024 + vc * 512:jc * 1024 + (vc + 1) * 512],
                        start=(jc == 0), stop=False),
                        waits=[("v", st["alphaT_ev"][-1]), ("v", evB[-1]),
                               ("v", st["pO_stt"]), ("v", self.t_ph4["u"])])
            return
        evB = st["evB"]
        t_xd = st["xd"]
        t_pv = None
        for jc in range(16, 32):
            for vc in range(2):
                t_pv = em("tensor", lambda e, jc=jc, vc=vc: e.matmul(
                    out=s.pO[:, vc * 512:(vc + 1) * 512],
                    lhsT=s.alphaT[:, jc * 128:(jc + 1) * 128],
                    rhs=s.u_tm[:, jc * 1024 + vc * 512:jc * 1024 + (vc + 1) * 512],
                    start=False, stop=(jc == 31)),
                    waits=[("v", st["alphaT_ev"][-1]), ("v", evB[-1]),
                           ("v", st["pO_stt"]), ("v", self.t_ph4["u"])],
                    inc="t" if (jc == 31 and vc == 1) else None)
        lb = (ic % 2) * 4
        t_s1 = em("vector", lambda e, lb=lb: e.scalar_tensor_tensor(
            out=s.x1b[:, 0:512], in0=s.pO[:, 0:512], scalar=s.lcolv[:, lb + 3:lb + 4],
            in1=s.x_st0[:, 0:512], op0=OP.mult, op1=OP.add),
            waits=[("t", t_pv), ("d_xr", t_xd), ("d_x1", st["x1"]),
                   ("d_dbg", st.get("x1dbg", 0)),
                   ("v", st.get("_ic_linv", 0))], inc="v")
        t_s2 = em("vector", lambda e, lb=lb: e.scalar_tensor_tensor(
            out=s.x1b[:, 512:1024], in0=s.pO[:, 512:1024],
            scalar=s.lcolv[:, lb + 3:lb + 4],
            in1=s.x_st0[:, 512:1024], op0=OP.mult, op1=OP.add),
            waits=[("v", t_s1)], inc="v")
        st["pO_stt"] = t_s2
        st["lcol_stt"] = t_s2
        st["lcol_stt2"][ic % 2] = t_s2
        st["x0"] = t_s2
        st["x1"] = em("gpsimd", lambda e, ic=ic: e.dma_start(
            out=s.x1_hbm[ic * 128:(ic + 1) * 128, :], in_=s.x1b[:]),
            waits=[("v", t_s2)], inc="d_x1")
        if self.debug and ic == 0:
            st["x1dbg"] = em("gpsimd", lambda e: e.dma_start(out=self.dbg["x1c"],
                                                             in_=s.x1b[:]),
                             waits=[("v", t_s2)], inc="d_dbg")

    def ph6(self):
        em, s = self.em, self
        self.esA.close()
        self.esR.close()
        self.esF = ExitStack()
        def sbf(name, shape, dt):
            return self.esF.enter_context(self.nc.sbuf_tensor(name, shape, dt))
        wpj = sbf("wpj_sb", [128, 32 * 1024], BF16)
        hid = sbf("hid", [128, 32 * 512], BF16)
        h2 = sbf("h2", [128, 4096], BF16)
        x1blk = sbf("x1blk", [128, 4096], F32)
        bpjgb = sbf("bpjgb", [128, 1024], F32)
        bfcT = sbf("bfcT_sb", [16, 4096], BF16)
        wfcst = [sbf("wfcst%d" % i, [128, 1024], BF16) for i in range(4)]
        xnb2 = sbf("xnb2", [128, 1024], BF16)
        outb = [sbf("outb0", [128, 1024], F32)]
        rowp = x1blk

        c5 = self.t_ph5
        bar6 = [(k, c5.get(k, 0)) for k in ("d_x1", "d_pc0", "d_pc1", "d_q0", "d_q1",
                                            "d_xr", "d_ws0", "d_ws1", "d_dbg",
                                            "d_am", "d_qe0", "d_qe1")]
        bar6e = [(k, c5[k]) for k in ("v", "s", "t", "g")]

        t_wpj = None
        for hc in range(32):
            t_wpj = em("sync", lambda e, hc=hc: e.dma_start(
                out=wpj[:, hc * 1024:(hc + 1) * 1024],
                in_=s.wpjq_hbm[hc * 128:(hc + 1) * 128, :]),
                waits=bar6, inc="d_wpj")
        t_bfcT = em("sync", lambda e: e.dma_start(out=bfcT[:], in_=s.bfcT_hbm),
                    waits=bar6 + [("d_lr1", self.cnt["d_lr1"])], inc="d_bfc")
        t_rp = em("sync", lambda e: e.dma_start(out=rowp[0:1, 0:1024],
                                                in_=s.bpjrow_hbm),
                  waits=bar6 + [("d_bpr", self.cnt["d_bpr"])], inc="d_rp")
        t_bcmm = None
        for hh in range(2):
            t_bcmm = em("tensor", lambda e, hh=hh: e.matmul(
                out=s.pO[:, hh * 512:(hh + 1) * 512], lhsT=s.onesf[:],
                rhs=rowp[0:1, hh * 512:(hh + 1) * 512], start=True, stop=True),
                waits=[("d_rp", t_rp), ("v", c5["v"]), ("s", c5["s"])] ,
                inc="t" if hh else None)
        t_bpjgb = em("vector", lambda e: e.tensor_copy(out=bpjgb[:], in_=s.pO[:]),
                     waits=[("t", t_bcmm)] + bar6, inc="v")

        t_stats = [c5["v"], c5["v"]]
        t_xnb2 = c5["t"]
        t_sq_prev = c5["s"]
        t_pT_ev = [c5["s"], c5["s"]]
        t_h2_mm = c5["t"]
        t_hid_mm = c5["t"]
        t_x1blk = [c5["v"]] * 4
        t_lora1_mm = c5["t"]
        t_lora1p_mm = c5["t"]
        t_wf = [0, 0, 0, 0]
        t_wf_mm = [c5["t"], c5["t"], c5["t"], c5["t"]]
        t_out = [0, 0]
        t_outv = [c5["v"], c5["v"]]
        pE_free = [c5["s"]] * 4
        pE_free_v = [c5["v"]] * 4
        t_pv_out = [t_bpjgb, t_bpjgb]
        for tb in range(4):
            t_xd = []
            for tt in range(4):
                w_extra = [("t", t_bcmm)] if (tb == 0 and tt == 0) else []
                t_xd.append(em("sync", lambda e, tb=tb, tt=tt: e.dma_start(
                    out=x1blk[:, tt * 1024:(tt + 1) * 1024],
                    in_=s.x1_hbm[tb * 512 + tt * 128:tb * 512 + (tt + 1) * 128, :]),
                    waits=[("v", t_x1blk[tt]), ("d_x1", c5["d_x1"])] + w_extra,
                    inc="d_xb%d" % tt))
            for tt in range(4):
                kt = tb * 4 + tt
                pb = 16 * (kt % 2)
                xap = x1blk[:, tt * 1024:(tt + 1) * 1024]
                t_sum = em("vector", lambda e, xap=xap, pb=pb: e.reduce_sum(
                    out=s.stats[:, pb:pb + 1], in_=xap, axis=AX.X),
                    waits=[("d_xb%d" % tt, t_xd[tt]), ("v", t_stats[kt % 2])] + bar6,
                    inc="v")
                t_sq = em("scalar", lambda e, xap=xap, pb=pb: e.activation(
                    out=xnb2[:], in_=xap, func=AF.Square,
                    accum_out=s.stats[:, pb + 1:pb + 2]),
                    waits=[("d_xb%d" % tt, t_xd[tt]), ("s", t_sq_prev), ("t", t_xnb2),
                           ("v", t_stats[kt % 2])], inc="s")
                t_sq_prev = t_sq
                t_m = em("vector", lambda e, pb=pb: e.tensor_scalar(
                    out=s.stats[:, pb + 2:pb + 3], in0=s.stats[:, pb:pb + 1],
                    scalar1=1.0 / 1024, scalar2=None, op0=OP.mult),
                    waits=[("v", t_sum)], inc="v")
                t_v1 = em("vector", lambda e, pb=pb: e.tensor_scalar(
                    out=s.stats[:, pb + 3:pb + 4], in0=s.stats[:, pb + 1:pb + 2],
                    scalar1=1.0 / 1024, scalar2=None, op0=OP.mult),
                    waits=[("s", t_sq), ("v", t_m)], inc="v")
                t_ms = em("vector", lambda e, pb=pb: e.tensor_tensor(
                    out=s.stats[:, pb + 4:pb + 5], in0=s.stats[:, pb + 2:pb + 3],
                    in1=s.stats[:, pb + 2:pb + 3], op=OP.mult),
                    waits=[("v", t_v1)], inc="v")
                t_var = em("vector", lambda e, pb=pb: e.tensor_tensor(
                    out=s.stats[:, pb + 5:pb + 6], in0=s.stats[:, pb + 3:pb + 4],
                    in1=s.stats[:, pb + 4:pb + 5], op=OP.subtract),
                    waits=[("v", t_ms)], inc="v")
                t_std = em("scalar", lambda e, pb=pb: e.activation(
                    out=s.stats[:, pb + 6:pb + 7], in_=s.stats[:, pb + 5:pb + 6],
                    func=AF.Sqrt, bias=s.kcol[:, 0:1]),
                    waits=[("v", t_var)], inc="s")
                t_rstd = em("vector", lambda e, pb=pb: e.reciprocal(
                    out=s.stats[:, pb + 7:pb + 8], in_=s.stats[:, pb + 6:pb + 7]),
                    waits=[("s", t_std)], inc="v")
                t_xn = em("vector", lambda e, xap=xap, pb=pb: e.tensor_scalar(
                    out=xnb2[:], in0=xap,
                    scalar1=s.stats[:, pb + 2:pb + 3], scalar2=s.stats[:, pb + 7:pb + 8],
                    op0=OP.subtract, op1=OP.mult),
                    waits=[("v", t_rstd), ("t", t_xnb2), ("s", t_sq)], inc="v")
                t_stats[kt % 2] = t_xn
                pp = kt % 2
                pT = (s.pT0, s.pT1)[pp]
                t_tr = None
                for ic in range(NCH):
                    t_tr = em("tensor", lambda e, ic=ic, pT=pT: e.transpose(
                        out=pT[:, ic * 128:(ic + 1) * 128],
                        in_=xnb2[:, ic * 128:(ic + 1) * 128], identity=s.idb[:, :]),
                        waits=[("v", t_xn), ("s", t_pT_ev[pp]), ("t", t_h2_mm)],
                        inc="t" if ic == NCH - 1 else None)
                t_xnb2 = t_tr
                t_ev = None
                for ic in range(NCH):
                    t_ev = em("scalar", lambda e, ic=ic, tt=tt, pT=pT: e.activation(
                        out=h2[:, ic * 512 + tt * 128:ic * 512 + (tt + 1) * 128],
                        in_=pT[:, ic * 128:(ic + 1) * 128], func=AF.Identity,
                        bias=s.sbb_col[:, ic:ic + 1], scale=s.cb1_col[:, ic:ic + 1]),
                        waits=[("t", t_tr)], inc="s" if ic == NCH - 1 else None)
                t_pT_ev[pp] = t_ev
            t_h2 = t_ev
            t_mm = None
            for ic in range(NCH):
                t_mm = em("tensor", lambda e, ic=ic: e.matmul(
                    out=s.pE[0:16, 0:512], lhsT=s.afcT[:, ic * 16:(ic + 1) * 16],
                    rhs=h2[:, ic * 512:(ic + 1) * 512],
                    start=(ic == 0), stop=(ic == NCH - 1)),
                    waits=[("s", t_h2), ("s", pE_free[0]), ("v", pE_free_v[0]),
                           ("t", t_lora1_mm)],
                    inc="t" if ic == NCH - 1 else None)
            t_l1 = em("vector", lambda e: e.tensor_copy(out=s.lora1[:],
                                                        in_=s.pE[0:16, 0:512]),
                      waits=[("t", t_mm)], inc="v")
            pE_free_v[0] = t_l1
            for oc in range(32):
                p = oc % 4
                if oc == 0:
                    for pf in range(3):
                        t_wf[pf] = em("sync", lambda e, pf=pf: e.dma_start(
                            out=wfcst[pf][:], in_=s.wfcq_hbm[pf]),
                            waits=[("t", t_wf_mm[pf])], inc="d_wf%d" % pf)
                if oc + 3 < 32:
                    pn = (oc + 3) % 4
                    t_wf[pn] = em("sync", lambda e, oc=oc, pn=pn: e.dma_start(
                        out=wfcst[pn][:], in_=s.wfcq_hbm[oc + 3]),
                        waits=[("t", t_wf_mm[pn])], inc="d_wf%d" % pn)
                t_wd = t_wf[p]
                sl = oc % 4
                t_mm = None
                for ic in range(NCH):
                    t_mm = em("tensor", lambda e, ic=ic, oc=oc, p=p, sl=sl: e.matmul(
                        out=s.pE[:, sl * 512:(sl + 1) * 512],
                        lhsT=wfcst[p][:, ic * 128:(ic + 1) * 128],
                        rhs=h2[:, ic * 512:(ic + 1) * 512],
                        start=(ic == 0), stop=False),
                        waits=[("d_wf%d" % p, t_wd), ("s", pE_free[sl]),
                               ("v", pE_free_v[sl]), ("s", t_h2), ("v", t_l1)],
                        inc=None)
                t_mm = em("tensor", lambda e, oc=oc, sl=sl: e.matmul(
                    out=s.pE[:, sl * 512:(sl + 1) * 512],
                    lhsT=bfcT[0:16, oc * 128:(oc + 1) * 128],
                    rhs=s.lora1[:],
                    start=False, stop=True),
                    waits=[("v", t_l1), ("d_bfc", t_bfcT)], inc="t")
                t_wf_mm[oc % 4] = t_mm
                pE_free[sl] = em("scalar", lambda e, oc=oc, sl=sl: e.activation(
                    out=hid[:, oc * 512:(oc + 1) * 512],
                    in_=s.pE[:, sl * 512:(sl + 1) * 512], func=AF.Gelu,
                    bias=s.bfc_col[:, oc:oc + 1]),
                    waits=[("t", t_mm), ("t", t_hid_mm)], inc="s")
            t_fc = pE_free[(32 - 1) % 4]
            t_lora1_mm = t_wf_mm[(32 - 1) % 2]
            t_h2_mm = t_wf_mm[(32 - 1) % 2]
            t_mm = None
            for hc in range(32):
                t_mm = em("tensor", lambda e, hc=hc: e.matmul(
                    out=s.pE[0:16, 512:1024], lhsT=s.apjT[:, hc * 16:(hc + 1) * 16],
                    rhs=hid[:, hc * 512:(hc + 1) * 512],
                    start=(hc == 0), stop=(hc == 31)),
                    waits=[("s", pE_free[1]), ("v", pE_free_v[1]),
                           ("s", pE_free[(32 - 1) % 4]),
                           ("t", t_lora1p_mm)],
                    inc="t" if hc == 31 else None)
            t_l1p = em("vector", lambda e: e.tensor_copy(out=s.lora1p[:],
                                                         in_=s.pE[0:16, 512:1024]),
                       waits=[("t", t_mm)], inc="v")
            pE_free_v[1] = t_l1p
            for tt in range(4):
                po = 0
                t_mm = None
                for vc in range(2):
                    for hc in range(32):
                        t_mm = em("tensor", lambda e, tt=tt, vc=vc, hc=hc: e.matmul(
                            out=s.pO[:, vc * 512:(vc + 1) * 512],
                            lhsT=hid[:, hc * 512 + tt * 128:hc * 512 + (tt + 1) * 128],
                            rhs=wpj[:, hc * 1024 + vc * 512:hc * 1024 + (vc + 1) * 512],
                            start=(hc == 0), stop=False),
                            waits=[("s", t_fc), ("v", t_pv_out[vc]),
                                   ("d_wpj", t_wpj), ("v", t_l1p)],
                            inc=None)
                    t_mm = em("tensor", lambda e, tt=tt, vc=vc: e.matmul(
                        out=s.pO[:, vc * 512:(vc + 1) * 512],
                        lhsT=s.lora1p[0:16, tt * 128:(tt + 1) * 128],
                        rhs=s.bpjT[0:16, vc * 512:(vc + 1) * 512],
                        start=False, stop=True),
                        waits=[("v", t_l1p)], inc="t")
                    t_lora1p_mm = t_mm
                t_hid_mm = t_mm
                t_e1 = em("vector", lambda e, tt=tt, po=po: e.scalar_tensor_tensor(
                    out=outb[po][:, 0:512], in0=s.pO[:, 0:512], scalar=1.0,
                    in1=bpjgb[:, 0:512], op0=OP.bypass, op1=OP.add),
                    waits=[("t", t_mm), ("v", t_bpjgb),
                           ("d_out%d" % po, t_out[po])], inc="v")
                t_e2 = em("vector", lambda e, tt=tt, po=po: e.scalar_tensor_tensor(
                    out=outb[po][:, 512:1024], in0=s.pO[:, 512:1024], scalar=1.0,
                    in1=bpjgb[:, 512:1024], op0=OP.bypass, op1=OP.add),
                    waits=[("v", t_e1)], inc="v")
                t_pv_out[0] = t_e1
                t_pv_out[1] = t_e2
                t_a1 = em("vector", lambda e, tt=tt, po=po: e.tensor_tensor(
                    out=outb[po][:], in0=outb[po][:],
                    in1=x1blk[:, tt * 1024:(tt + 1) * 1024], op=OP.add),
                    waits=[("v", t_e2)], inc="v")
                t_x1blk[tt] = t_a1
                t_outv[po] = t_a1
                t_out[po] = em("gpsimd", lambda e, tb=tb, tt=tt, po=po: e.dma_start(
                    out=s.out_hbm[tb * 512 + tt * 128:tb * 512 + (tt + 1) * 128, :],
                    in_=outb[po][:]),
                    waits=[("v", t_a1)], inc="d_out%d" % po)
        self.t_ph6 = dict(self.cnt)
        if self.debug:
            em("gpsimd", lambda e: e.dma_start(out=self.dbg["cols"][:, 40:48],
                                               in_=s.pjscale_col[:]),
               waits=[("v", self.t_qfin)], inc="d_dbg")
            em("gpsimd", lambda e: e.dma_start(out=self.dbg["wpjqc"][0:16, :],
                                               in_=s.bpjT[:]),
               waits=[("v", self.t_bpjT)], inc="d_dbg")
            em("gpsimd", lambda e: e.dma_start(out=self.dbg["wfcqc"][0:128, 0:1024],
                                               in_=wpj[:, 4 * 1024:5 * 1024]),
               waits=[("d_wpj", self.cnt["d_wpj"])], inc="d_dbg")
            em("gpsimd", lambda e: e.dma_start(out=self.dbg["wpjqc"][16:32, 0:1024],
                                               in_=bpjgb[0:16, :]),
               waits=[("v", t_bpjgb)], inc="d_dbg")

    def finish(self):
        nc = self.nc
        final_waits = [(k, v) for k, v in self.cnt.items()
                       if k.startswith("d") and v > 0]
        self.em("gpsimd", lambda e: e.engine_nop(), waits=final_waits)
        sem = {k: self.es.enter_context(nc.semaphore("sem_" + k))
               for k in self.cnt}
        self.sem = sem
        blk = self.es.enter_context(nc.Block())
        last_wait = {e: {} for e in ENGINES}

        def run_section(engname, engobj):
            for fn, waits, inc, inc_n in self.ops[engname]:
                for sname, v in waits:
                    if last_wait[engname].get(sname, 0) < v:
                        engobj.wait_ge(sem[sname], v)
                        last_wait[engname][sname] = v
                instr = fn(engobj)
                if inc is not None:
                    instr.then_inc(sem[inc], inc_n)

        @blk.tensor
        def _(e):
            run_section("tensor", e)

        @blk.vector
        def _(e):
            run_section("vector", e)

        @blk.scalar
        def _(e):
            run_section("scalar", e)

        @blk.sync
        def _(e):
            run_section("sync", e)

        @blk.gpsimd
        def _(e):
            run_section("gpsimd", e)

        return nc


def build(debug=0, phases=99):
    mk = MK(debug=debug, phases=phases)
    mk.declare()
    mk.ph0()
    if phases >= 1:
        mk.ph1()
    if phases >= 3:
        mk.ph3()
    if phases >= 4:
        mk.ph4()
    if phases >= 5:
        mk.ph5()
    if phases >= 6:
        mk.ph6()
    return mk.finish()





_NC_CACHE = {}


def _get_nc():
    if "nc" not in _NC_CACHE:
        _NC_CACHE["nc"] = build(debug=0, phases=6)
    return _NC_CACHE["nc"]


def kernel(L0_attn_x, L0_attn_Wk, L0_attn_Wq, L0_attn_Wv, L0_attn_Wa, L0_attn_Wb,
           L0_ffn_W_fc, L0_ffn_b_fc, L0_ffn_A_fc, L0_ffn_B_fc, L0_ffn_W_proj,
           L0_ffn_b_proj, L0_ffn_A_proj, L0_ffn_B_proj, L0_c, L0_wmod):
    x = np.ascontiguousarray(np.asarray(L0_attn_x, dtype=np.float32))
    idb = np.eye(128, dtype=ml_dtypes.bfloat16)
    idf = np.eye(128, dtype=np.float32)
    onesf = np.ones((1, 128), dtype=np.float32)
    common = {
        "c_in": None,
        "wmod_in": np.ascontiguousarray(L0_wmod, dtype=np.float32),
        "wk_in": np.ascontiguousarray(L0_attn_Wk, dtype=np.float32),
        "wq_in": np.ascontiguousarray(L0_attn_Wq, dtype=np.float32),
        "wv_in": np.ascontiguousarray(L0_attn_Wv, dtype=np.float32),
        "wa_in": np.ascontiguousarray(L0_attn_Wa, dtype=np.float32),
        "wb_in": np.ascontiguousarray(L0_attn_Wb, dtype=np.float32),
        "wfc_in": np.ascontiguousarray(L0_ffn_W_fc, dtype=np.float32),
        "bfc_in": np.ascontiguousarray(L0_ffn_b_fc, dtype=np.float32),
        "afc_in": np.ascontiguousarray(L0_ffn_A_fc, dtype=np.float32),
        "bfcB_in": np.ascontiguousarray(L0_ffn_B_fc, dtype=np.float32),
        "wpj_in": np.ascontiguousarray(L0_ffn_W_proj, dtype=np.float32),
        "bpj_in": np.ascontiguousarray(
            np.asarray(L0_ffn_b_proj, dtype=np.float32).reshape(1, D)),
        "apj_in": np.ascontiguousarray(L0_ffn_A_proj, dtype=np.float32),
        "bpjB_in": np.ascontiguousarray(L0_ffn_B_proj, dtype=np.float32),
        "idb_in": idb, "idf_in": idf, "onesf_in": onesf,
    }
    in_maps = []
    for c in range(8):
        b, hf = c // 2, c % 2
        xb = x[b]
        x_c = np.concatenate([xb[hf * OWN:(hf + 1) * OWN],
                              xb[(1 - hf) * OWN:(2 - hf) * OWN]], axis=0)
        m = dict(common)
        m["x_in"] = np.ascontiguousarray(x_c)
        m["c_in"] = np.ascontiguousarray(
            np.asarray(L0_c, dtype=np.float32)[b:b + 1, :])
        in_maps.append(m)
    nc = _get_nc()
    res = run_bass_kernel_spmd(nc, in_maps, core_ids=list(range(8)))
    out = np.empty((4, S, D), dtype=np.float32)
    for c in range(8):
        b, hf = c // 2, c % 2
        out[b, hf * OWN:(hf + 1) * OWN, :] = res.results[c]["out"]
    return out


# version 29
# speedup vs baseline: 1.1254x; 1.0323x over previous
import numpy as np
import ml_dtypes
import concourse.bass as bass
import concourse.mybir as mybir
from concourse.bass_utils import run_bass_kernel_spmd
from concourse.alu_op_type import AluOpType as OP
from contextlib import ExitStack

F32 = mybir.dt.float32
BF16 = mybir.dt.bfloat16
I32 = mybir.dt.int32
AF = mybir.ActivationFunctionType
AX = mybir.AxisListType

S, OWN, D, FF, R = 4096, 2048, 1024, 4096, 16
NCH = 8
NG = 8
NGOWN = 4
NQT = 16
ENGINES = ("tensor", "vector", "scalar", "sync", "gpsimd")


class MK:
    def __init__(self, debug=0, phases=99):
        self.debug = debug
        self.phases = phases
        self.nc = bass.Bass("TRN2", target_bir_lowering=False)
        self.ops = {e: [] for e in ENGINES}
        self.cnt = {k: 0 for k in ("t", "v", "s", "g")}
        self.es = ExitStack()
        self.esR = ExitStack()
        self.dbg = {}

    def em(self, eng, fn, waits=(), inc=None, inc_n=None):
        if inc is not None and inc not in self.cnt:
            self.cnt[inc] = 0
        for snm, _v in waits:
            if snm not in self.cnt:
                self.cnt[snm] = 0
        if inc is not None and inc_n is None:
            inc_n = 16 if inc.startswith("d") else 1
        tok = None
        if inc is not None:
            self.cnt[inc] += inc_n
            tok = self.cnt[inc]
        self.ops[eng].append((fn, tuple((s, v) for (s, v) in waits if v), inc, inc_n))
        return tok

    def sb(self, name, shape, dt=F32, right=False):
        stack = self.esR if right else self.es
        kw = {"side": "right"} if right else {}
        return stack.enter_context(self.nc.sbuf_tensor(name, shape, dt, **kw))

    def ps(self, name, shape, dt=F32):
        return self.es.enter_context(self.nc.psum_tensor(name, shape, dt))

    def dram(self, name, shape, dt, kind):
        return self.nc.dram_tensor(name, shape, dt, kind=kind).ap()

    def declare(self):
        nc, d = self.nc, self.dram
        self.x_in = d("x_in", [S, D], F32, "ExternalInput")
        self.c_in = d("c_in", [1, D], F32, "ExternalInput")
        self.wmod_in = d("wmod_in", [D, 6 * D], F32, "ExternalInput")
        self.wk_in = d("wk_in", [D, D], F32, "ExternalInput")
        self.wq_in = d("wq_in", [D, D], F32, "ExternalInput")
        self.wv_in = d("wv_in", [D, D], F32, "ExternalInput")
        self.wa_in = d("wa_in", [D, D], F32, "ExternalInput")
        self.wb_in = d("wb_in", [D, D], F32, "ExternalInput")
        self.wfc_in = d("wfc_in", [FF, D], F32, "ExternalInput")
        self.bfc_in = d("bfc_in", [FF], F32, "ExternalInput")
        self.afc_in = d("afc_in", [R, D], F32, "ExternalInput")
        self.bfcB_in = d("bfcB_in", [FF, R], F32, "ExternalInput")
        self.wpj_in = d("wpj_in", [D, FF], F32, "ExternalInput")
        self.bpj_in = d("bpj_in", [1, D], F32, "ExternalInput")
        self.apj_in = d("apj_in", [R, FF], F32, "ExternalInput")
        self.bpjB_in = d("bpjB_in", [D, R], F32, "ExternalInput")
        self.idb_in = d("idb_in", [128, 128], BF16, "ExternalInput")
        self.idf_in = d("idf_in", [128, 128], F32, "ExternalInput")
        self.onesf_in = d("onesf_in", [1, 128], F32, "ExternalInput")

        self.out_hbm = d("out", [OWN, D], F32, "ExternalOutput")
        self.x1_hbm = d("x1_sc", [OWN, D], F32, "Internal")
        self.q_hbm = d("q_sc", [16, 128, NCH, 128], BF16, "Internal")
        self.wfcq_hbm = d("wfcq_sc", [32, 128, 8, 128], BF16, "Internal")
        self.wpjq_hbm = d("wpjq_sc", [FF, D], BF16, "Internal")
        self.bfcT_hbm = d("bfcT_sc", [R, FF], BF16, "Internal")
        self.bpjrow_hbm = d("bpjrow_sc", [1, D], F32, "Internal")

        if self.debug:
            for nm, shp in (("mod", [1, 2048]), ("cols", [128, 48]),
                            ("hst", [128, 4096]), ("akc", [128, 1024]),
                            ("uc", [128, 1024]), ("qc", [128, 1024]),
                            ("amax", [128, 8]), ("x1c", [128, 1024]), ("wfcqc", [128, 1024]),
                            ("wpjqc", [128, 1024])):
                self.dbg[nm] = d("dbg_" + nm, shp, F32, "ExternalOutput")


        sb = self.sb
        self.idb = sb("idb", [128, 128], BF16)
        self.idf = sb("idf", [128, 128], F32)
        self.onesf = sb("onesf", [1, 128], F32)
        self.kcol = sb("kcol", [128, 16], F32)
        self.ca1_col = sb("ca1_col", [128, 8], F32)
        self.sa_col = sb("sa_col", [128, 8], F32)
        self.cb1_col = sb("cb1_col", [128, 8], F32)
        self.sbb_col = sb("sbb_col", [128, 8], F32)
        self.gbT_col = sb("gbT_col", [128, 8], F32)
        self.pjscale_col = sb("pjscale_col", [128, 8], F32)
        self.bfc_col = sb("bfc_col", [128, 32], F32)
        self.scT = sb("scT", [128, 16], BF16)
        self.x_st0 = sb("x_st0", [128, 1024], F32)
        self.xnb = sb("xnb", [128, 1024], BF16)
        self.trash = sb("trash", [128, 1024], F32)
        self.amst = [sb("amst%d" % i, [128, 256], F32, right=True) for i in range(2)]
        self.stats = sb("stats", [128, 32], F32)
        self.amx = sb("amx", [128, 8], F32)
        self.afcT = sb("afcT", [128, NCH * R], BF16)
        self.apjT = sb("apjT", [128, 32 * R], BF16)
        self.bpjT = sb("bpjT", [R, 1024], BF16)
        self.lora1 = sb("lora1", [R, 512], BF16)
        self.lora1p = sb("lora1p", [R, 512], BF16)
        self.mcol = sb("mcol", [128, 8], F32)
        self.lcolv = sb("lcolv", [128, 8], F32)

        self.esP1 = ExitStack()
        def sb1(name, shape, dt=F32):
            return self.esP1.enter_context(nc.sbuf_tensor(name, shape, dt))
        self.garow = sb1("garow", [1, 1024])
        self.gbrow = sb1("gbrow", [1, 1024])
        self.crow = sb1("crow", [1, 1024])
        self.scrow = sb1("scrow", [1, 1024])
        self.rowa = sb1("rowa", [1, 1024])
        self.rowb = sb1("rowb", [1, 1024])
        self.rowc = sb1("rowc", [1, 1024])
        self.wm = sb1("wm", [128, 3072])

        self.ak_fm = sb("ak_fm", [128, NCH * S], BF16, right=True)
        self.u_tm = sb("u_tm", [128, 32 * 1024], BF16, right=True)
        self.schi = self.u_tm[0:1, 6144:7168]
        self.sclo = self.u_tm[0:1, 7168:8192]
        self.l16b = self.u_tm[:, 8192:9216]
        self.l16a = self.wm[:, 0:1024]

        self.pE = self.ps("pE", [128, 2048], F32)
        self.pT0 = self.ps("pT0", [128, 1024], BF16)
        self.pT1 = self.ps("pT1", [128, 1024], BF16)
        self.pO = self.ps("pO", [128, 1024], F32)

    def ph0(self):
        em, s = self.em, self
        em("sync", lambda e: e.dma_start(out=s.idb[:], in_=s.idb_in), inc="d_ph0")
        em("sync", lambda e: e.dma_start(out=s.idf[:], in_=s.idf_in), inc="d_ph0")
        em("sync", lambda e: e.dma_start(out=s.onesf[:], in_=s.onesf_in), inc="d_ph0")
        em("sync", lambda e: e.dma_start(out=s.crow[:], in_=s.c_in), inc="d_ph0")
        for cc in range(32):
            em("sync", lambda e, cc=cc: e.dma_start(
                out=s.bfc_col[:, cc:cc + 1], in_=s.bfc_in[cc * 128:(cc + 1) * 128]),
                inc="d_ph0")
        self.t_ld0 = self.cnt["d_ph0"]
        em("gpsimd", lambda e: e.memset(s.kcol[:, 0:1], 1e-5))
        em("gpsimd", lambda e: e.memset(s.kcol[:, 1:2], -0.5))
        em("gpsimd", lambda e: e.memset(s.kcol[:, 2:3], 0.5))
        t_ms = em("gpsimd", lambda e: e.memset(s.kcol[:, 3:4], 127.0), inc="g")
        self.t_rec127 = em("vector",
                           lambda e: e.reciprocal(out=s.kcol[:, 4:5], in_=s.kcol[:, 3:4]),
                           waits=[("g", t_ms)], inc="v")

    def ph1(self):
        em, s = self.em, self
        t_silu = em("scalar", lambda e: e.activation(out=s.scrow[:], in_=s.crow[:],
                                                     func=AF.Silu),
                    waits=[("d_ph0", self.t_ld0)], inc="s")
        t_hi = em("vector", lambda e: e.tensor_copy(out=s.schi, in_=s.scrow[:]),
                  waits=[("s", t_silu)], inc="v")
        t_lo = em("vector", lambda e: e.tensor_tensor(out=s.sclo, in0=s.scrow[:],
                                                      in1=s.schi, op=OP.subtract),
                  waits=[("v", t_hi)], inc="v")
        t_tr = None
        for kc in range(NCH):
            for j, src in ((0, s.schi), (1, s.sclo)):
                t_tr = em("tensor", lambda e, kc=kc, src=src, j=j: e.transpose(
                    out=s.pT0[:, (kc * 2 + j) * 2:(kc * 2 + j) * 2 + 1],
                    in_=src[0:1, kc * 128:(kc + 1) * 128], identity=s.idb[0:1, 0:1]),
                    waits=[("v", t_lo), ("d_ph0", self.t_ld0)],
                    inc="t" if (kc == NCH - 1 and j == 1) else None)
        t_scT = em("vector", lambda e: e.tensor_copy(out=s.scT[:], in_=s.pT0[:, 0:32:2]),
                   waits=[("t", t_tr)], inc="v")
        self.t_scT = t_scT

        whi = [s.u_tm[:, 0:1536], s.u_tm[:, 3072:4608]]
        wlo = [s.u_tm[:, 1536:3072], s.u_tm[:, 4608:6144]]
        wmh = [s.wm[:, 0:1536], s.wm[:, 1536:3072]]
        acc = [s.pE[0:1, 512 * k:512 * (k + 1)] for k in range(4)] + \
              [s.pO[0:1, 0:512], s.pO[0:1, 512:1024]]
        t_split_v = t_split_s = 0
        t_mmgrp = 0
        t_mm_half = [0, 0]
        t_split_prev = [(0, 0), (0, 0)]
        t_proc_v = 0
        t_row = {}
        for rnd in range(2):
            for kc in range(NCH):
                for hp in range(2):
                    p = (kc * 2 + hp) % 2
                    t_wd = em("sync", lambda e, kc=kc, rnd=rnd, hp=hp, p=p: e.dma_start(
                        out=wmh[p], in_=s.wmod_in[kc * 128:(kc + 1) * 128,
                                                  rnd * 3072 + hp * 1536:
                                                  rnd * 3072 + (hp + 1) * 1536]),
                        waits=[("v", t_split_prev[p][0]), ("s", t_split_prev[p][1])],
                        inc="d_wm%d" % p)
                    t_split_s = em("scalar", lambda e, p=p: e.activation(
                        out=whi[p], in_=wmh[p], func=AF.Copy),
                        waits=[("d_wm%d" % p, t_wd), ("t", t_mm_half[p])], inc="s")
                    t_split_v = em("vector", lambda e, p=p: e.tensor_tensor(
                        out=wlo[p], in0=wmh[p], in1=whi[p], op=OP.subtract),
                        waits=[("s", t_split_s)], inc="v")
                    t_split_prev[p] = (t_split_v, t_split_s)
                    for s3 in range(3):
                        sl = hp * 3 + s3
                        for ti, (par, rhs) in enumerate(((0, whi[p]), (1, whi[p]),
                                                         (0, wlo[p]))):
                            st = (kc == 0 and ti == 0)
                            sp = (kc == NCH - 1 and ti == 2)
                            tok = em("tensor", lambda e, kc=kc, par=par, rhs=rhs, sl=sl,
                                     s3=s3, st=st, sp=sp: e.matmul(
                                         out=acc[sl],
                                         lhsT=s.scT[:, kc * 2 + par:kc * 2 + par + 1],
                                         rhs=rhs[:, s3 * 512:(s3 + 1) * 512],
                                         start=st, stop=sp),
                                     waits=[("v", t_split_v), ("v", t_scT)],
                                     inc="t" if (s3 == 2 and ti == 2) else None)
                            if tok:
                                t_mmgrp = tok
                                t_mm_half[p] = tok
            names = (("sa", "ca", "ga"), ("sb", "cb", "gb"))[rnd]
            rows = {"sa": s.rowa, "ca": s.rowc, "ga": s.garow,
                    "sb": s.rowa, "cb": s.rowc, "gb": s.gbrow}
            t_ev = {}
            for rr, name in enumerate(names):
                for h2 in range(2):
                    t_ev[name] = em("vector", lambda e, rr=rr, h2=h2, name=name:
                                    e.tensor_copy(
                                        out=rows[name][0:1, h2 * 512:(h2 + 1) * 512],
                                        in_=acc[rr * 2 + h2]),
                                    waits=[("t", t_mmgrp), ("v", t_proc_v)], inc="v")
            for name in names:
                if name in ("ga", "gb"):
                    g = rows[name]
                    t_sg = em("scalar", lambda e, g=g: e.activation(
                        out=s.rowb[:], in_=g[:], func=AF.Sign),
                        waits=[("v", t_ev[name])], inc="s")
                    t_ab = em("scalar", lambda e, g=g: e.activation(
                        out=s.scrow[:], in_=g[:], func=AF.Abs),
                        waits=[("s", t_sg)], inc="s")
                    t_mx = em("vector", lambda e: e.tensor_scalar(
                        out=s.scrow[:], in0=s.scrow[:], scalar1=0.25, scalar2=None,
                        op0=OP.max),
                        waits=[("s", t_ab)], inc="v")
                    t_g = em("vector", lambda e, g=g: e.tensor_tensor(
                        out=g[:], in0=s.rowb[:], in1=s.scrow[:], op=OP.mult),
                        waits=[("v", t_mx)], inc="v")
                    t_row[name] = t_g
                    t_proc_v = t_g
                else:
                    dcol = {"sa": s.sa_col, "ca": s.ca1_col,
                            "sb": s.sbb_col, "cb": s.cb1_col}[name]
                    plus1 = name in ("ca", "cb")
                    g = rows[name]
                    t_ct = None
                    for kc in range(NCH):
                        t_ct = em("tensor", lambda e, kc=kc, g=g: e.transpose(
                            out=s.pO[:, 512 + kc:513 + kc],
                            in_=g[0:1, kc * 128:(kc + 1) * 128],
                            identity=s.idf[0:1, 0:1]),
                            waits=[("v", t_ev[name]), ("v", t_proc_v),
                                   ("v", t_ev.get(names[2], t_ev[name]))],
                            inc="t" if kc == NCH - 1 else None)
                    t_proc_v = em("vector", lambda e, dcol=dcol, plus1=plus1:
                                  e.tensor_scalar(out=dcol[:, 0:8],
                                                  in0=s.pO[:, 512:520],
                                                  scalar1=1.0 if plus1 else 0.0,
                                                  scalar2=None, op0=OP.add),
                                  waits=[("t", t_ct)], inc="v")
                    t_row[name + "_col"] = t_proc_v
            t_mmgrp = self.cnt["t"]
        self.t_row = t_row

        t_ct = None
        for kc in range(NCH):
            t_ct = em("tensor", lambda e, kc=kc: e.transpose(
                out=s.pO[:, 512 + kc:513 + kc],
                in_=s.gbrow[0:1, kc * 128:(kc + 1) * 128], identity=s.idf[0:1, 0:1]),
                waits=[("v", t_row["gb"]), ("v", t_proc_v)],
                inc="t" if kc == NCH - 1 else None)
        t_gbT = em("vector", lambda e: e.tensor_copy(out=s.gbT_col[:, 0:8],
                                                     in_=s.pO[:, 512:520]),
                   waits=[("t", t_ct)], inc="v")
        self.t_gbT = t_gbT

        def bcast_row(row_ap, out_sb, wait_list):
            t_mm = None
            for hh in range(2):
                t_mm = em("tensor", lambda e, hh=hh, row_ap=row_ap: e.matmul(
                    out=s.pO[:, hh * 512:(hh + 1) * 512], lhsT=s.onesf[:],
                    rhs=row_ap[0:1, hh * 512:(hh + 1) * 512], start=True, stop=True),
                    waits=wait_list, inc="t" if hh else None)
            return em("vector", lambda e, out_sb=out_sb: e.tensor_copy(
                out=out_sb[:], in_=s.pO[:]), waits=[("t", t_mm)], inc="v")

        t_gbbc = bcast_row(s.gbrow, s.trash, [("v", t_row["gb"]), ("v", t_gbT)])

        t_d = em("sync", lambda e: e.dma_start(out=s.l16a[0:16, 0:1024], in_=s.afc_in),
                 waits=[("s", self.cnt["s"]), ("v", self.cnt["v"])], inc="d_l16")
        t_cast = em("scalar", lambda e: e.activation(out=s.l16b[0:16, 0:1024],
                                                     in_=s.l16a[0:16, 0:1024],
                                                     func=AF.Copy, scale=2.0),
                    waits=[("d_l16", t_d)], inc="s")
        t_tr2 = None
        for ic in range(NCH):
            t_tr2 = em("tensor", lambda e, ic=ic: e.transpose(
                out=s.pT0[:, ic * 16:(ic + 1) * 16],
                in_=s.l16b[0:16, ic * 128:(ic + 1) * 128], identity=s.idb[0:16, 0:16]),
                waits=[("s", t_cast), ("v", t_scT)], inc="t" if ic == NCH - 1 else None)
        t_afcT = em("vector", lambda e: e.tensor_copy(out=s.afcT[:],
                                                      in_=s.pT0[:, 0:NCH * 16]),
                    waits=[("t", t_tr2)], inc="v")
        t_apjT = 0
        for pc in range(4):
            t_d = em("sync", lambda e, pc=pc: e.dma_start(
                out=s.l16a[0:16, 0:1024], in_=s.apj_in[0:16, pc * 1024:(pc + 1) * 1024]),
                waits=[("s", t_cast), ("t", t_tr2)], inc="d_l16")
            t_cast = em("scalar", lambda e: e.activation(
                out=s.l16b[0:16, 0:1024], in_=s.l16a[0:16, 0:1024], func=AF.Copy, scale=2.0),
                waits=[("d_l16", t_d), ("v", t_afcT if pc == 0 else t_apjT)], inc="s")
            for j in range(NCH):
                t_tr2 = em("tensor", lambda e, j=j: e.transpose(
                    out=s.pT0[:, j * 16:(j + 1) * 16],
                    in_=s.l16b[0:16, j * 128:(j + 1) * 128],
                    identity=s.idb[0:16, 0:16]),
                    waits=[("s", t_cast)], inc="t" if j == NCH - 1 else None)
            t_apjT = em("vector", lambda e, pc=pc: e.tensor_copy(
                out=s.apjT[:, pc * 128:(pc + 1) * 128], in_=s.pT0[:, 0:128]),
                waits=[("t", t_tr2)], inc="v")
        t_d = em("sync", lambda e: e.dma_start(
            out=s.l16a[:, 0:128].rearrange("p (j c) -> p j c", c=16),
            in_=s.bpjB_in.rearrange("(j p) c -> p j c", p=128)),
            waits=[("s", t_cast), ("t", t_tr2), ("v", t_apjT)], inc="d_l16")
        t_cast2 = em("scalar", lambda e: e.activation(
            out=s.l16b[:, 0:128], in_=s.l16a[:, 0:128], func=AF.Copy),
            waits=[("d_l16", t_d)], inc="s")
        t_bt = None
        for dc in range(NCH):
            t_bt = em("tensor", lambda e, dc=dc: e.transpose(
                out=s.pT1[0:16, dc * 128:(dc + 1) * 128],
                in_=s.l16b[:, dc * 16:(dc + 1) * 16],
                identity=s.idb[:, :]),
                waits=[("s", t_cast2)], inc="t" if dc == NCH - 1 else None)
        t_bp0 = em("vector", lambda e: e.tensor_copy(out=s.bpjT[:],
                                                     in_=s.pT1[0:16, 0:1024]),
                   waits=[("t", t_bt)], inc="v")
        t_bpjT = em("vector", lambda e: e.tensor_tensor(
            out=s.bpjT[:], in0=s.bpjT[:], in1=s.trash[0:16, :], op=OP.mult),
            waits=[("v", t_bp0), ("v", t_gbbc)], inc="v")
        self.t_bpjT = t_bpjT
        t_d = em("sync", lambda e: e.dma_start(out=s.rowb[:], in_=s.bpj_in),
                 waits=[("v", self.cnt["v"])], inc="d_bpj")
        t_bb = em("vector", lambda e: e.tensor_tensor(out=s.rowb[:], in0=s.rowb[:],
                                                      in1=s.gbrow[:], op=OP.mult),
                  waits=[("d_bpj", t_d), ("v", t_bpjT)], inc="v")
        t_bpjgb = em("gpsimd", lambda e: e.dma_start(out=s.bpjrow_hbm, in_=s.rowb[:]),
                     waits=[("v", t_bb)], inc="d_bpr")
        self.t_bpjgb = t_bpjgb
        t_gabc = bcast_row(s.garow, s.trash, [("v", t_row["ga"]), ("v", t_bpjT),
                                              ("v", t_bb)])
        self.t_gabc = t_gabc
        self.ga_bc = s.trash
        l16c = s.u_tm[:, 9216:10240]
        t_pc = 0
        t_cast3 = [0, 0]
        t_bt3 = [t_bt, t_bt]
        t_ev2 = 0
        for b in range(4):
            rb = b % 2
            t_d = em("sync", lambda e, b=b, rb=rb: e.dma_start(
                out=s.l16a[:, 128 + rb * 128:128 + (rb + 1) * 128]
                .rearrange("p (j c) -> p j c", c=16),
                in_=s.bfcB_in[b * 1024:(b + 1) * 1024, :]
                .rearrange("(j p) c -> p j c", p=128)),
                waits=[("s", t_cast3[rb] or t_cast2)], inc="d_lb%d" % rb)
            t_cast3[rb] = em("scalar", lambda e, rb=rb: e.activation(
                out=s.l16b[:, 128 + rb * 128:128 + (rb + 1) * 128],
                in_=s.l16a[:, 128 + rb * 128:128 + (rb + 1) * 128], func=AF.Copy),
                waits=[("d_lb%d" % rb, t_d), ("t", t_bt3[rb])], inc="s")
            t_tr3 = None
            for j in range(8):
                t_tr3 = em("tensor", lambda e, j=j, rb=rb: e.transpose(
                    out=s.pT1[0:16, j * 128:(j + 1) * 128],
                    in_=s.l16b[:, 128 + rb * 128 + j * 16:128 + rb * 128 + (j + 1) * 16],
                    identity=s.idb[:, :]),
                    waits=[("s", t_cast3[rb]), ("v", t_ev2)],
                    inc="t" if j == 7 else None)
            t_bt3[rb] = t_tr3
            t_ev2 = em("vector", lambda e: e.tensor_copy(out=l16c[0:16, 0:1024],
                                                         in_=s.pT1[0:16, 0:1024]),
                       waits=[("t", t_tr3), ("d_lr1", t_pc)], inc="v")
            t_pc = em("gpsimd", lambda e, b=b: e.dma_start(
                out=s.bfcT_hbm[0:16, b * 1024:(b + 1) * 1024],
                in_=l16c[0:16, 0:1024]),
                waits=[("v", t_ev2)], inc="d_lr1")
        self.t_ph1_end = dict(self.cnt)

        if self.debug:
            em("gpsimd", lambda e: e.dma_start(out=self.dbg["mod"][0:1, 0:1024],
                                               in_=s.garow[:]),
               waits=[("v", t_row["ga"])], inc="d_dbg")
            em("gpsimd", lambda e: e.dma_start(out=self.dbg["mod"][0:1, 1024:2048],
                                               in_=s.gbrow[:]),
               waits=[("v", t_row["gb"])], inc="d_dbg")
            em("gpsimd", lambda e: e.dma_start(out=self.dbg["cols"][:, 0:8],
                                               in_=s.ca1_col[:]),
               waits=[("v", t_row["ca_col"])], inc="d_dbg")
            em("gpsimd", lambda e: e.dma_start(out=self.dbg["cols"][:, 8:16],
                                               in_=s.sa_col[:]),
               waits=[("v", t_row["sa_col"])], inc="d_dbg")
            em("gpsimd", lambda e: e.dma_start(out=self.dbg["cols"][:, 16:24],
                                               in_=s.gbT_col[:]),
               waits=[("v", t_gbT)], inc="d_dbg")
            em("gpsimd", lambda e: e.dma_start(out=self.dbg["cols"][:, 24:32],
                                               in_=s.cb1_col[:]),
               waits=[("v", t_row["cb_col"])], inc="d_dbg")
            em("gpsimd", lambda e: e.dma_start(out=self.dbg["cols"][:, 32:40],
                                               in_=s.sbb_col[:]),
               waits=[("v", t_row["sb_col"])], inc="d_dbg")

    def ph3(self):
        em, s = self.em, self
        self.esP1.close()
        self._t_am_v = self.cnt["v"]
        self._t_am_v2 = [self.cnt["v"], self.cnt["v"]]
        self._t_am_b = [self.cnt["v"], self.cnt["v"]]
        self._am_idx = 0
        self.esW = ExitStack()
        self.wqT = self.esW.enter_context(self.nc.sbuf_tensor("wqT", [128, 8192], BF16))
        self.wkaT = self.esW.enter_context(self.nc.sbuf_tensor("wkaT", [128, 8192], BF16))
        self.wvbT = self.esW.enter_context(self.nc.sbuf_tensor("wvbT", [128, 8192], BF16))
        self.h_stage = self.esW.enter_context(self.nc.sbuf_tensor("h_stage", [128, 4096], BF16))
        self.x_st1 = self.esW.enter_context(self.nc.sbuf_tensor("x_st1", [128, 1024], F32))

        self.qev = [self.u_tm[:, 31744:32256], self.u_tm[:, 32256:32768]]
        self.x_stage = [self.x_st0, self.x_st1]
        fold1 = s.ak_fm[:, 0:8192]
        fold2 = s.ak_fm[:, 8192:16384]
        wtmp = s.u_tm[:, 9216:10240]

        t_xs = [0, 0]
        t_xs_eng = ["v", "v"]
        t_wtmp = 0
        t_pT = [self.cnt["t"], self.cnt["t"]]
        t_pT_ev = [self.cnt["s"], self.cnt["s"]]
        t_pT_ev = [0, 0]
        ph1_t = self.cnt["t"]
        ph1_v = self.cnt["v"]
        ph1_s = self.cnt["s"]
        bar = [(k, self.cnt.get(k, 0)) for k in ("d_bpr", "d_lr1", "d_l16", "d_wm",
                                                 "d_bpj", "d_ph0", "d_dbg")]

        def load_cast(src_ap, kc, dst_ap, p, extra_waits=()):
            t_d = em("sync", lambda e, src_ap=src_ap, p=p: e.dma_start(
                out=s.x_stage[p][:], in_=src_ap),
                waits=[(t_xs_eng[p], t_xs[p])] + list(extra_waits) + bar,
                inc="d_xs%d" % p)
            t_c = em("vector", lambda e, p=p, dst_ap=dst_ap: e.tensor_copy(
                out=dst_ap, in_=s.x_stage[p][:]),
                waits=[("d_xs%d" % p, t_d)] + bar, inc="v")
            t_xs[p] = t_c
            t_xs_eng[p] = "v"
            self.amax_pump(2)
            return t_c

        t_wk = []
        for kc in range(NCH):
            t_wk.append(load_cast(s.wk_in[kc * 128:(kc + 1) * 128, :], kc,
                                  fold1[:, kc * 1024:(kc + 1) * 1024], kc % 2))
        t_wa_ev = []
        for ot in range(NCH):
            t_c = load_cast(s.wa_in[ot * 128:(ot + 1) * 128, :], ot, wtmp, ot % 2,
                            extra_waits=[("t", t_wtmp)])
            pp = ot % 2
            pT = (s.pT0, s.pT1)[pp]
            t_tr = None
            for kc in range(NCH):
                t_tr = em("tensor", lambda e, kc=kc, pT=pT: e.transpose(
                    out=pT[:, kc * 128:(kc + 1) * 128],
                    in_=wtmp[:, kc * 128:(kc + 1) * 128], identity=s.idb[:, :]),
                    waits=[("v", t_c), ("s", t_pT_ev[pp]), ("t", ph1_t),
                           ("v", ph1_v)] + bar,
                    inc="t" if kc == NCH - 1 else None)
            t_wtmp = t_tr
            t_pT[pp] = t_tr
            t_ev = None
            for kc in range(NCH):
                t_ev = em("scalar", lambda e, kc=kc, ot=ot, pT=pT: e.activation(
                    out=fold2[:, kc * 1024 + ot * 128:kc * 1024 + (ot + 1) * 128],
                    in_=pT[:, kc * 128:(kc + 1) * 128], func=AF.Copy),
                    waits=[("t", t_tr)] + bar, inc="s" if kc == NCH - 1 else None)
            t_pT_ev[pp] = t_ev
            t_wa_ev.append(t_ev)
        pE_free = [ph1_v] * 4
        t_fold_mm = []
        last_wka_mm = None
        for ic in range(NCH):
            for oh in range(2):
                sl = (ic * 2 + oh) % 4
                t_mm = None
                for kc in range(NCH):
                    t_mm = em("tensor", lambda e, ic=ic, oh=oh, kc=kc, sl=sl: e.matmul(
                        out=s.pE[:, sl * 512:(sl + 1) * 512],
                        lhsT=fold1[:, kc * 1024 + ic * 128:kc * 1024 + (ic + 1) * 128],
                        rhs=fold2[:, kc * 1024 + oh * 512:kc * 1024 + (oh + 1) * 512],
                        start=(kc == 0), stop=(kc == NCH - 1)),
                        waits=[("v", t_wk[-1]), ("s", t_wa_ev[-1]),
                               ("v", pE_free[sl])],
                        inc="t" if kc == NCH - 1 else None)
                last_wka_mm = t_mm
                pE_free[sl] = em("vector", lambda e, ic=ic, oh=oh, sl=sl: e.tensor_copy(
                    out=s.wkaT[:, ic * 1024 + oh * 512:ic * 1024 + (oh + 1) * 512],
                    in_=s.pE[:, sl * 512:(sl + 1) * 512]),
                    waits=[("t", t_mm)], inc="v")
                self.amax_pump(2)
        t_wka_done = pE_free[:]
        t_wv = []
        for kc in range(NCH):
            t_wv.append(load_cast(s.wv_in[kc * 128:(kc + 1) * 128, :], kc,
                                  fold1[:, kc * 1024:(kc + 1) * 1024], kc % 2,
                                  extra_waits=[("t", last_wka_mm)]))
        t_wb = []
        for ot in range(NCH):
            t_c = load_cast(s.wb_in[ot * 128:(ot + 1) * 128, :], ot, wtmp, ot % 2,
                            extra_waits=[("t", t_wtmp), ("t", last_wka_mm)])
            pp = ot % 2
            pT = (s.pT0, s.pT1)[pp]
            t_tr = None
            for kc in range(NCH):
                t_tr = em("tensor", lambda e, kc=kc, pT=pT: e.transpose(
                    out=pT[:, kc * 128:(kc + 1) * 128],
                    in_=wtmp[:, kc * 128:(kc + 1) * 128], identity=s.idb[:, :]),
                    waits=[("v", t_c), ("s", t_pT_ev[pp]), ("t", last_wka_mm)],
                    inc="t" if kc == NCH - 1 else None)
            t_wtmp = t_tr
            t_pT[pp] = t_tr
            t_ev = None
            for kc in range(NCH):
                t_ev = em("scalar", lambda e, kc=kc, ot=ot, pT=pT: e.activation(
                    out=fold2[:, kc * 1024 + ot * 128:kc * 1024 + (ot + 1) * 128],
                    in_=pT[:, kc * 128:(kc + 1) * 128], func=AF.Copy),
                    waits=[("t", t_tr)], inc="s" if kc == NCH - 1 else None)
            t_pT_ev[pp] = t_ev
            t_wb.append(t_ev)
        last_wvb_mm = None
        for ic in range(NCH):
            for oh in range(2):
                sl = (ic * 2 + oh) % 4
                t_mm = None
                for kc in range(NCH):
                    t_mm = em("tensor", lambda e, ic=ic, oh=oh, kc=kc, sl=sl: e.matmul(
                        out=s.pE[:, sl * 512:(sl + 1) * 512],
                        lhsT=fold1[:, kc * 1024 + ic * 128:kc * 1024 + (ic + 1) * 128],
                        rhs=fold2[:, kc * 1024 + oh * 512:kc * 1024 + (oh + 1) * 512],
                        start=(kc == 0), stop=(kc == NCH - 1)),
                        waits=[("v", t_wv[-1]), ("s", t_wb[-1]), ("v", pE_free[sl])],
                        inc="t" if kc == NCH - 1 else None)
                last_wvb_mm = t_mm
                pE_free[sl] = em("vector", lambda e, ic=ic, oh=oh, sl=sl: e.tensor_tensor(
                    out=s.wvbT[:, ic * 1024 + oh * 512:ic * 1024 + (oh + 1) * 512],
                    in0=s.pE[:, sl * 512:(sl + 1) * 512],
                    in1=s.trash[:, oh * 512:(oh + 1) * 512], op=OP.mult),
                    waits=[("t", t_mm), ("v", self.t_gabc)], inc="v")
                self.amax_pump(2)
        for ot in range(NCH):
            t_c = load_cast(s.wq_in[ot * 128:(ot + 1) * 128, :], ot, wtmp, ot % 2,
                            extra_waits=[("t", t_wtmp)])
            pp = ot % 2
            pT = (s.pT0, s.pT1)[pp]
            t_tr = None
            for kc in range(NCH):
                t_tr = em("tensor", lambda e, kc=kc, pT=pT: e.transpose(
                    out=pT[:, kc * 128:(kc + 1) * 128],
                    in_=wtmp[:, kc * 128:(kc + 1) * 128], identity=s.idb[:, :]),
                    waits=[("v", t_c), ("s", t_pT_ev[pp]), ("v", last_wvb_mm and 0 or 0),
                           ("t", last_wvb_mm)],
                    inc="t" if kc == NCH - 1 else None)
            t_wtmp = t_tr
            t_ev = None
            for kc in range(NCH):
                t_ev = em("scalar", lambda e, kc=kc, ot=ot, pT=pT: e.activation(
                    out=s.wqT[:, kc * 1024 + ot * 128:kc * 1024 + (ot + 1) * 128],
                    in_=pT[:, kc * 128:(kc + 1) * 128], func=AF.Copy),
                    waits=[("t", t_tr)], inc="s" if kc == NCH - 1 else None)
            t_pT_ev[pp] = t_ev
        self.t_ph3 = {"v": self.cnt["v"], "s": self.cnt["s"], "t": self.cnt["t"]}
        self.pE_free = pE_free
        self.t_xs = t_xs
        self.t_xs_eng = t_xs_eng
        self.t_pT = t_pT
        self.t_pT_ev = t_pT_ev

        if self.debug:
            em("gpsimd", lambda e: e.dma_start(out=self.dbg["qc"], in_=s.wqT[:, 0:1024]),
               waits=[("s", self.t_ph3["s"])], inc="d_dbg")


    def amax_pump(self, n=1):
        em, s = self.em, self
        for _ in range(n):
            idx = self._am_idx
            if idx >= 256:
                return
            self._am_idx += 1
            p = idx % 2
            k = idx // 2
            if p == 0:
                ot, hf = k // 4, k % 4
                src_ap = s.wfc_in[ot * 128:(ot + 1) * 128, hf * 256:(hf + 1) * 256]
                c0, c1 = 0, 1
            else:
                dc, rest = k // 16, k % 16
                src_ap = s.wpj_in[dc * 128:(dc + 1) * 128, rest * 256:(rest + 1) * 256]
                c0, c1 = 2, 3
            first = idx < 2
            t_d = em("gpsimd", lambda e, src_ap=src_ap, p=p: e.dma_start(
                out=s.amst[p][:], in_=src_ap),
                waits=[("v", self._t_am_b[p])], inc="d_am%d" % p)
            t_r = em("vector", lambda e, c0=c0, c1=c1, first=first, p=p: e.tensor_reduce(
                out=s.amx[:, (c0 if first else c1):(c0 if first else c1) + 1],
                in_=s.amst[p][:], axis=AX.X, op=OP.max, apply_absolute_value=True),
                waits=[("d_am%d" % p, t_d), ("v", self._t_am_v2[p])], inc="v")
            if not first:
                t_r = em("vector", lambda e, c0=c0, c1=c1: e.tensor_tensor(
                    out=s.amx[:, c0:c0 + 1], in0=s.amx[:, c0:c0 + 1],
                    in1=s.amx[:, c1:c1 + 1], op=OP.max),
                    waits=[("v", t_r)], inc="v")
            self._t_am_b[p] = t_r
            self._t_am_v2[p] = t_r
            self._t_am_v = t_r

    def ph4(self):
        em, s = self.em, self
        t_xs, t_xs_eng = self.t_xs, self.t_xs_eng
        t_pT, t_pT_ev = self.t_pT, self.t_pT_ev
        pE_free = self.pE_free
        pO_free = [self.t_gabc, self.t_bpjgb]
        pO_free = [self.cnt["v"], self.cnt["v"]]
        t_trash_free = self.t_ph3["v"]
        t_hst_mm = 0
        t_stats = [0, 0]
        t_xnb = 0
        t_qev = [0, 0]
        self.q_done = []
        t_akev = t_uev = t_qv = 0
        for g in range(NG):
            t_evac_grp = []
            gpar = 16 * (g % 2)
            for pair in range(2):
              t_sumv = 0
              t_sq = 0
              for tt2 in range(2):
                tt = pair * 2 + tt2
                kt = g * 4 + tt
                p = kt % 2
                cb = gpar + tt * 4
                t_d = em("sync", lambda e, kt=kt, p=p: e.dma_start(
                    out=s.x_stage[p][:], in_=s.x_in[kt * 128:(kt + 1) * 128, :]),
                    waits=[(t_xs_eng[p], t_xs[p])], inc="d_xs%d" % p)
                t_sumv = em("vector", lambda e, p=p, cb=cb: e.reduce_sum(
                    out=s.stats[:, cb:cb + 1], in_=s.x_stage[p][:], axis=AX.X),
                    waits=[("d_xs%d" % p, t_d), ("v", t_stats[g % 2])], inc="v")
                t_sq = em("scalar", lambda e, p=p, cb=cb: e.activation(
                    out=s.trash[:], in_=s.x_stage[p][:], func=AF.Square,
                    accum_out=s.stats[:, cb + 1:cb + 2]),
                    waits=[("d_xs%d" % p, t_d), ("v", t_trash_free),
                           ("s", getattr(self, "_t_sq_prev", 0)),
                           ("v", t_stats[g % 2])], inc="s")
                self._t_sq_prev = t_sq
              pb0 = gpar + pair * 8
              t_mv = em("vector", lambda e, pb0=pb0: e.tensor_scalar(
                  out=s.stats[:, pb0 + 2:pb0 + 8:4],
                  in0=s.stats[:, pb0 + 0:pb0 + 8:4],
                  scalar1=1.0 / 1024, scalar2=None, op0=OP.mult),
                  waits=[("v", t_sumv), ("s", t_sq)], inc="v")
              t_v1 = em("vector", lambda e, pb0=pb0: e.tensor_scalar(
                  out=s.stats[:, pb0 + 3:pb0 + 8:4],
                  in0=s.stats[:, pb0 + 1:pb0 + 8:4],
                  scalar1=1.0 / 1024, scalar2=None, op0=OP.mult),
                  waits=[("v", t_mv)], inc="v")
              t_ms = em("vector", lambda e, pb0=pb0: e.tensor_tensor(
                  out=s.stats[:, pb0 + 1:pb0 + 8:4],
                  in0=s.stats[:, pb0 + 2:pb0 + 8:4],
                  in1=s.stats[:, pb0 + 2:pb0 + 8:4], op=OP.mult),
                  waits=[("v", t_v1)], inc="v")
              t_var = em("vector", lambda e, pb0=pb0: e.tensor_tensor(
                  out=s.stats[:, pb0 + 0:pb0 + 8:4],
                  in0=s.stats[:, pb0 + 3:pb0 + 8:4],
                  in1=s.stats[:, pb0 + 1:pb0 + 8:4], op=OP.subtract),
                  waits=[("v", t_ms)], inc="v")
              t_std = em("scalar", lambda e, pb0=pb0: e.activation(
                  out=s.stats[:, pb0 + 1:pb0 + 8:4],
                  in_=s.stats[:, pb0 + 0:pb0 + 8:4],
                  func=AF.Sqrt, bias=s.kcol[:, 0:1]),
                  waits=[("v", t_var), ("s", self._t_sq_prev)], inc="s")
              self._t_sq_prev = t_std
              t_rstd = em("vector", lambda e, pb0=pb0: e.reciprocal(
                  out=s.stats[:, pb0 + 3:pb0 + 8:4],
                  in_=s.stats[:, pb0 + 1:pb0 + 8:4]),
                  waits=[("s", t_std)], inc="v")
              for tt2 in range(2):
                tt = pair * 2 + tt2
                kt = g * 4 + tt
                p = kt % 2
                cb = gpar + tt * 4
                t_xn = em("vector", lambda e, p=p, cb=cb: e.tensor_scalar(
                    out=s.xnb[:], in0=s.x_stage[p][:],
                    scalar1=s.stats[:, cb + 2:cb + 3],
                    scalar2=s.stats[:, cb + 3:cb + 4],
                    op0=OP.subtract, op1=OP.mult),
                    waits=[("v", t_rstd), ("t", t_xnb)], inc="v")
                t_xs[p] = t_xn
                t_xs_eng[p] = "v"
                pp = kt % 2
                pT = (s.pT0, s.pT1)[pp]
                t_tr = None
                for ic in range(NCH):
                    t_tr = em("tensor", lambda e, ic=ic, pT=pT: e.transpose(
                        out=pT[:, ic * 128:(ic + 1) * 128],
                        in_=s.xnb[:, ic * 128:(ic + 1) * 128], identity=s.idb[:, :]),
                        waits=[("v", t_xn), ("s", t_pT_ev[pp]), ("t", t_hst_mm)],
                        inc="t" if ic == NCH - 1 else None)
                t_xnb = t_tr
                t_pT[pp] = t_tr
                t_ev = None
                for ic in range(NCH):
                    t_ev = em("scalar", lambda e, ic=ic, tt=tt, pT=pT: e.activation(
                        out=s.h_stage[:, ic * 512 + tt * 128:ic * 512 + (tt + 1) * 128],
                        in_=pT[:, ic * 128:(ic + 1) * 128], func=AF.Identity,
                        bias=s.sa_col[:, ic:ic + 1], scale=s.ca1_col[:, ic:ic + 1]),
                        waits=[("t", t_tr)], inc="s" if ic == NCH - 1 else None)
                t_pT_ev[pp] = t_ev
                t_evac_grp.append(t_ev)
            t_stats[g % 2] = self.cnt["v"]
            t_hs = t_evac_grp[-1]
            for oc in range(NCH):
                sl = oc % 4
                t_mm = None
                for ic in range(NCH):
                    t_mm = em("tensor", lambda e, oc=oc, ic=ic, g=g, sl=sl: e.matmul(
                        out=s.pE[:, sl * 512:(sl + 1) * 512],
                        lhsT=s.wkaT[:, ic * 1024 + oc * 128:ic * 1024 + (oc + 1) * 128],
                        rhs=s.h_stage[:, ic * 512:(ic + 1) * 512],
                        start=(ic == 0), stop=(ic == NCH - 1)),
                        waits=[("s", t_hs), ("v", pE_free[sl])],
                        inc="t" if ic == NCH - 1 else None)
                t_hst_mm = t_mm
                pE_free[sl] = em("vector", lambda e, oc=oc, g=g, sl=sl: e.tensor_copy(
                    out=s.ak_fm[:, oc * 4096 + g * 512:oc * 4096 + (g + 1) * 512],
                    in_=s.pE[:, sl * 512:(sl + 1) * 512]),
                    waits=[("t", t_mm)], inc="v")
                t_akev = pE_free[sl]
                self.amax_pump(1)
            for tt in range(4):
                for vc in range(2):
                    t_mm = None
                    for ic in range(NCH):
                        t_mm = em("tensor", lambda e, tt=tt, vc=vc, ic=ic, g=g: e.matmul(
                            out=s.pO[:, vc * 512:(vc + 1) * 512],
                            lhsT=s.h_stage[:, ic * 512 + tt * 128:ic * 512 + (tt + 1) * 128],
                            rhs=s.wvbT[:, ic * 1024 + vc * 512:ic * 1024 + (vc + 1) * 512],
                            start=(ic == 0), stop=(ic == NCH - 1)),
                            waits=[("s", t_hs), ("v", pO_free[vc])],
                            inc="t" if ic == NCH - 1 else None)
                    t_hst_mm = t_mm
                    uev_w = [("t", t_mm)]
                    if g * 4 + tt == 31:
                        uev_w += [("d_qe0", self.cnt.get("d_qe0", 0)),
                                  ("d_qe1", self.cnt.get("d_qe1", 0))]
                    pO_free[vc] = em("vector", lambda e, tt=tt, vc=vc, g=g: e.tensor_copy(
                        out=s.u_tm[:, (g * 4 + tt) * 1024 + vc * 512:
                                   (g * 4 + tt) * 1024 + (vc + 1) * 512],
                        in_=s.pO[:, vc * 512:(vc + 1) * 512]),
                        waits=uev_w, inc="v")
                    t_uev = pO_free[vc]
                    self.amax_pump(1)
            if g < NGOWN:
                for oc in range(NCH):
                    sl = oc % 4
                    t_mm = None
                    for ic in range(NCH):
                        t_mm = em("tensor", lambda e, oc=oc, ic=ic, g=g, sl=sl: e.matmul(
                            out=s.pE[:, sl * 512:(sl + 1) * 512],
                            lhsT=s.wqT[:, ic * 1024 + oc * 128:ic * 1024 + (oc + 1) * 128],
                            rhs=s.h_stage[:, ic * 512:(ic + 1) * 512],
                            start=(ic == 0), stop=(ic == NCH - 1)),
                            waits=[("s", t_hs), ("v", pE_free[sl])],
                            inc="t" if ic == NCH - 1 else None)
                    t_hst_mm = t_mm
                    qp = oc % 2
                    t_qv = em("vector", lambda e, oc=oc, g=g, sl=sl, qp=qp: e.tensor_copy(
                        out=s.qev[qp][:], in_=s.pE[:, sl * 512:(sl + 1) * 512]),
                        waits=[("t", t_mm), ("d_qe%d" % qp, t_qev[qp])], inc="v")
                    pE_free[sl] = t_qv
                    for iq in range(4):
                        t_qev[qp] = em("gpsimd", lambda e, oc=oc, g=g, qp=qp, iq=iq:
                                       e.dma_start(
                                           out=s.q_hbm[g * 4 + iq, :, oc, :],
                                           in_=s.qev[qp][:, iq * 128:(iq + 1) * 128]),
                                       waits=[("v", t_qv)], inc="d_qe%d" % qp)
        self.pE_free = pE_free
        self.pO_free = pO_free
        self.t_hst_mm = t_hst_mm
        self.t_pT = t_pT
        self.t_pT_ev = t_pT_ev
        self.t_ph4 = {"v": self.cnt["v"], "s": self.cnt["s"], "t": self.cnt["t"],
                      "ak": t_akev, "u": t_uev}
        if self.debug:
            em("gpsimd", lambda e: e.dma_start(out=self.dbg["hst"], in_=s.h_stage[:]),
               waits=[("s", self.cnt["s"]), ("t", t_hst_mm)], inc="d_dbg")
            em("gpsimd", lambda e: e.dma_start(out=self.dbg["akc"],
                                               in_=s.ak_fm[:, 0:1024]),
               waits=[("v", t_akev)], inc="d_dbg")
            em("gpsimd", lambda e: e.dma_start(out=self.dbg["uc"],
                                               in_=s.u_tm[:, 0:1024]),
               waits=[("v", t_uev)], inc="d_dbg")

    def ph5_alloc(self):
        em, s = self.em, self
        self.esW.close()
        self.esA = ExitStack()
        def sba(name, shape, dt):
            return self.esA.enter_context(self.nc.sbuf_tensor(name, shape, dt))
        self.alphaA = sba("alphaA", [128, 2048], BF16)
        self.alphaB = sba("alphaB", [128, 2048], BF16)
        self.alphaT = sba("alphaT", [128, 4096], BF16)
        self.q_stage = [sba("q_stg%d" % i, [128, 1024], BF16) for i in range(2)]
        self.x1b = sba("x1b", [128, 1024], F32)
        self.wstb = [sba("wst%d" % i, [128, 1024], F32) for i in range(2)]
        self.qt = [sba("qt%d" % i, [128, 1024], F32) for i in range(2)]
        self.qki = sba("qki", [128, 512], I32)
        self.qkf = [sba("qkf%d" % i, [128, 1024], F32) for i in range(2)]
        self.qbf = [sba("qbf%d" % i, [128, 1024], BF16) for i in range(5)]

    def quant_finalize(self):
        em, s = self.em, self
        while self._am_idx < 256:
            self.amax_pump(8)
        t_g = self._t_am_v
        t_prev_t = self.cnt["t"]
        t_prev_ev = 0
        for mi, (acol, dstc) in enumerate(((0, 8), (2, 9))):
            t_tr = em("tensor", lambda e, acol=acol: e.transpose(
                out=s.pO[0:1, 0:128], in_=s.amx[:, acol:acol + 1],
                identity=s.idf[:, :]),
                waits=[("v", t_g), ("v", t_prev_ev), ("t", t_prev_t),
                       ("v", self.t_ph4["v"]),
                       ("s", self.t_ph4["s"]),
                       ("d_dbg", self.cnt.get("d_dbg", 0))], inc="t")
            t_r = em("vector", lambda e: e.tensor_reduce(
                out=s.kcol[0:1, 10:11], in_=s.pO[0:1, 0:128], axis=AX.X, op=OP.max),
                waits=[("t", t_tr)], inc="v")
            t_bc = em("tensor", lambda e: e.matmul(
                out=s.pO[:, 0:1], lhsT=s.onesf[:], rhs=s.kcol[0:1, 10:11],
                start=True, stop=True),
                waits=[("v", t_r)], inc="t")
            t_v = em("vector", lambda e, dstc=dstc: e.tensor_copy(
                out=s.kcol[:, dstc:dstc + 1], in_=s.pO[:, 0:1]),
                waits=[("t", t_bc)], inc="v")
            t_prev_t = t_bc
            t_prev_ev = t_v
        t_r1 = em("vector", lambda e: e.reciprocal(out=s.kcol[:, 10:11],
                                                   in_=s.kcol[:, 8:9]),
                  waits=[("v", t_v)], inc="v")
        t_m127f = em("vector", lambda e: e.tensor_scalar(
            out=s.kcol[:, 5:6], in0=s.kcol[:, 10:11], scalar1=127.0, scalar2=None,
            op0=OP.mult), waits=[("v", t_r1)], inc="v")
        t_r2 = em("vector", lambda e: e.reciprocal(out=s.kcol[:, 10:11],
                                                   in_=s.kcol[:, 9:10]),
                  waits=[("v", t_m127f)], inc="v")
        t_m127p = em("vector", lambda e: e.tensor_scalar(
            out=s.kcol[:, 7:8], in0=s.kcol[:, 10:11], scalar1=127.0, scalar2=None,
            op0=OP.mult), waits=[("v", t_r2)], inc="v")
        t_qsf = em("vector", lambda e: e.tensor_tensor(
            out=s.kcol[:, 6:7], in0=s.kcol[:, 8:9], in1=s.kcol[:, 4:5], op=OP.mult),
            waits=[("v", t_m127p)], inc="v")
        t_pjq = em("vector", lambda e: e.tensor_tensor(
            out=s.kcol[:, 10:11], in0=s.kcol[:, 9:10], in1=s.kcol[:, 4:5], op=OP.mult),
            waits=[("v", t_qsf)], inc="v")
        t_pjs = em("vector", lambda e: e.tensor_scalar(
            out=s.pjscale_col[:, 0:8], in0=s.gbT_col[:, 0:8],
            scalar1=s.kcol[:, 10:11], scalar2=None, op0=OP.mult),
            waits=[("v", t_pjq), ("v", self.t_gbT)], inc="v")
        self.t_qfin = t_pjs
        if self.debug:
            em("gpsimd", lambda e: e.dma_start(out=self.dbg["amax"][:, 0:4],
                                               in_=s.amx[:, 0:4]),
               waits=[("v", t_pjs)], inc="d_dbg")

    def _qsrc(self, idx):
        s = self
        if idx < 32:
            return (s.wfc_in[idx * 128:(idx + 1) * 128, :],
                    s.kcol[:, 5:6], s.kcol[:, 6:7])
        k = idx - 32
        dc, pc = k // 4, k % 4
        return (s.wpj_in[dc * 128:(dc + 1) * 128, pc * 1024:(pc + 1) * 1024],
                s.kcol[:, 7:8], s.pjscale_col[:, dc:dc + 1])

    def quant_stageA(self, idx):
        em, s = self.em, self
        st = self._qstate
        p = idx % 2
        qt, qkf = s.qt[p], s.qkf[p]
        _, m127, _ = self._qsrc(idx)
        if idx == 0:
            src_ap, _, _ = self._qsrc(0)
            st["dws"][0] = em("sync", lambda e, src_ap=src_ap: e.dma_start(
                out=s.wstb[0][:], in_=src_ap),
                waits=[("v", self.t_qfin)], inc="d_ws0")
        if idx + 1 < 64:
            src_ap, _, _ = self._qsrc(idx + 1)
            pn = (idx + 1) % 2
            st["dws"][pn] = em("sync", lambda e, src_ap=src_ap, pn=pn: e.dma_start(
                out=s.wstb[pn][:], in_=src_ap),
                waits=[("v", st["e1_%d" % pn]), ("v", self.t_qfin)],
                inc="d_ws%d" % pn)
        t_t = em("vector", lambda e, p=p, m127=m127, qt=qt: e.tensor_scalar(
            out=qt[:], in0=s.wstb[p][:], scalar1=m127, scalar2=None, op0=OP.mult),
            waits=[("d_ws%d" % p, st["dws"][p]), ("s", st["s12%d" % p]),
                   ("v", st["qq%d" % p]), ("g", st["d%d" % p])], inc="v")
        st["wst"][p] = t_t
        t_kf = None
        for hh in range(2):
            t_ki = em("vector", lambda e, qt=qt, hh=hh: e.tensor_copy(
                out=s.qki[:], in_=qt[:, hh * 512:(hh + 1) * 512]),
                waits=[("v", t_t), ("s", st["kf"] if hh == 0 else t_kf)], inc="v")
            t_kf = em("scalar", lambda e, qkf=qkf, hh=hh: e.activation(
                out=qkf[:, hh * 512:(hh + 1) * 512], in_=s.qki[:], func=AF.Copy),
                waits=[("v", t_ki), ("v", st["qq%d" % p]), ("s", st["q%d" % p]),
                       ("g", st["d%d" % p])],
                inc="s")
        st["kf"] = t_kf
        t_dd = em("gpsimd", lambda e, qt=qt, qkf=qkf: e.tensor_tensor(
            out=qt[:], in0=qt[:], in1=qkf[:], op=OP.subtract),
            waits=[("s", t_kf)], inc="g")
        st["d%d" % p] = t_dd

    def quant_stageB1(self, idx):
        em, s = self.em, self
        st = self._qstate
        p = idx % 2
        qt, wsb = s.qt[p], s.wstb[p]
        t_s1 = em("scalar", lambda e, qt=qt, wsb=wsb: e.activation(
            out=wsb[:], in_=qt[:], func=AF.Sign, bias=s.kcol[:, 1:2]),
            waits=[("g", st["d%d" % p]), ("v", st["wst"][p])], inc="s")
        t_s2 = em("scalar", lambda e, qt=qt: e.activation(
            out=qt[:], in_=qt[:], func=AF.Sign, bias=s.kcol[:, 2:3]),
            waits=[("s", t_s1)], inc="s")
        st["s12%d" % p] = t_s2
        st["ts1_%d" % p] = t_s1

    def quant_stageB2(self, idx):
        em, s = self.em, self
        st = self._qstate
        p = idx % 2
        qt, qkf, wsb = s.qt[p], s.qkf[p], s.wstb[p]
        t_e1 = em("vector", lambda e, qkf=qkf, wsb=wsb: e.scalar_tensor_tensor(
            out=qkf[:], in0=wsb[:], scalar=0.5, in1=qkf[:],
            op0=OP.mult, op1=OP.add),
            waits=[("s", st["ts1_%d" % p]), ("s", st["q%d" % p])], inc="v")
        st["e1_%d" % p] = t_e1
        t_q = em("vector", lambda e, qkf=qkf, qt=qt: e.scalar_tensor_tensor(
            out=qkf[:], in0=qt[:], scalar=0.5, in1=qkf[:],
            op0=OP.mult, op1=OP.add),
            waits=[("s", st["s12%d" % p]), ("v", t_e1)], inc="v")
        st["qq%d" % p] = t_q

    def quant_stageB3(self, idx):
        em, s = self.em, self
        st = self._qstate
        p = idx % 2
        pp = idx % 5
        qkf, qb = s.qkf[p], s.qbf[pp]
        _, _, scale = self._qsrc(idx)
        t_b = em("scalar", lambda e, scale=scale, qb=qb, qkf=qkf: e.activation(
            out=qb[:], in_=qkf[:], func=AF.Identity, scale=scale),
            waits=[("v", st["qq%d" % p]),
                   ("d_pc0", self.cnt.get("d_pc0", 0))], inc="s")
        st["q%d" % p] = t_b
        st["qb"][pp] = t_b

    def quant_slot(self, k):
        if 0 <= k - 1 < 64:
            self.quant_stageB1(k - 1)
            self.quant_stageB2(k - 1)
        if 0 <= k - 2 < 64:
            self.quant_stageB3(k - 2)
        if k < 64:
            self.quant_stageA(k)
        if 0 <= k - 6 < 64:
            self.quant_tile_store(k - 6)

    def quant_tile_store(self, idx):
        em, s = self.em, self
        st = self._qstate
        pp = idx % 5
        pT = s.pT0
        t_b = st["qb"][pp]
        qq = 0
        t_tr = None
        for j in range(8):
            t_tr = em("tensor", lambda e, j=j, pT=pT, pp=pp: e.transpose(
                out=pT[:, j * 128:(j + 1) * 128],
                in_=s.qbf[pp][:, j * 128:(j + 1) * 128], identity=s.idb[:, :]),
                waits=[("s", t_b), self.pTst["free"][qq]],
                inc="t" if j == 7 else None)
        self.pTst["tr"][qq] = t_tr
        t_ev = em("vector", lambda e, pT=pT, pp=pp: e.tensor_copy(
            out=s.qbf[pp][:], in_=pT[:]),
            waits=[("t", t_tr), ("d_pc0", self.cnt.get("d_pc0", 0))], inc="v")
        self.pTst["free"][qq] = ("v", t_ev)
        if idx < 32:
            dst = s.wfcq_hbm[idx]
        else:
            k = idx - 32
            dc, pc = k // 4, k % 4
            dst = s.wpjq_hbm[pc * 1024:(pc + 1) * 1024, dc * 128:(dc + 1) * 128] \
                .rearrange("(j p) o -> p j o", p=128)
        em("gpsimd", lambda e, dst=dst, pp=pp: e.dma_start(
            out=dst, in_=s.qbf[pp][:]),
            waits=[("v", t_ev)], inc="d_pc0")

    def ph5(self):
        em, s = self.em, self
        self.ph5_alloc()
        self.quant_finalize()
        self._qstate = {"wst": [self.t_qfin, self.t_qfin], "kf": 0,
                        "s120": 0, "s121": 0, "d0": 0, "d1": 0,
                        "q0": 0, "q1": 0, "qq0": 0, "qq1": 0,
                        "ts1_0": 0, "ts1_1": 0, "e1_0": 0, "e1_1": 0,
                        "qb": [0, 0, 0, 0, 0], "dws": [0, 0]}
        qslot_next = [0]

        ph4 = self.t_ph4
        self.pTst = {"free": [("v", self.cnt["v"]), ("v", self.cnt["v"])],
                     "tr": [self.cnt["t"], self.cnt["t"]]}
        st = {
            "expB": 0,
            "expA": 0,
            "transA_done": 0,
            "transB_done": 0,
            "alphaT_ev": [],

            "pO_stt": self.cnt["v"],
            "lcol_stt": 0,
            "lcol_stt2": [0, 0],
            "rescale": 0,
            "linv": 0,
            "x0": 0,
            "qs": [0, 0],
            "qs_mm": [ph4["t"], ph4["t"]],
            "x1": 0,
        }
        qbar = [("d_qe0", self.cnt.get("d_qe0", 0)), ("d_qe1", self.cnt.get("d_qe1", 0)),
                ("d_dbg", self.cnt.get("d_dbg", 0))]

        def emit_trans(half, ic_chunk, alpha, t_alpha_ready, eng_ready):
            evs = []
            t_tr = None
            for grp in range(2):
                pp = 1
                pT = s.pT1
                for j in range(8):
                    jb = grp * 8 + j
                    t_tr = em("tensor", lambda e, jb=jb, j=j, pT=pT, alpha=alpha:
                              e.transpose(
                                  out=pT[:, j * 128:(j + 1) * 128],
                                  in_=alpha[:, jb * 128:(jb + 1) * 128],
                                  identity=s.idb[:, :]),
                              waits=[(eng_ready, t_alpha_ready),
                                     self.pTst["free"][pp]],
                              inc="t" if j == 7 else None)
                self.pTst["tr"][pp] = t_tr
                ev = em("vector", lambda e, half=half, grp=grp, pT=pT: e.tensor_copy(
                    out=s.alphaT[:, half * 2048 + grp * 1024:
                                 half * 2048 + (grp + 1) * 1024],
                    in_=pT[:]),
                    waits=[("t", t_tr)], inc="v")
                self.pTst["free"][pp] = ("v", ev)
                evs.append(ev)
            return t_tr, evs

        def pump1():
            if qslot_next[0] < 70:
                self.quant_slot(qslot_next[0])
                qslot_next[0] += 1

        for ic in range(NQT):
            p = ic % 2
            t_q = em("sync", lambda e, ic=ic, p=p: e.dma_start(
                out=s.q_stage[p][:], in_=s.q_hbm[ic]),
                waits=[("t", st["qs_mm"][p])] + qbar, inc="d_q%d" % p)
            st["qs"][p] = t_q
            t_eA = None
            for oc in range(NCH):
                for jb in range(4):
                    t_eA = em("tensor", lambda e, oc=oc, jb=jb, p=p: e.matmul(
                        out=s.pE[:, jb * 512:(jb + 1) * 512],
                        lhsT=s.q_stage[p][:, oc * 128:(oc + 1) * 128],
                        rhs=s.ak_fm[:, oc * 4096 + jb * 512:oc * 4096 + (jb + 1) * 512],
                        start=(oc == 0), stop=(oc == NCH - 1)),
                        waits=[("d_q%d" % p, t_q), ("s", st["expB"]),
                               ("v", ph4["ak"])],
                        inc="t" if (oc == NCH - 1 and jb == 3) else None)
            pump1()
            t_mA = em("vector", lambda e: e.tensor_reduce(
                out=s.mcol[:, 0:1], in_=s.pE[:], axis=AX.X, op=OP.max, negate=True),
                waits=[("t", t_eA), ("s", st["expA"]), ("v", st["rescale"])], inc="v")
            lb = (ic % 2) * 4
            t_expA = em("scalar", lambda e, lb=lb: e.activation(
                out=s.alphaA[:], in_=s.pE[:], func=AF.Exp, bias=s.mcol[:, 0:1],
                accum_out=s.lcolv[:, lb:lb + 1]),
                waits=[("v", t_mA), ("t", st["transA_done"]), ("v", st["linv"]),
                       ("v", st["lcol_stt2"][ic % 2]),
                       ("d_dbg", self.cnt.get("d_dbg", 0) if ic == 0 else 0)], inc="s")
            st["expA"] = t_expA
            if ic > 0:
                self._emit_pv(ic - 1, st, emit_trans, part=0)
            t_eB = None
            for oc in range(NCH):
                for jb in range(4):
                    t_eB = em("tensor", lambda e, oc=oc, jb=jb, p=p: e.matmul(
                        out=s.pE[:, jb * 512:(jb + 1) * 512],
                        lhsT=s.q_stage[p][:, oc * 128:(oc + 1) * 128],
                        rhs=s.ak_fm[:, oc * 4096 + 2048 + jb * 512:
                                    oc * 4096 + 2048 + (jb + 1) * 512],
                        start=(oc == 0), stop=(oc == NCH - 1)),
                        waits=[("s", t_expA)],
                        inc="t" if (oc == NCH - 1 and jb == 3) else None)
            st["qs_mm"][p] = t_eB
            pump1()
            t_mB = em("vector", lambda e: e.tensor_reduce(
                out=s.mcol[:, 2:3], in_=s.pE[:], axis=AX.X, op=OP.max, negate=True),
                waits=[("t", t_eB), ("s", st["expA"])], inc="v")
            t_nm = em("vector", lambda e: e.tensor_tensor(
                out=s.mcol[:, 1:2], in0=s.mcol[:, 0:1], in1=s.mcol[:, 2:3], op=OP.min),
                waits=[("v", t_mB)], inc="v")
            t_dc = em("vector", lambda e: e.tensor_tensor(
                out=s.mcol[:, 3:4], in0=s.mcol[:, 1:2], in1=s.mcol[:, 0:1],
                op=OP.subtract),
                waits=[("v", t_nm)], inc="v")
            t_expB = em("scalar", lambda e, lb=lb: e.activation(
                out=s.alphaB[:], in_=s.pE[:], func=AF.Exp, bias=s.mcol[:, 1:2],
                accum_out=s.lcolv[:, lb + 1:lb + 2]),
                waits=[("v", t_nm), ("t", st["transB_done"]),
                       ("v", st["lcol_stt2"][ic % 2])], inc="s")
            st["expB"] = t_expB
            t_sA = em("scalar", lambda e: e.activation(
                out=s.mcol[:, 4:5], in_=s.mcol[:, 3:4], func=AF.Exp),
                waits=[("v", t_dc)], inc="s")
            t_rs = em("vector", lambda e: e.tensor_scalar(
                out=s.alphaA[:], in0=s.alphaA[:], scalar1=s.mcol[:, 4:5],
                scalar2=None, op0=OP.mult),
                waits=[("s", t_sA), ("s", t_expA)], inc="v")
            st["rescale"] = t_rs
            t_l2 = em("vector", lambda e, lb=lb: e.tensor_tensor(
                out=s.lcolv[:, lb + 2:lb + 3], in0=s.lcolv[:, lb:lb + 1],
                in1=s.mcol[:, 4:5], op=OP.mult),
                waits=[("s", t_sA), ("s", t_expA), ("v", st["lcol_stt"])], inc="v")
            t_l = em("vector", lambda e, lb=lb: e.tensor_tensor(
                out=s.lcolv[:, lb + 2:lb + 3], in0=s.lcolv[:, lb + 2:lb + 3],
                in1=s.lcolv[:, lb + 1:lb + 2], op=OP.add),
                waits=[("v", t_l2), ("s", t_expB)], inc="v")
            t_linv = em("vector", lambda e, lb=lb: e.reciprocal(
                out=s.lcolv[:, lb + 3:lb + 4], in_=s.lcolv[:, lb + 2:lb + 3]),
                waits=[("v", t_l)], inc="v")
            st["linv"] = t_linv
            pump1()
            if ic > 0:
                self._emit_pv(ic - 1, st, emit_trans, part=1)
            pump1()
            tA, evA = emit_trans(0, ic, s.alphaA, t_rs, "v")
            st["transA_done"] = tA
            st["alphaT_ev"] = evA
            st["_ic_linv"] = t_linv
        self._emit_pv(NQT - 1, st, emit_trans, part=0)
        self._emit_pv(NQT - 1, st, emit_trans, part=1)
        while qslot_next[0] < 70:
            self.quant_slot(qslot_next[0])
            qslot_next[0] += 1
        self.t_ph5 = dict(self.cnt)
        self.st5 = st

    def _emit_pv(self, ic, st, emit_trans, part=0):
        em, s = self.em, self
        if part == 0:
            tB, evB = emit_trans(1, ic, s.alphaB, st["expB"], "s")
            st["transB_done"] = tB
            st["evB"] = evB
            st["xd"] = em("sync", lambda e, ic=ic: e.dma_start(
                out=s.x_st0[:], in_=s.x_in[ic * 128:(ic + 1) * 128, :]),
                waits=[("v", st["x0"])], inc="d_xr")
            for jc in range(16):
                for vc in range(2):
                    em("tensor", lambda e, jc=jc, vc=vc: e.matmul(
                        out=s.pO[:, vc * 512:(vc + 1) * 512],
                        lhsT=s.alphaT[:, jc * 128:(jc + 1) * 128],
                        rhs=s.u_tm[:, jc * 1024 + vc * 512:jc * 1024 + (vc + 1) * 512],
                        start=(jc == 0), stop=False),
                        waits=[("v", st["alphaT_ev"][-1]), ("v", evB[-1]),
                               ("v", st["pO_stt"]), ("v", self.t_ph4["u"])])
            return
        evB = st["evB"]
        t_xd = st["xd"]
        t_pv = None
        for jc in range(16, 32):
            for vc in range(2):
                t_pv = em("tensor", lambda e, jc=jc, vc=vc: e.matmul(
                    out=s.pO[:, vc * 512:(vc + 1) * 512],
                    lhsT=s.alphaT[:, jc * 128:(jc + 1) * 128],
                    rhs=s.u_tm[:, jc * 1024 + vc * 512:jc * 1024 + (vc + 1) * 512],
                    start=False, stop=(jc == 31)),
                    waits=[("v", st["alphaT_ev"][-1]), ("v", evB[-1]),
                           ("v", st["pO_stt"]), ("v", self.t_ph4["u"])],
                    inc="t" if (jc == 31 and vc == 1) else None)
        lb = (ic % 2) * 4
        t_s1 = em("vector", lambda e, lb=lb: e.scalar_tensor_tensor(
            out=s.x1b[:, 0:512], in0=s.pO[:, 0:512], scalar=s.lcolv[:, lb + 3:lb + 4],
            in1=s.x_st0[:, 0:512], op0=OP.mult, op1=OP.add),
            waits=[("t", t_pv), ("d_xr", t_xd), ("d_x1", st["x1"]),
                   ("d_dbg", st.get("x1dbg", 0)),
                   ("v", st.get("_ic_linv", 0))], inc="v")
        t_s2 = em("vector", lambda e, lb=lb: e.scalar_tensor_tensor(
            out=s.x1b[:, 512:1024], in0=s.pO[:, 512:1024],
            scalar=s.lcolv[:, lb + 3:lb + 4],
            in1=s.x_st0[:, 512:1024], op0=OP.mult, op1=OP.add),
            waits=[("v", t_s1)], inc="v")
        st["pO_stt"] = t_s2
        st["lcol_stt"] = t_s2
        st["lcol_stt2"][ic % 2] = t_s2
        st["x0"] = t_s2
        st["x1"] = em("gpsimd", lambda e, ic=ic: e.dma_start(
            out=s.x1_hbm[ic * 128:(ic + 1) * 128, :], in_=s.x1b[:]),
            waits=[("v", t_s2)], inc="d_x1")
        if self.debug and ic == 0:
            st["x1dbg"] = em("gpsimd", lambda e: e.dma_start(out=self.dbg["x1c"],
                                                             in_=s.x1b[:]),
                             waits=[("v", t_s2)], inc="d_dbg")

    def ph6(self):
        em, s = self.em, self
        self.esA.close()
        self.esR.close()
        self.esF = ExitStack()
        def sbf(name, shape, dt):
            return self.esF.enter_context(self.nc.sbuf_tensor(name, shape, dt))
        wpj = sbf("wpj_sb", [128, 32 * 1024], BF16)
        hid = sbf("hid", [128, 32 * 512], BF16)
        h2 = sbf("h2", [128, 4096], BF16)
        x1blk = sbf("x1blk", [128, 4096], F32)
        bpjgb = sbf("bpjgb", [128, 1024], F32)
        bfcT = sbf("bfcT_sb", [16, 4096], BF16)
        wfcst = [sbf("wfcst%d" % i, [128, 1024], BF16) for i in range(4)]
        xnb2 = sbf("xnb2", [128, 1024], BF16)
        outb = [sbf("outb0", [128, 1024], F32)]
        rowp = x1blk

        c5 = self.t_ph5
        bar6 = [(k, c5.get(k, 0)) for k in ("d_x1", "d_pc0", "d_pc1", "d_q0", "d_q1",
                                            "d_xr", "d_ws0", "d_ws1", "d_dbg",
                                            "d_am", "d_qe0", "d_qe1")]
        bar6e = [(k, c5[k]) for k in ("v", "s", "t", "g")]

        t_wpj = None
        for hc in range(32):
            t_wpj = em("sync", lambda e, hc=hc: e.dma_start(
                out=wpj[:, hc * 1024:(hc + 1) * 1024],
                in_=s.wpjq_hbm[hc * 128:(hc + 1) * 128, :]),
                waits=bar6, inc="d_wpj")
        t_bfcT = em("sync", lambda e: e.dma_start(out=bfcT[:], in_=s.bfcT_hbm),
                    waits=bar6 + [("d_lr1", self.cnt["d_lr1"])], inc="d_bfc")
        t_rp = em("sync", lambda e: e.dma_start(out=rowp[0:1, 0:1024],
                                                in_=s.bpjrow_hbm),
                  waits=bar6 + [("d_bpr", self.cnt["d_bpr"])], inc="d_rp")
        t_bcmm = None
        for hh in range(2):
            t_bcmm = em("tensor", lambda e, hh=hh: e.matmul(
                out=s.pO[:, hh * 512:(hh + 1) * 512], lhsT=s.onesf[:],
                rhs=rowp[0:1, hh * 512:(hh + 1) * 512], start=True, stop=True),
                waits=[("d_rp", t_rp), ("v", c5["v"]), ("s", c5["s"])] ,
                inc="t" if hh else None)
        t_bpjgb = em("vector", lambda e: e.tensor_copy(out=bpjgb[:], in_=s.pO[:]),
                     waits=[("t", t_bcmm)] + bar6, inc="v")

        t_stats = [c5["v"], c5["v"]]
        t_xnb2 = c5["t"]
        t_sq_prev = c5["s"]
        t_pT_ev = [c5["s"], c5["s"]]
        t_h2_mm = c5["t"]
        t_hid_mm = c5["t"]
        t_x1blk = [c5["v"]] * 4
        t_lora1_mm = c5["t"]
        t_lora1p_mm = c5["t"]
        t_wf = [0, 0, 0, 0]
        t_wf_mm = [c5["t"], c5["t"], c5["t"], c5["t"]]
        t_out = [0, 0]
        t_outv = [c5["v"], c5["v"]]
        pE_free = [c5["s"]] * 4
        pE_free_v = [c5["v"]] * 4
        t_pv_out = [t_bpjgb, t_bpjgb]
        for tb in range(4):
            t_xd = []
            for tt in range(4):
                w_extra = [("t", t_bcmm)] if (tb == 0 and tt == 0) else []
                t_xd.append(em("sync", lambda e, tb=tb, tt=tt: e.dma_start(
                    out=x1blk[:, tt * 1024:(tt + 1) * 1024],
                    in_=s.x1_hbm[tb * 512 + tt * 128:tb * 512 + (tt + 1) * 128, :]),
                    waits=[("v", t_x1blk[tt]), ("d_x1", c5["d_x1"])] + w_extra,
                    inc="d_xb%d" % tt))
            for tt in range(4):
                kt = tb * 4 + tt
                pb = 16 * (kt % 2)
                xap = x1blk[:, tt * 1024:(tt + 1) * 1024]
                t_sum = em("vector", lambda e, xap=xap, pb=pb: e.reduce_sum(
                    out=s.stats[:, pb:pb + 1], in_=xap, axis=AX.X),
                    waits=[("d_xb%d" % tt, t_xd[tt]), ("v", t_stats[kt % 2])] + bar6,
                    inc="v")
                t_sq = em("scalar", lambda e, xap=xap, pb=pb: e.activation(
                    out=xnb2[:], in_=xap, func=AF.Square,
                    accum_out=s.stats[:, pb + 1:pb + 2]),
                    waits=[("d_xb%d" % tt, t_xd[tt]), ("s", t_sq_prev), ("t", t_xnb2),
                           ("v", t_stats[kt % 2])], inc="s")
                t_sq_prev = t_sq
                t_m = em("vector", lambda e, pb=pb: e.tensor_scalar(
                    out=s.stats[:, pb + 2:pb + 3], in0=s.stats[:, pb:pb + 1],
                    scalar1=1.0 / 1024, scalar2=None, op0=OP.mult),
                    waits=[("v", t_sum)], inc="v")
                t_v1 = em("vector", lambda e, pb=pb: e.tensor_scalar(
                    out=s.stats[:, pb + 3:pb + 4], in0=s.stats[:, pb + 1:pb + 2],
                    scalar1=1.0 / 1024, scalar2=None, op0=OP.mult),
                    waits=[("s", t_sq), ("v", t_m)], inc="v")
                t_ms = em("vector", lambda e, pb=pb: e.tensor_tensor(
                    out=s.stats[:, pb + 4:pb + 5], in0=s.stats[:, pb + 2:pb + 3],
                    in1=s.stats[:, pb + 2:pb + 3], op=OP.mult),
                    waits=[("v", t_v1)], inc="v")
                t_var = em("vector", lambda e, pb=pb: e.tensor_tensor(
                    out=s.stats[:, pb + 5:pb + 6], in0=s.stats[:, pb + 3:pb + 4],
                    in1=s.stats[:, pb + 4:pb + 5], op=OP.subtract),
                    waits=[("v", t_ms)], inc="v")
                t_std = em("scalar", lambda e, pb=pb: e.activation(
                    out=s.stats[:, pb + 6:pb + 7], in_=s.stats[:, pb + 5:pb + 6],
                    func=AF.Sqrt, bias=s.kcol[:, 0:1]),
                    waits=[("v", t_var)], inc="s")
                t_rstd = em("vector", lambda e, pb=pb: e.reciprocal(
                    out=s.stats[:, pb + 7:pb + 8], in_=s.stats[:, pb + 6:pb + 7]),
                    waits=[("s", t_std)], inc="v")
                t_xn = em("vector", lambda e, xap=xap, pb=pb: e.tensor_scalar(
                    out=xnb2[:], in0=xap,
                    scalar1=s.stats[:, pb + 2:pb + 3], scalar2=s.stats[:, pb + 7:pb + 8],
                    op0=OP.subtract, op1=OP.mult),
                    waits=[("v", t_rstd), ("t", t_xnb2), ("s", t_sq)], inc="v")
                t_stats[kt % 2] = t_xn
                pp = kt % 2
                pT = (s.pT0, s.pT1)[pp]
                t_tr = None
                for ic in range(NCH):
                    t_tr = em("tensor", lambda e, ic=ic, pT=pT: e.transpose(
                        out=pT[:, ic * 128:(ic + 1) * 128],
                        in_=xnb2[:, ic * 128:(ic + 1) * 128], identity=s.idb[:, :]),
                        waits=[("v", t_xn), ("s", t_pT_ev[pp]), ("t", t_h2_mm)],
                        inc="t" if ic == NCH - 1 else None)
                t_xnb2 = t_tr
                t_ev = None
                for ic in range(NCH):
                    t_ev = em("scalar", lambda e, ic=ic, tt=tt, pT=pT: e.activation(
                        out=h2[:, ic * 512 + tt * 128:ic * 512 + (tt + 1) * 128],
                        in_=pT[:, ic * 128:(ic + 1) * 128], func=AF.Identity,
                        bias=s.sbb_col[:, ic:ic + 1], scale=s.cb1_col[:, ic:ic + 1]),
                        waits=[("t", t_tr)], inc="s" if ic == NCH - 1 else None)
                t_pT_ev[pp] = t_ev
            t_h2 = t_ev
            t_mm = None
            for ic in range(NCH):
                t_mm = em("tensor", lambda e, ic=ic: e.matmul(
                    out=s.pE[0:16, 0:512], lhsT=s.afcT[:, ic * 16:(ic + 1) * 16],
                    rhs=h2[:, ic * 512:(ic + 1) * 512],
                    start=(ic == 0), stop=(ic == NCH - 1)),
                    waits=[("s", t_h2), ("s", pE_free[0]), ("v", pE_free_v[0]),
                           ("t", t_lora1_mm)],
                    inc="t" if ic == NCH - 1 else None)
            t_l1 = em("vector", lambda e: e.tensor_copy(out=s.lora1[:],
                                                        in_=s.pE[0:16, 0:512]),
                      waits=[("t", t_mm)], inc="v")
            pE_free_v[0] = t_l1
            for oc in range(32):
                p = oc % 4
                if oc == 0:
                    for pf in range(3):
                        t_wf[pf] = em("sync", lambda e, pf=pf: e.dma_start(
                            out=wfcst[pf][:], in_=s.wfcq_hbm[pf]),
                            waits=[("t", t_wf_mm[pf])], inc="d_wf%d" % pf)
                if oc + 3 < 32:
                    pn = (oc + 3) % 4
                    t_wf[pn] = em("sync", lambda e, oc=oc, pn=pn: e.dma_start(
                        out=wfcst[pn][:], in_=s.wfcq_hbm[oc + 3]),
                        waits=[("t", t_wf_mm[pn])], inc="d_wf%d" % pn)
                t_wd = t_wf[p]
                sl = oc % 4
                t_mm = None
                for ic in range(NCH):
                    t_mm = em("tensor", lambda e, ic=ic, oc=oc, p=p, sl=sl: e.matmul(
                        out=s.pE[:, sl * 512:(sl + 1) * 512],
                        lhsT=wfcst[p][:, ic * 128:(ic + 1) * 128],
                        rhs=h2[:, ic * 512:(ic + 1) * 512],
                        start=(ic == 0), stop=False),
                        waits=[("d_wf%d" % p, t_wd), ("s", pE_free[sl]),
                               ("v", pE_free_v[sl]), ("s", t_h2), ("v", t_l1)],
                        inc=None)
                t_mm = em("tensor", lambda e, oc=oc, sl=sl: e.matmul(
                    out=s.pE[:, sl * 512:(sl + 1) * 512],
                    lhsT=bfcT[0:16, oc * 128:(oc + 1) * 128],
                    rhs=s.lora1[:],
                    start=False, stop=True),
                    waits=[("v", t_l1), ("d_bfc", t_bfcT)], inc="t")
                t_wf_mm[oc % 4] = t_mm
                pE_free[sl] = em("scalar", lambda e, oc=oc, sl=sl: e.activation(
                    out=hid[:, oc * 512:(oc + 1) * 512],
                    in_=s.pE[:, sl * 512:(sl + 1) * 512], func=AF.Gelu,
                    bias=s.bfc_col[:, oc:oc + 1]),
                    waits=[("t", t_mm), ("t", t_hid_mm)], inc="s")
            t_fc = pE_free[(32 - 1) % 4]
            t_lora1_mm = t_wf_mm[(32 - 1) % 2]
            t_h2_mm = t_wf_mm[(32 - 1) % 2]
            t_mm = None
            for hc in range(32):
                t_mm = em("tensor", lambda e, hc=hc: e.matmul(
                    out=s.pE[0:16, 512:1024], lhsT=s.apjT[:, hc * 16:(hc + 1) * 16],
                    rhs=hid[:, hc * 512:(hc + 1) * 512],
                    start=(hc == 0), stop=(hc == 31)),
                    waits=[("s", pE_free[1]), ("v", pE_free_v[1]),
                           ("s", pE_free[(32 - 1) % 4]),
                           ("t", t_lora1p_mm)],
                    inc="t" if hc == 31 else None)
            t_l1p = em("vector", lambda e: e.tensor_copy(out=s.lora1p[:],
                                                         in_=s.pE[0:16, 512:1024]),
                       waits=[("t", t_mm)], inc="v")
            pE_free_v[1] = t_l1p
            for tt in range(4):
                po = 0
                t_mm = None
                for vc in range(2):
                    for hc in range(32):
                        t_mm = em("tensor", lambda e, tt=tt, vc=vc, hc=hc: e.matmul(
                            out=s.pO[:, vc * 512:(vc + 1) * 512],
                            lhsT=hid[:, hc * 512 + tt * 128:hc * 512 + (tt + 1) * 128],
                            rhs=wpj[:, hc * 1024 + vc * 512:hc * 1024 + (vc + 1) * 512],
                            start=(hc == 0), stop=False),
                            waits=[("s", t_fc), ("v", t_pv_out[vc]),
                                   ("d_wpj", t_wpj), ("v", t_l1p)],
                            inc=None)
                    t_mm = em("tensor", lambda e, tt=tt, vc=vc: e.matmul(
                        out=s.pO[:, vc * 512:(vc + 1) * 512],
                        lhsT=s.lora1p[0:16, tt * 128:(tt + 1) * 128],
                        rhs=s.bpjT[0:16, vc * 512:(vc + 1) * 512],
                        start=False, stop=True),
                        waits=[("v", t_l1p)], inc="t")
                    t_lora1p_mm = t_mm
                t_hid_mm = t_mm
                t_e1 = em("vector", lambda e, tt=tt, po=po: e.scalar_tensor_tensor(
                    out=outb[po][:, 0:512], in0=s.pO[:, 0:512], scalar=1.0,
                    in1=bpjgb[:, 0:512], op0=OP.bypass, op1=OP.add),
                    waits=[("t", t_mm), ("v", t_bpjgb),
                           ("d_out%d" % po, t_out[po])], inc="v")
                t_e2 = em("vector", lambda e, tt=tt, po=po: e.scalar_tensor_tensor(
                    out=outb[po][:, 512:1024], in0=s.pO[:, 512:1024], scalar=1.0,
                    in1=bpjgb[:, 512:1024], op0=OP.bypass, op1=OP.add),
                    waits=[("v", t_e1)], inc="v")
                t_pv_out[0] = t_e1
                t_pv_out[1] = t_e2
                t_a1 = em("vector", lambda e, tt=tt, po=po: e.tensor_tensor(
                    out=outb[po][:], in0=outb[po][:],
                    in1=x1blk[:, tt * 1024:(tt + 1) * 1024], op=OP.add),
                    waits=[("v", t_e2)], inc="v")
                t_x1blk[tt] = t_a1
                t_outv[po] = t_a1
                t_out[po] = em("gpsimd", lambda e, tb=tb, tt=tt, po=po: e.dma_start(
                    out=s.out_hbm[tb * 512 + tt * 128:tb * 512 + (tt + 1) * 128, :],
                    in_=outb[po][:]),
                    waits=[("v", t_a1)], inc="d_out%d" % po)
        self.t_ph6 = dict(self.cnt)
        if self.debug:
            em("gpsimd", lambda e: e.dma_start(out=self.dbg["cols"][:, 40:48],
                                               in_=s.pjscale_col[:]),
               waits=[("v", self.t_qfin)], inc="d_dbg")
            em("gpsimd", lambda e: e.dma_start(out=self.dbg["wpjqc"][0:16, :],
                                               in_=s.bpjT[:]),
               waits=[("v", self.t_bpjT)], inc="d_dbg")
            em("gpsimd", lambda e: e.dma_start(out=self.dbg["wfcqc"][0:128, 0:1024],
                                               in_=wpj[:, 4 * 1024:5 * 1024]),
               waits=[("d_wpj", self.cnt["d_wpj"])], inc="d_dbg")
            em("gpsimd", lambda e: e.dma_start(out=self.dbg["wpjqc"][16:32, 0:1024],
                                               in_=bpjgb[0:16, :]),
               waits=[("v", t_bpjgb)], inc="d_dbg")

    def finish(self):
        nc = self.nc
        final_waits = [(k, v) for k, v in self.cnt.items()
                       if k.startswith("d") and v > 0]
        self.em("gpsimd", lambda e: e.engine_nop(), waits=final_waits)
        sem = {k: self.es.enter_context(nc.semaphore("sem_" + k))
               for k in self.cnt}
        self.sem = sem
        blk = self.es.enter_context(nc.Block())
        last_wait = {e: {} for e in ENGINES}

        def run_section(engname, engobj):
            for fn, waits, inc, inc_n in self.ops[engname]:
                for sname, v in waits:
                    if last_wait[engname].get(sname, 0) < v:
                        engobj.wait_ge(sem[sname], v)
                        last_wait[engname][sname] = v
                instr = fn(engobj)
                if inc is not None:
                    instr.then_inc(sem[inc], inc_n)

        @blk.tensor
        def _(e):
            run_section("tensor", e)

        @blk.vector
        def _(e):
            run_section("vector", e)

        @blk.scalar
        def _(e):
            run_section("scalar", e)

        @blk.sync
        def _(e):
            run_section("sync", e)

        @blk.gpsimd
        def _(e):
            run_section("gpsimd", e)

        return nc


def build(debug=0, phases=99):
    mk = MK(debug=debug, phases=phases)
    mk.declare()
    mk.ph0()
    if phases >= 1:
        mk.ph1()
    if phases >= 3:
        mk.ph3()
    if phases >= 4:
        mk.ph4()
    if phases >= 5:
        mk.ph5()
    if phases >= 6:
        mk.ph6()
    return mk.finish()





_NC_CACHE = {}


def _get_nc():
    if "nc" not in _NC_CACHE:
        _NC_CACHE["nc"] = build(debug=0, phases=6)
    return _NC_CACHE["nc"]


def kernel(L0_attn_x, L0_attn_Wk, L0_attn_Wq, L0_attn_Wv, L0_attn_Wa, L0_attn_Wb,
           L0_ffn_W_fc, L0_ffn_b_fc, L0_ffn_A_fc, L0_ffn_B_fc, L0_ffn_W_proj,
           L0_ffn_b_proj, L0_ffn_A_proj, L0_ffn_B_proj, L0_c, L0_wmod):
    x = np.ascontiguousarray(np.asarray(L0_attn_x, dtype=np.float32))
    idb = np.eye(128, dtype=ml_dtypes.bfloat16)
    idf = np.eye(128, dtype=np.float32)
    onesf = np.ones((1, 128), dtype=np.float32)
    common = {
        "c_in": None,
        "wmod_in": np.ascontiguousarray(L0_wmod, dtype=np.float32),
        "wk_in": np.ascontiguousarray(L0_attn_Wk, dtype=np.float32),
        "wq_in": np.ascontiguousarray(L0_attn_Wq, dtype=np.float32),
        "wv_in": np.ascontiguousarray(L0_attn_Wv, dtype=np.float32),
        "wa_in": np.ascontiguousarray(L0_attn_Wa, dtype=np.float32),
        "wb_in": np.ascontiguousarray(L0_attn_Wb, dtype=np.float32),
        "wfc_in": np.ascontiguousarray(L0_ffn_W_fc, dtype=np.float32),
        "bfc_in": np.ascontiguousarray(L0_ffn_b_fc, dtype=np.float32),
        "afc_in": np.ascontiguousarray(L0_ffn_A_fc, dtype=np.float32),
        "bfcB_in": np.ascontiguousarray(L0_ffn_B_fc, dtype=np.float32),
        "wpj_in": np.ascontiguousarray(L0_ffn_W_proj, dtype=np.float32),
        "bpj_in": np.ascontiguousarray(
            np.asarray(L0_ffn_b_proj, dtype=np.float32).reshape(1, D)),
        "apj_in": np.ascontiguousarray(L0_ffn_A_proj, dtype=np.float32),
        "bpjB_in": np.ascontiguousarray(L0_ffn_B_proj, dtype=np.float32),
        "idb_in": idb, "idf_in": idf, "onesf_in": onesf,
    }
    in_maps = []
    for c in range(8):
        b, hf = c // 2, c % 2
        xb = x[b]
        x_c = np.concatenate([xb[hf * OWN:(hf + 1) * OWN],
                              xb[(1 - hf) * OWN:(2 - hf) * OWN]], axis=0)
        m = dict(common)
        m["x_in"] = np.ascontiguousarray(x_c)
        m["c_in"] = np.ascontiguousarray(
            np.asarray(L0_c, dtype=np.float32)[b:b + 1, :])
        in_maps.append(m)
    nc = _get_nc()
    res = run_bass_kernel_spmd(nc, in_maps, core_ids=list(range(8)))
    out = np.empty((4, S, D), dtype=np.float32)
    for c in range(8):
        b, hf = c // 2, c % 2
        out[b, hf * OWN:(hf + 1) * OWN, :] = res.results[c]["out"]
    return out
